# version 02
# baseline (speedup 1.0000x reference)
import contextlib
import numpy as np
import concourse.bass as bass
import concourse.mybir as mybir
from concourse.bass_utils import run_bass_kernel_spmd

F32 = mybir.dt.float32
BF16 = mybir.dt.bfloat16
AF = mybir.ActivationFunctionType
ALU = mybir.AluOpType
AX = mybir.AxisListType

NCORES = 8
R = 4096
D = 1024
NT = R // 128
EPS = 1e-5

ENGINES = ("sync", "scalar", "tensor", "vector", "gpsimd")


class Ctr:
    def __init__(self, h):
        self.h = h
        self.v = 0


class P:

    def __init__(self):
        self.nc = bass.Bass("TRN2", target_bir_lowering=False)
        self.es = contextlib.ExitStack()
        self.q = {e: [] for e in ENGINES}

    def inp(self, name, shape, dt=F32):
        return self.nc.dram_tensor(name, list(shape), dt, kind="ExternalInput").ap()

    def out(self, name, shape, dt=F32):
        return self.nc.dram_tensor(name, list(shape), dt, kind="ExternalOutput").ap()

    def dram(self, name, shape, dt=F32):
        return self.nc.dram_tensor(name, list(shape), dt).ap()

    def sbuf(self, name, shape, dt):
        return self.es.enter_context(self.nc.sbuf_tensor(name, list(shape), dt))

    def psum(self, name, shape, dt=F32):
        return self.es.enter_context(self.nc.psum_tensor(name, list(shape), dt))

    def sem(self, name):
        return Ctr(self.es.enter_context(self.nc.semaphore(name)))

    def op(self, eng, fn, inc=None, delta=1):
        if inc is None:
            self.q[eng].append(fn)
            return None
        inc.v += delta
        self.q[eng].append(lambda e, fn=fn, c=inc, d=delta: fn(e).then_inc(c.h, d))
        return inc.v

    def wt(self, eng, ctr, val=None):
        v = ctr.v if val is None else val
        self.q[eng].append(lambda e, c=ctr, v=v: e.wait_ge(c.h, v))

    def finalize(self):
        blk = self.es.enter_context(self.nc.Block())
        for name in ENGINES:
            ops = self.q[name]
            if not ops:
                continue

            def body(eng, ops=ops):
                for f in ops:
                    f(eng)

            getattr(blk, name)(body)
        return self.nc


def emit_ln(p, x_sb, nmean, rstd, st6, mvbuf, ve, sc, nt=NT, x_off=0, dv=D):
    for t in range(nt):
        xt = x_sb[:, x_off + t * dv: x_off + t * dv + D]
        p.op("vector", lambda e, xt=xt: e.bn_stats(out=st6[:, 0:6], in_=xt[:, 0:512]), inc=ve)
        p.wt("vector", ve)
        p.op("vector", lambda e, xt=xt: e.bn_stats(out=st6[:, 6:12], in_=xt[:, 512:1024]), inc=ve)
        p.wt("vector", ve)
        p.op("vector", lambda e, t=t: e.bn_aggr(out=mvbuf[:, 2 * t:2 * t + 2], in_=st6[:, 0:12]), inc=ve)
        p.wt("vector", ve)
    p.op("vector", lambda e: e.tensor_scalar(out=rstd[:, 0:nt], in0=mvbuf[:, 1:2 * nt:2],
                                             scalar1=float(EPS), scalar2=None, op0=ALU.add), inc=ve)
    p.wt("scalar", ve)
    p.op("scalar", lambda e: e.sqrt(out=rstd[:, 0:nt], in_=rstd[:, 0:nt]), inc=sc)
    p.wt("vector", sc)
    p.op("vector", lambda e: e.reciprocal(out=rstd[:, 0:nt], in_=rstd[:, 0:nt]), inc=ve)
    p.op("vector", lambda e: e.tensor_scalar(out=nmean[:, 0:nt], in0=mvbuf[:, 0:2 * nt:2],
                                             scalar1=-1.0, scalar2=None, op0=ALU.mult), inc=ve)
    p.wt("vector", ve)


def emit_norm_tile(p, x_ap, out_bf, nmean_col, rstd_col, ve):
    p.op("vector", lambda e: e.tensor_scalar(out=out_bf, in0=x_ap, scalar1=nmean_col,
                                             scalar2=rstd_col, op0=ALU.add, op1=ALU.mult), inc=ve)


def emit_transpose_tile(p, src_bf, ps_t, dst, te, ve, n128=8, dst_blocks=None, ident=None):
    for j in range(n128):
        p.op("tensor", lambda e, j=j: e.transpose(ps_t[:, j * 128:(j + 1) * 128],
                                                  src_bf[:, j * 128:(j + 1) * 128], ident[:]),
             inc=(te if j == n128 - 1 else None))
    p.wt("vector", te)
    if dst_blocks is None:
        p.op("vector", lambda e: e.tensor_copy(out=dst, in_=ps_t[:, 0:n128 * 128]), inc=ve)
    else:
        for j, db in enumerate(dst_blocks):
            p.op("vector", lambda e, j=j, db=db: e.tensor_copy(out=db, in_=ps_t[:, j * 128:(j + 1) * 128]),
                 inc=(ve if j == n128 - 1 else None))
    p.wt("tensor", ve)


def build_l1f():
    p = P()
    x_h = p.inp("x", [R, D])
    w_h = p.inp("w", [D, D])
    ident_h = p.inp("ident", [128, 128])
    y_h = p.out("y", [R, D])

    x_sb = p.sbuf("x_sb", [128, NT * D], F32)
    ident = p.sbuf("ident_sb", [128, 128], BF16)
    wf32 = p.sbuf("wf32", [128, 2 * D], F32)
    wsg = p.sbuf("wsg", [128, 2 * D], BF16)
    wsgT = p.sbuf("wsgT", [128, 8 * D], BF16)
    st6 = p.sbuf("st6", [128, 12], F32)
    mvbuf = p.sbuf("mvbuf", [128, 2 * NT], F32)
    nmean = p.sbuf("nmean", [128, NT], F32)
    rstd = p.sbuf("rstd", [128, NT], F32)
    ln_bf = p.sbuf("ln_bf", [128, 2 * D], BF16)
    lnt = p.sbuf("lnt", [128, 2 * D], BF16)
    y_sb = p.sbuf("y_sb", [128, 2 * D], F32)

    ps_t = p.psum("ps_t", [128, 1024], BF16)
    ps_t2 = p.psum("ps_t2", [128, 1024], BF16)
    ps_a = p.psum("ps_a", [128, 512], F32)
    ps_b = p.psum("ps_b", [128, 512], F32)

    ldh = p.sem("ldh")
    ldi = p.sem("ldi")
    ldw = [p.sem("ldw0"), p.sem("ldw1")]
    sc = p.sem("sc")
    te = p.sem("te")
    ve = p.sem("ve")
    sty = [p.sem("sty0"), p.sem("sty1")]

    for t in range(NT):
        p.op("sync", lambda e, t=t: e.dma_start(out=x_sb[:, t * D:(t + 1) * D],
                                                in_=x_h[t * 128:(t + 1) * 128, :]), inc=ldh, delta=16)
    x_loaded = ldh.v
    p.op("gpsimd", lambda e: e.dma_start(out=ident[:], in_=ident_h), inc=ldi, delta=16)
    ident_ld = ldi.v

    sign_done = [0] * 8
    wprep_te = [0] * 8
    for o in range(8):
        par = o % 2
        if o >= 2:
            p.wt("gpsimd", sc, sign_done[o - 2])
        p.op("gpsimd", lambda e, par=par, o=o: e.dma_start(
            out=wf32[:, par * D:(par + 1) * D], in_=w_h[o * 128:(o + 1) * 128, :]), inc=ldw[par], delta=16)
        p.wt("scalar", ldw[par])
        if o >= 2:
            p.wt("scalar", te, wprep_te[o - 2])
        sign_done[o] = p.op("scalar", lambda e, par=par: e.sign(
            out=wsg[:, par * D:(par + 1) * D], in_=wf32[:, par * D:(par + 1) * D]), inc=sc)
        pst = ps_t if par == 0 else ps_t2
        p.wt("tensor", sc, sign_done[o])
        if o >= 2:
            p.wt("tensor", ve)
        for j in range(8):
            p.op("tensor", lambda e, j=j, par=par, pst=pst: e.transpose(
                pst[:, j * 128:(j + 1) * 128], wsg[:, par * D + j * 128: par * D + (j + 1) * 128],
                ident[:]), inc=(te if j == 7 else None))
        wprep_te[o] = te.v
        p.wt("vector", te, wprep_te[o])
        p.op("vector", lambda e, o=o, pst=pst: e.tensor_copy(
            out=wsgT[:].rearrange("p (j od) -> p j od", j=8)[:, :, o * 128:(o + 1) * 128],
            in_=pst[:].rearrange("p (j c) -> p j c", j=8)), inc=ve)
    wprep_ve = ve.v


    p.wt("vector", ldh, x_loaded)
    emit_ln(p, x_sb, nmean, rstd, st6, mvbuf, ve, sc)
    stats_ve = ve.v

    norm_done = [0] * NT
    lnt_done = [0] * NT
    tr_done = [0] * NT
    mm_done = [0] * NT
    y_done = [0] * NT
    st_done = [0] * NT
    for t in range(NT):
        par = t % 2
        if t >= 2:
            p.wt("vector", te, tr_done[t - 2])
        emit_norm_tile(p, x_sb[:, t * D:(t + 1) * D], ln_bf[:, par * D:(par + 1) * D],
                       nmean[:, t:t + 1], rstd[:, t:t + 1], ve)
        norm_done[t] = ve.v
        pst = ps_t if par == 0 else ps_t2
        p.wt("tensor", ve, norm_done[t])
        for j in range(8):
            p.op("tensor", lambda e, j=j, par=par, pst=pst: e.transpose(
                pst[:, j * 128:(j + 1) * 128],
                ln_bf[:, par * D + j * 128: par * D + (j + 1) * 128], ident[:]),
                inc=(te if j == 7 else None))
        tr_done[t] = te.v
        tr_te = te.v
        p.wt("vector", te, tr_te)
        p.op("vector", lambda e, par=par, pst=pst: e.tensor_copy(
            out=lnt[:, par * D:(par + 1) * D], in_=pst[:, :]), inc=ve)
        lnt_done[t] = ve.v
        p.wt("tensor", ve, lnt_done[t])
        if t >= 1:
            p.wt("tensor", ve, y_done[t - 1])
        for h, ps in ((0, ps_a), (1, ps_b)):
            for j in range(8):
                p.op("tensor", lambda e, j=j, h=h, ps=ps, par=par: e.matmul(
                    ps[:, :], lnt[:, par * D + j * 128: par * D + (j + 1) * 128],
                    wsgT[:, j * D + h * 512: j * D + h * 512 + 512],
                    start=(j == 0), stop=(j == 7)),
                    inc=(te if (h, j) == (1, 7) else None))
        mm_done[t] = te.v
        p.wt("vector", te, mm_done[t])
        if t >= 2:
            p.wt("vector", sty[par], st_done[t - 2])
        for h, ps in ((0, ps_a), (1, ps_b)):
            p.op("vector", lambda e, h=h, ps=ps, par=par, t=t: e.scalar_tensor_tensor(
                out=y_sb[:, par * D + h * 512: par * D + h * 512 + 512], in0=ps[:, :],
                scalar=0.5, in1=x_sb[:, t * D + h * 512: t * D + h * 512 + 512],
                op0=ALU.mult, op1=ALU.add), inc=(ve if h == 1 else None))
        y_done[t] = ve.v
        p.wt("scalar", ve, y_done[t])
        p.op("scalar", lambda e, t=t, par=par: e.dma_start(
            out=y_h[t * 128:(t + 1) * 128, :], in_=y_sb[:, par * D:(par + 1) * D]),
            inc=sty[par], delta=16)
        st_done[t] = sty[par].v
    p.wt("vector", sty[0], sty[0].v)
    p.wt("vector", sty[1], sty[1].v)

    p.q["tensor"].insert(0, lambda e: e.wait_ge(ldi.h, ident_ld))
    return p



def emit_wT_prep(p, w_h, dst, wf32, wbf, ps_pair, ident, ldw_pair, sc, te, ve,
                 rows=D, cols=D, op="copy"):
    nrow = rows // 128
    ncol = cols // 128
    sc_base = sc.v
    te_base = te.v
    sign_done = [0] * nrow
    tdone = [0] * nrow
    for o in range(nrow):
        par = o % 2
        if o >= 2:
            p.wt("gpsimd", sc, sign_done[o - 2])
        else:
            p.wt("gpsimd", sc, sc_base)
        p.op("gpsimd", lambda e, par=par, o=o: e.dma_start(
            out=wf32[:, par * cols:par * cols + cols],
            in_=w_h[o * 128:(o + 1) * 128, :]), inc=ldw_pair[par], delta=16)
        p.wt("scalar", ldw_pair[par])
        p.wt("scalar", te, tdone[o - 2] if o >= 2 else te_base)
        fn = (lambda e, par=par: e.sign(out=wbf[:, par * cols:par * cols + cols],
                                        in_=wf32[:, par * cols:par * cols + cols])) if op == "sign" else              (lambda e, par=par: e.copy(out=wbf[:, par * cols:par * cols + cols],
                                        in_=wf32[:, par * cols:par * cols + cols]))
        sign_done[o] = p.op("scalar", fn, inc=sc)
        pst = ps_pair[par]
        p.wt("tensor", sc, sign_done[o])
        p.wt("tensor", ve)
        for j in range(ncol):
            p.op("tensor", lambda e, j=j, par=par, pst=pst: e.transpose(
                pst[:, j * 128:(j + 1) * 128],
                wbf[:, par * cols + j * 128: par * cols + (j + 1) * 128], ident[:]),
                inc=(te if j == ncol - 1 else None))
        tdone[o] = te.v
        p.wt("vector", te, tdone[o])
        p.op("vector", lambda e, o=o, pst=pst: e.tensor_copy(
            out=dst[:].rearrange("p (j od) -> p j od", j=ncol)[:, :, o * 128:(o + 1) * 128],
            in_=pst[:].rearrange("p (j c) -> p j c", j=ncol)[:, :, 0:128]), inc=ve)


def build_l3f():
    p = P()
    x_h = p.inp("x", [R, D])
    wr_h = p.inp("wr", [D, D])
    wk_h = p.inp("wk", [D, D])
    wv_h = p.inp("wv", [D, D])
    mur_h = p.inp("mur", [1, D])
    muk_h = p.inp("muk", [1, D])
    xp_h = p.inp("xp", [1, D])
    ident_h = p.inp("ident", [128, 128])
    y_h = p.out("y", [R, D])

    x_sb = p.sbuf("x_sb", [128, 4 * D], F32)
    ident = p.sbuf("ident_sb", [128, 128], BF16)
    wf32 = p.sbuf("wf32", [128, 2 * D], F32)
    wbf = p.sbuf("wbf", [128, 2 * D], BF16)
    wrT = p.sbuf("wrT", [128, 8 * D], BF16)
    wkT = p.sbuf("wkT", [128, 8 * D], BF16)
    wvT = p.sbuf("wvT", [128, 8 * D], BF16)
    mu_row = p.sbuf("mu_row", [1, 2 * D], F32)
    mu_fm = p.sbuf("mu_fm", [128, 4 * 8], F32)
    xp_sb = p.sbuf("xp_sb", [1, D], F32)
    xpln = p.sbuf("xpln", [1, D], BF16)
    st6 = p.sbuf("st6", [128, 12], F32)
    mvbuf = p.sbuf("mvbuf", [128, 2 * NT], F32)
    nmean = p.sbuf("nmean", [128, NT], F32)
    rstd = p.sbuf("rstd", [128, NT], F32)
    pst6 = p.sbuf("pst6", [1, 12], F32)
    pmv = p.sbuf("pmv", [1, 2], F32)
    ln_bf = p.sbuf("ln_bf", [128, 2 * D], BF16)
    lnt = p.sbuf("lnt", [128, 2 * 8 * 129], BF16)
    xrT = p.sbuf("xrT", [128, D], BF16)
    xkT = p.sbuf("xkT", [128, D], BF16)
    r_sb = p.sbuf("r_sb", [128, D], F32)
    kact = p.sbuf("kact", [128, D], F32)
    kact_bf = p.sbuf("kact_bf", [128, D], BF16)
    kactT = p.sbuf("kactT", [128, D], BF16)
    vtmp = p.sbuf("vtmp", [128, D], F32)
    y_sb = p.sbuf("y_sb", [128, 2 * D], F32)

    ps_t = p.psum("ps_t", [128, 1024], BF16)
    ps_t2 = p.psum("ps_t2", [128, 1024], BF16)
    ps_a = p.psum("ps_a", [128, 512], F32)
    ps_b = p.psum("ps_b", [128, 512], F32)
    ps_c = p.psum("ps_c", [128, 512], F32)
    ps_d = p.psum("ps_d", [128, 512], F32)

    ldh = p.sem("ldh")
    ldx = [p.sem("ldx%d" % i) for i in range(4)]
    ldi = p.sem("ldi")
    ldw = [p.sem("ldw0"), p.sem("ldw1")]
    sc = p.sem("sc")
    te = p.sem("te")
    ve = p.sem("ve")
    sty = [p.sem("sty0"), p.sem("sty1")]

    p.op("sync", lambda e: e.dma_start(out=mu_row[:, 0:D], in_=mur_h), inc=ldh, delta=16)
    p.op("sync", lambda e: e.dma_start(out=mu_row[:, D:2 * D], in_=muk_h), inc=ldh, delta=16)
    p.op("sync", lambda e: e.dma_start(out=xp_sb[:], in_=xp_h), inc=ldh, delta=16)
    all_ld = ldh.v
    p.op("gpsimd", lambda e: e.dma_start(out=ident[:], in_=ident_h), inc=ldi, delta=16)
    p.q["tensor"].append(lambda e: e.wait_ge(ldi.h, 16))

    emit_wT_prep(p, wr_h, wrT, wf32, wbf, (ps_t, ps_t2), ident, ldw, sc, te, ve)
    emit_wT_prep(p, wk_h, wkT, wf32, wbf, (ps_t, ps_t2), ident, ldw, sc, te, ve)
    emit_wT_prep(p, wv_h, wvT, wf32, wbf, (ps_t, ps_t2), ident, ldw, sc, te, ve)

    stats_done = [0] * NT
    xld = [0] * NT
    for t in range(NT):
        sl = t % 4
        if t >= 4:
            p.wt("sync", ve, stats_done[t - 4])
        p.op("sync", lambda e, t=t, sl=sl: e.dma_start(
            out=x_sb[:, sl * D:(sl + 1) * D], in_=x_h[t * 128:(t + 1) * 128, :]),
            inc=ldx[sl], delta=16)
        xld[t] = ldx[sl].v
        p.wt("vector", ldx[sl], xld[t])
        p.op("vector", lambda e, sl=sl: e.bn_stats(out=st6[:, 0:6], in_=x_sb[:, sl * D:sl * D + 512]), inc=ve)
        p.wt("vector", ve)
        p.op("vector", lambda e, sl=sl: e.bn_stats(out=st6[:, 6:12], in_=x_sb[:, sl * D + 512:sl * D + 1024]), inc=ve)
        p.wt("vector", ve)
        p.op("vector", lambda e, t=t: e.bn_aggr(out=mvbuf[:, 2 * t:2 * t + 2], in_=st6[:, 0:12]), inc=ve)
        p.wt("vector", ve)
        stats_done[t] = ve.v
    p.op("vector", lambda e: e.tensor_scalar(out=rstd[:, 0:NT], in0=mvbuf[:, 1:2 * NT:2],
                                             scalar1=float(EPS), scalar2=None, op0=ALU.add), inc=ve)
    p.wt("scalar", ve)
    p.op("scalar", lambda e: e.sqrt(out=rstd[:, 0:NT], in_=rstd[:, 0:NT]), inc=sc)
    p.wt("vector", sc)
    p.op("vector", lambda e: e.reciprocal(out=rstd[:, 0:NT], in_=rstd[:, 0:NT]), inc=ve)
    p.op("vector", lambda e: e.tensor_scalar(out=nmean[:, 0:NT], in0=mvbuf[:, 0:2 * NT:2],
                                             scalar1=-1.0, scalar2=None, op0=ALU.mult), inc=ve)
    p.wt("vector", ve)
    p.wt("vector", ldh, all_ld)
    p.op("vector", lambda e: e.bn_stats(out=pst6[:, 0:6], in_=xp_sb[:, 0:512]), inc=ve)
    p.wt("vector", ve)
    p.op("vector", lambda e: e.bn_stats(out=pst6[:, 6:12], in_=xp_sb[:, 512:1024]), inc=ve)
    p.wt("vector", ve)
    p.op("vector", lambda e: e.bn_aggr(out=pmv[:, 0:2], in_=pst6[:, 0:12]), inc=ve)
    p.wt("vector", ve)
    p.op("vector", lambda e: e.tensor_scalar(out=pmv[:, 1:2], in0=pmv[:, 1:2],
                                             scalar1=float(EPS), scalar2=None, op0=ALU.add), inc=ve)
    p.wt("scalar", ve)
    p.op("scalar", lambda e: e.sqrt(out=pmv[:, 1:2], in_=pmv[:, 1:2]), inc=sc)
    p.wt("vector", sc)
    p.op("vector", lambda e: e.reciprocal(out=pmv[:, 1:2], in_=pmv[:, 1:2]), inc=ve)
    p.op("vector", lambda e: e.tensor_scalar(out=pmv[:, 0:1], in0=pmv[:, 0:1],
                                             scalar1=-1.0, scalar2=None, op0=ALU.mult), inc=ve)
    p.wt("vector", ve)
    p.op("vector", lambda e: e.tensor_scalar(out=xpln[:], in0=xp_sb[:], scalar1=pmv[:, 0:1],
                                             scalar2=pmv[:, 1:2], op0=ALU.add, op1=ALU.mult), inc=ve)
    xpln_ve = ve.v

    mu_bf = p.sbuf("mu_bf", [1, 2 * D], BF16)
    p.wt("scalar", ldh, all_ld)
    mu_cast = p.op("scalar", lambda e: e.copy(out=mu_bf[:], in_=mu_row[:]), inc=sc)
    p.wt("tensor", sc, mu_cast)
    p.wt("tensor", ve)
    for i in range(16):
        p.op("tensor", lambda e, i=i: e.transpose(
            ps_t[:, i * 64:i * 64 + 1] if False else ps_t[:, i * 128:(i + 1) * 128][:, 0:1],
            mu_bf[:, i * 128:(i + 1) * 128], ident[:]), inc=None) if False else None
    for i in range(16):
        p.op("tensor", lambda e, i=i: e.transpose(
            ps_t2[:, 2 * i:2 * i + 1], mu_bf[:, i * 128:(i + 1) * 128], ident[0:1, 0:1]),
            inc=(te if i == 15 else None))
    mu_t = te.v
    p.wt("vector", te, mu_t)
    p.op("vector", lambda e: e.tensor_copy(out=mu_fm[:, 0:8], in_=ps_t2[:, 0:16:2]), inc=ve)
    p.op("vector", lambda e: e.tensor_copy(out=mu_fm[:, 16:24], in_=ps_t2[:, 16:32:2]), inc=ve)
    p.wt("vector", ve)
    p.op("vector", lambda e: e.tensor_scalar(out=mu_fm[:, 8:16], in0=mu_fm[:, 0:8],
                                             scalar1=-1.0, scalar2=1.0, op0=ALU.mult, op1=ALU.add), inc=ve)
    p.op("vector", lambda e: e.tensor_scalar(out=mu_fm[:, 24:32], in0=mu_fm[:, 16:24],
                                             scalar1=-1.0, scalar2=1.0, op0=ALU.mult, op1=ALU.add), inc=ve)
    mu_ve = ve.v

    p.wt("tensor", ve, mu_ve)
    for k in range(8):
        p.op("tensor", lambda e, k=k: e.transpose(
            ps_t[:, 2 * k:2 * k + 1], xpln[:, k * 128:(k + 1) * 128], ident[0:1, 0:1]),
            inc=(te if k == 7 else None))
    xpt_te = te.v
    p.wt("vector", te, xpt_te)
    p.op("vector", lambda e: e.tensor_copy(
        out=lnt[:].rearrange("p (k c) -> p k c", c=129)[:, 0:8, 0:1],
        in_=ps_t[:, 0:16:2].rearrange("p (k one) -> p k one", one=1)), inc=ve)
    margin0_ve = ve.v

    lnt_w = 8 * 129
    norm_done = [0] * NT
    tr_done = [0] * NT
    lnt_done = [0] * NT
    mix_done = [0] * NT
    mm_rk = [0] * NT
    y_done = [0] * NT
    st_done = [0] * NT
    kT_done = [0] * NT
    for t in range(NT):
        par = t % 2
        sl = t % 4
        if t >= 4:
            p.wt("sync", ve, y_done[t - 4])
        else:
            p.wt("sync", ve, stats_done[NT - 1])
        p.op("sync", lambda e, t=t, sl=sl: e.dma_start(
            out=x_sb[:, sl * D:(sl + 1) * D], in_=x_h[t * 128:(t + 1) * 128, :]),
            inc=ldx[sl], delta=16)
        xld[t] = ldx[sl].v
        if t >= 2:
            p.wt("vector", te, tr_done[t - 2])
        p.wt("vector", ldx[sl], xld[t])
        emit_norm_tile(p, x_sb[:, sl * D:(sl + 1) * D], ln_bf[:, par * D:(par + 1) * D],
                       nmean[:, t:t + 1], rstd[:, t:t + 1], ve)
        norm_done[t] = ve.v
        pst = ps_t if par == 0 else ps_t2
        p.wt("tensor", ve, norm_done[t])
        if t >= 2:
            p.wt("tensor", ve, kT_done[t - 2])
        if t == 0:
            p.wt("tensor", ve, margin0_ve)
        for j in range(8):
            p.op("tensor", lambda e, j=j, par=par, pst=pst: e.transpose(
                pst[:, j * 128:(j + 1) * 128],
                ln_bf[:, par * D + j * 128: par * D + (j + 1) * 128], ident[:]),
                inc=(te if j == 7 else None))
        tr_done[t] = te.v
        p.wt("vector", te, tr_done[t])
        p.op("vector", lambda e, par=par, pst=pst: e.tensor_copy(
            out=lnt[:, par * lnt_w:(par + 1) * lnt_w].rearrange("p (k c) -> p k c", c=129)[:, :, 1:129],
            in_=pst[:].rearrange("p (k c) -> p k c", c=128)), inc=ve)
        p.wt("vector", ve)
        p.op("vector", lambda e, par=par: e.tensor_copy(
            out=lnt[:, (1 - par) * lnt_w:(2 - par) * lnt_w].rearrange("p (k c) -> p k c", c=129)[:, :, 0:1],
            in_=lnt[:, par * lnt_w:(par + 1) * lnt_w].rearrange("p (k c) -> p k c", c=129)[:, :, 128:129]),
            inc=ve)
        lnt_done[t] = ve.v
        p.wt("vector", ve, lnt_done[t])
        lcur = lnt[:, par * lnt_w:(par + 1) * lnt_w].rearrange("p (k c) -> p k c", c=129)[:, :, 1:129]
        lprev = lnt[:, par * lnt_w:(par + 1) * lnt_w].rearrange("p (k c) -> p k c", c=129)[:, :, 0:128]
        xr3 = xrT[:].rearrange("p (k c) -> p k c", c=128)
        xk3 = xkT[:].rearrange("p (k c) -> p k c", c=128)
        p.op("vector", lambda e, lcur=lcur, xr3=xr3: e.tensor_tensor(
            out=xr3, in0=lcur, in1=mu_fm[:, 0:8].unsqueeze(2).broadcast_to((128, 8, 128)),
            op=ALU.mult), inc=ve)
        p.op("vector", lambda e, lprev=lprev: e.tensor_tensor(
            out=vtmp[:].rearrange("p (k c) -> p k c", c=128), in0=lprev,
            in1=mu_fm[:, 8:16].unsqueeze(2).broadcast_to((128, 8, 128)), op=ALU.mult), inc=ve)
        p.wt("vector", ve)
        p.op("vector", lambda e, xr3=xr3: e.tensor_tensor(
            out=xr3, in0=xr3, in1=vtmp[:].rearrange("p (k c) -> p k c", c=128), op=ALU.add), inc=ve)
        p.op("vector", lambda e, lcur=lcur, xk3=xk3: e.tensor_tensor(
            out=xk3, in0=lcur, in1=mu_fm[:, 16:24].unsqueeze(2).broadcast_to((128, 8, 128)),
            op=ALU.mult), inc=ve)
        p.wt("vector", ve)
        p.op("vector", lambda e, lprev=lprev: e.tensor_tensor(
            out=vtmp[:].rearrange("p (k c) -> p k c", c=128), in0=lprev,
            in1=mu_fm[:, 24:32].unsqueeze(2).broadcast_to((128, 8, 128)), op=ALU.mult), inc=ve)
        p.wt("vector", ve)
        p.op("vector", lambda e, xk3=xk3: e.tensor_tensor(
            out=xk3, in0=xk3, in1=vtmp[:].rearrange("p (k c) -> p k c", c=128), op=ALU.add), inc=ve)
        mix_done[t] = ve.v
        p.wt("tensor", ve, mix_done[t])
        if t >= 1:
            p.wt("vector", te, 0) if False else None
        for (ps2, wT, src) in ((( ps_a, ps_b), wrT, xrT), ((ps_c, ps_d), wkT, xkT)):
            for h, ps in ((0, ps2[0]), (1, ps2[1])):
                for j in range(8):
                    p.op("tensor", lambda e, j=j, h=h, ps=ps, wT=wT, src=src: e.matmul(
                        ps[:, :], src[:, j * 128:(j + 1) * 128],
                        wT[:, j * D + h * 512: j * D + h * 512 + 512],
                        start=(j == 0), stop=(j == 7)),
                        inc=(te if j == 7 else None))
        mm_rk[t] = te.v
        p.wt("scalar", te, mm_rk[t])
        if t >= 1:
            p.wt("scalar", ve, y_done[t - 1])
        p.op("scalar", lambda e: e.activation(out=r_sb[:, 0:512], in_=ps_a[:, :], func=AF.Sigmoid), inc=sc)
        p.op("scalar", lambda e: e.activation(out=r_sb[:, 512:1024], in_=ps_b[:, :], func=AF.Sigmoid), inc=sc)
        sig_sc = sc.v
        p.wt("vector", sc, sig_sc)
        p.op("vector", lambda e: e.tensor_scalar(out=kact[:, 0:512], in0=ps_c[:, :],
                                                 scalar1=0.0, scalar2=None, op0=ALU.max), inc=ve)
        p.op("vector", lambda e: e.tensor_scalar(out=kact[:, 512:1024], in0=ps_d[:, :],
                                                 scalar1=0.0, scalar2=None, op0=ALU.max), inc=ve)
        p.wt("vector", ve)
        p.op("vector", lambda e: e.tensor_tensor(out=kact_bf[:], in0=kact[:], in1=kact[:], op=ALU.mult), inc=ve)
        kk_ve = ve.v
        p.wt("tensor", ve, kk_ve)
        pst2 = ps_t if par == 0 else ps_t2
        for j in range(8):
            p.op("tensor", lambda e, j=j, pst2=pst2: e.transpose(
                pst2[:, j * 128:(j + 1) * 128], kact_bf[:, j * 128:(j + 1) * 128], ident[:]),
                inc=(te if j == 7 else None))
        kT_te = te.v
        p.wt("vector", te, kT_te)
        p.op("vector", lambda e, pst2=pst2: e.tensor_copy(out=kactT[:], in_=pst2[:, :]), inc=ve)
        kT_done[t] = ve.v
        p.wt("tensor", ve, kT_done[t])
        for h, ps in ((0, ps_a), (1, ps_b)):
            for j in range(8):
                p.op("tensor", lambda e, j=j, h=h, ps=ps: e.matmul(
                    ps[:, :], kactT[:, j * 128:(j + 1) * 128],
                    wvT[:, j * D + h * 512: j * D + h * 512 + 512],
                    start=(j == 0), stop=(j == 7)),
                    inc=(te if (h, j) == (1, 7) else None))
        mm_v = te.v
        p.wt("vector", te, mm_v)
        if t >= 2:
            p.wt("vector", sty[par], st_done[t - 2])
        for h, ps in ((0, ps_a), (1, ps_b)):
            p.op("vector", lambda e, h=h, ps=ps: e.tensor_tensor(
                out=vtmp[:, h * 512:(h + 1) * 512], in0=ps[:, :],
                in1=r_sb[:, h * 512:(h + 1) * 512], op=ALU.mult), inc=ve)
        p.wt("vector", ve)
        for h in (0, 1):
            p.op("vector", lambda e, h=h, par=par, sl=sl: e.scalar_tensor_tensor(
                out=y_sb[:, par * D + h * 512: par * D + h * 512 + 512],
                in0=vtmp[:, h * 512:(h + 1) * 512], scalar=0.5,
                in1=x_sb[:, sl * D + h * 512: sl * D + h * 512 + 512],
                op0=ALU.mult, op1=ALU.add), inc=(ve if h == 1 else None))
        y_done[t] = ve.v
        p.wt("scalar", ve, y_done[t])
        p.op("scalar", lambda e, t=t, par=par: e.dma_start(
            out=y_h[t * 128:(t + 1) * 128, :], in_=y_sb[:, par * D:(par + 1) * D]),
            inc=sty[par], delta=16)
        st_done[t] = sty[par].v
    p.wt("vector", sty[0], sty[0].v)
    p.wt("vector", sty[1], sty[1].v)
    return p



H2 = 4096


def build_l2f():
    p = P()
    x_h = p.inp("x", [R, D])
    w1_h = p.inp("W1", [H2, D])
    b1_h = p.inp("b1", [1, H2])
    w2_h = p.inp("W2", [D, H2])
    b2_h = p.inp("b2", [1, D])
    scale_h = p.inp("scale", [1, 1])
    lt_h = p.inp("lt_raw", [16, D])
    ident_h = p.inp("ident", [128, 128])
    y_h = p.out("y", [R, D])

    w1t_d = p.dram("w1t_d", [H2, D], BF16)
    w2t_d = p.dram("w2t_d", [4 * 32 * 128, 256], BF16)
    lnlt_d = p.dram("lnlt_d", [16, D], F32)

    x_sb = p.sbuf("x_sb", [128, 8 * D], F32)
    ident = p.sbuf("ident_sb", [128, 128], BF16)
    wf32 = p.sbuf("wf32", [128, 2 * D], F32)
    wbf = p.sbuf("wbf", [128, D], BF16)
    wtmp = p.sbuf("wtmp", [128, D], BF16)
    st6 = p.sbuf("st6", [128, 12], F32)
    mvbuf = p.sbuf("mvbuf", [128, 2 * NT], F32)
    nmean = p.sbuf("nmean", [128, NT], F32)
    rstd = p.sbuf("rstd", [128, NT], F32)
    ln_bf = p.sbuf("ln_bf", [128, 2 * D], BF16)
    xlnT = p.sbuf("xlnT", [128, 8 * 512], BF16)
    lt_sb = p.sbuf("lt_sb", [16, D], F32)
    lt6 = p.sbuf("lt6", [16, 12], F32)
    ltmv = p.sbuf("ltmv", [16, 2], F32)
    lnlt = p.sbuf("lnlt", [16, D], BF16)
    ltT = p.sbuf("ltT", [128, 8 * 16], BF16)
    e1 = p.sbuf("e1", [4, H2], BF16)
    mt = p.sbuf("mt", [128, 32 * 16], BF16)
    wtil = p.sbuf("wtil", [16, D], BF16)
    b1_sb = p.sbuf("b1_sb", [1, H2], F32)
    b1_bf = p.sbuf("b1_bf", [1, H2], BF16)
    b2_sb = p.sbuf("b2_sb", [1, D], F32)
    b2_bf = p.sbuf("b2_bf", [1, D], BF16)
    ones_bf = p.sbuf("ones_bf", [1, 512], BF16)
    onesf = p.sbuf("onesf", [1, 128], F32)
    scale_sb = p.sbuf("scale_sb", [1, 1], F32)
    scale_bc = p.sbuf("scale_bc", [128, 1], F32)
    w1g = p.sbuf("w1g", [128, 8 * 1024], BF16)
    w2g = p.sbuf("w2g", [128, 32 * 256], BF16)
    h_sb = p.sbuf("h_sb", [128, 32 * 512], BF16)
    dgs_bf = p.sbuf("dgs_bf", [4, 512], BF16)
    dot2_bf = p.sbuf("dot2_bf", [16, 512], BF16)
    y_sb = p.sbuf("y_sb", [128, 4 * D], F32)

    ps_t = p.psum("ps_t", [128, 1024], BF16)
    ps_t2 = p.psum("ps_t2", [128, 1024], BF16)
    ps_h = p.psum("ps_h", [128, 512], F32)
    ps_h2 = p.psum("ps_h2", [128, 512], F32)
    ps_sm = p.psum("ps_sm", [16, 512], F32)
    ps_o4 = p.psum("ps_o4", [128, 1024], F32)

    ldh = p.sem("ldh")
    ldx = [p.sem("ldx%d" % i) for i in range(8)]
    ldi = p.sem("ldi")
    ldw = [p.sem("ldw0"), p.sem("ldw1")]
    ldwg = [p.sem("ldwg0"), p.sem("ldwg1")]
    sc = p.sem("sc")
    te = p.sem("te")
    ve = p.sem("ve")
    gp = p.sem("gp")
    stw = [p.sem("stw0"), p.sem("stw1")]
    sty = [p.sem("sty0"), p.sem("sty1")]
    stl = p.sem("stl")

    p.op("sync", lambda e: e.dma_start(out=lt_sb[:], in_=lt_h), inc=ldh, delta=16)
    p.op("sync", lambda e: e.dma_start(out=b1_sb[:], in_=b1_h), inc=ldh, delta=16)
    p.op("sync", lambda e: e.dma_start(out=b2_sb[:], in_=b2_h), inc=ldh, delta=16)
    p.op("sync", lambda e: e.dma_start(out=scale_sb[:], in_=scale_h), inc=ldh, delta=16)
    small_ld = ldh.v
    p.op("gpsimd", lambda e: e.dma_start(out=ident[:], in_=ident_h), inc=ldi, delta=16)
    p.op("gpsimd", lambda e: e.memset(ones_bf[:], 1.0), inc=gp)
    p.op("gpsimd", lambda e: e.memset(onesf[:], 1.0), inc=gp)
    p.op("gpsimd", lambda e: e.memset(mt[:], 0.0), inc=gp)
    memset_gp = gp.v
    p.q["tensor"].append(lambda e: e.wait_ge(ldi.h, 16))

    p.wt("tensor", ldh, small_ld)
    p.wt("tensor", gp, memset_gp)
    sb_mm = p.op("tensor", lambda e: e.matmul(ps_sm[0:128, 0:1] if False else ps_h[0:128, 0:1],
                                              onesf[:, 0:128], scale_sb[:, 0:1],
                                              start=True, stop=True), inc=te)
    p.wt("vector", te, sb_mm)
    p.op("vector", lambda e: e.tensor_copy(out=scale_bc[:], in_=ps_h[:, 0:1]), inc=ve)
    scale_ve = ve.v

    p.wt("scalar", ldh, small_ld)
    p.op("scalar", lambda e: e.copy(out=b1_bf[:], in_=b1_sb[:]), inc=sc)
    p.op("scalar", lambda e: e.copy(out=b2_bf[:], in_=b2_sb[:]), inc=sc)
    bias_sc = sc.v

    p.wt("vector", ldh, small_ld)
    p.op("vector", lambda e: e.bn_stats(out=lt6[:, 0:6], in_=lt_sb[:, 0:512]), inc=ve)
    p.wt("vector", ve)
    p.op("vector", lambda e: e.bn_stats(out=lt6[:, 6:12], in_=lt_sb[:, 512:1024]), inc=ve)
    p.wt("vector", ve)
    p.op("vector", lambda e: e.bn_aggr(out=ltmv[:, 0:2], in_=lt6[:, 0:12]), inc=ve)
    p.wt("vector", ve)
    p.op("vector", lambda e: e.tensor_scalar(out=ltmv[:, 1:2], in0=ltmv[:, 1:2],
                                             scalar1=float(EPS), scalar2=None, op0=ALU.add), inc=ve)
    p.wt("scalar", ve)
    p.wt("scalar", sc, bias_sc)
    p.op("scalar", lambda e: e.sqrt(out=ltmv[:, 1:2], in_=ltmv[:, 1:2]), inc=sc)
    p.wt("vector", sc)
    p.op("vector", lambda e: e.reciprocal(out=ltmv[:, 1:2], in_=ltmv[:, 1:2]), inc=ve)
    p.wt("vector", ve)
    p.op("vector", lambda e: e.tensor_scalar(out=ltmv[:, 0:1], in0=ltmv[:, 0:1],
                                             scalar1=-1.0, scalar2=None, op0=ALU.mult), inc=ve)
    p.wt("vector", ve)
    p.op("vector", lambda e: e.tensor_scalar(out=lnlt[:], in0=lt_sb[:], scalar1=ltmv[:, 0:1],
                                             scalar2=ltmv[:, 1:2], op0=ALU.add, op1=ALU.mult), inc=ve)
    lnlt_ve = ve.v
    p.wt("gpsimd", ve, lnlt_ve)
    p.op("gpsimd", lambda e: e.dma_start(out=lnlt_d, in_=lnlt[:]), inc=stl, delta=16)
    lnlt_st = stl.v
    p.wt("tensor", ve, lnlt_ve)
    for k in range(8):
        p.op("tensor", lambda e, k=k: e.transpose(
            ps_t[:, k * 16:(k + 1) * 16], lnlt[:, k * 128:(k + 1) * 128], ident[0:16, 0:16]),
            inc=(te if k == 7 else None))
    ltT_te = te.v
    p.wt("vector", te, ltT_te)
    p.op("vector", lambda e: e.tensor_copy(out=ltT[:], in_=ps_t[:, 0:128]), inc=ve)
    ltT_ve = ve.v
    p.wt("vector", ve, ltT_ve)
    p.wt("vector", gp, memset_gp)
    for j in range(4):
        for vc in range(8):
            hhc = j * 8 + vc
            p.op("vector", lambda e, j=j, vc=vc, hhc=hhc: e.tensor_copy(
                out=mt[:, hhc * 16 + j: hhc * 16 + 16: 4],
                in_=ltT[:, vc * 16 + 9: vc * 16 + 16: 2]),
                inc=(ve if hhc == 31 else None))
    mt_ve = ve.v
    p.wt("gpsimd", stl, lnlt_st)
    p.op("gpsimd", lambda e: e.memset(e1[:], 0.0), inc=gp)
    p.op("gpsimd", lambda e: e.memset(wtil[:], 0.0), inc=gp)
    p.wt("gpsimd", gp)
    for g in range(4):
        p.op("gpsimd", lambda e, g=g: e.dma_start(
            out=e1[g:g + 1, g * 1024:(g + 1) * 1024], in_=lnlt_d[2 * g:2 * g + 1, :]),
            inc=stl, delta=16)
    for g in range(4):
        p.op("gpsimd", lambda e, g=g: e.dma_start(
            out=wtil[4 * g:4 * g + 4, g * 256:(g + 1) * 256],
            in_=lnlt_d[8 + 2 * g:9 + 2 * g, :].rearrange("one (c j) -> (one j) c", j=4),
            allow_slow_non_contiguous=True), inc=stl, delta=16)
    ewig_st = stl.v

    prep_last_store = [None]

    def prep_w(w_h_, nrow, dst_fn):
        for o in range(nrow):
            par = o % 2
            p.wt("gpsimd", sc)
            p.op("gpsimd", lambda e, par=par, o=o: e.dma_start(
                out=wf32[:, par * D:(par + 1) * D], in_=w_h_[o * 128:(o + 1) * 128, :]),
                inc=ldw[par], delta=16)
            p.wt("scalar", ldw[par])
            p.wt("scalar", te)
            cast_sc = p.op("scalar", lambda e, par=par: e.copy(
                out=wbf[:, 0:D], in_=wf32[:, par * D:(par + 1) * D]), inc=sc)
            pst = ps_t if par == 0 else ps_t2
            p.wt("tensor", sc, cast_sc)
            p.wt("tensor", ve)
            for j in range(8):
                p.op("tensor", lambda e, j=j, pst=pst: e.transpose(
                    pst[:, j * 128:(j + 1) * 128],
                    wbf[:, j * 128:(j + 1) * 128], ident[:]),
                    inc=(te if j == 7 else None))
            tr_v = te.v
            p.wt("vector", te, tr_v)
            if prep_last_store[0] is not None:
                p.wt("vector", prep_last_store[0][0], prep_last_store[0][1])
            p.op("vector", lambda e, pst=pst: e.tensor_copy(out=wtmp[:], in_=pst[:, 0:1024]), inc=ve)
            gat_ve = ve.v
            p.wt("gpsimd", ve, gat_ve)
            p.op("gpsimd", lambda e, o=o, dst=dst_fn(o): e.dma_start(out=dst, in_=wtmp[:]),
                 inc=stw[par], delta=16)
            prep_last_store[0] = (stw[par], stw[par].v)

    def w1_dst(o):
        hhg, g8 = o // 8, o % 8
        v = w1t_d.rearrange("(hhg j p) c -> hhg j p c", j=8, p=128)[o // 8]
        v = v[:, :, (o % 8) * 128:(o % 8) * 128 + 128]
        return v.transpose((1, 0, 2))

    def w2_dst(cb):
        def f(o):
            oc4 = (o * 128) // 256
            v = w2t_d.rearrange("(oc4 hhc p) c -> oc4 hhc p c", oc4=4, hhc=32)[oc4]
            v = v[cb * 8:(cb + 1) * 8, :, (o % 2) * 128:(o % 2) * 128 + 128]
            return v.transpose((1, 0, 2))
        return f

    prep_w(w1_h, 32, w1_dst)
    for cb in range(4):
        prep_w(w2_h[:, cb * 1024:(cb + 1) * 1024], 8, w2_dst(cb))
    wprep_stw = (stw[0].v, stw[1].v)

    stats_done = [0] * NT
    xld = [0] * NT
    for t in range(NT):
        sl = t % 8
        if t >= 8:
            p.wt("sync", ve, stats_done[t - 8])
        p.op("sync", lambda e, t=t, sl=sl: e.dma_start(
            out=x_sb[:, sl * D:(sl + 1) * D], in_=x_h[t * 128:(t + 1) * 128, :]),
            inc=ldx[sl], delta=16)
        xld[t] = ldx[sl].v
        p.wt("vector", ldx[sl], xld[t])
        p.op("vector", lambda e, sl=sl: e.bn_stats(out=st6[:, 0:6], in_=x_sb[:, sl * D:sl * D + 512]), inc=ve)
        p.wt("vector", ve)
        p.op("vector", lambda e, sl=sl: e.bn_stats(out=st6[:, 6:12], in_=x_sb[:, sl * D + 512:sl * D + 1024]), inc=ve)
        p.wt("vector", ve)
        p.op("vector", lambda e, t=t: e.bn_aggr(out=mvbuf[:, 2 * t:2 * t + 2], in_=st6[:, 0:12]), inc=ve)
        p.wt("vector", ve)
        stats_done[t] = ve.v
    p.op("vector", lambda e: e.tensor_scalar(out=rstd[:, 0:NT], in0=mvbuf[:, 1:2 * NT:2],
                                             scalar1=float(EPS), scalar2=None, op0=ALU.add), inc=ve)
    p.wt("scalar", ve)
    p.op("scalar", lambda e: e.sqrt(out=rstd[:, 0:NT], in_=rstd[:, 0:NT]), inc=sc)
    p.wt("vector", sc)
    p.op("vector", lambda e: e.reciprocal(out=rstd[:, 0:NT], in_=rstd[:, 0:NT]), inc=ve)
    p.op("vector", lambda e: e.tensor_scalar(out=nmean[:, 0:NT], in0=mvbuf[:, 0:2 * NT:2],
                                             scalar1=-1.0, scalar2=None, op0=ALU.mult), inc=ve)
    p.wt("vector", ve)
    stats_all = ve.v

    NTB = NT // 4
    y_done = [0] * NT
    st_done = [0] * NT
    h_ve_prev = 0
    out_te_prev = 0
    xlnT_te_prev = 0
    w1g_ld = [0, 0]
    w2g_ld = [0, 0]
    for tb in range(NTB):
        tbp = tb % 2
        tiles = [tb * 4 + i for i in range(4)]
        for i, t in enumerate(tiles):
            sl = t % 8
            if t >= 8:
                p.wt("sync", ve, y_done[t - 8])
            else:
                p.wt("sync", ve, stats_done[NT - 1])
            p.op("sync", lambda e, t=t, sl=sl: e.dma_start(
                out=x_sb[:, sl * D:(sl + 1) * D], in_=x_h[t * 128:(t + 1) * 128, :]),
                inc=ldx[sl], delta=16)
            xld[t] = ldx[sl].v
        for i, t in enumerate(tiles):
            sl = t % 8
            par = i % 2
            p.wt("vector", ldx[sl], xld[t])
            if i >= 2:
                p.wt("vector", te)
            emit_norm_tile(p, x_sb[:, sl * D:(sl + 1) * D], ln_bf[:, par * D:(par + 1) * D],
                           nmean[:, t:t + 1], rstd[:, t:t + 1], ve)
            nv = ve.v
            pst = ps_t if par == 0 else ps_t2
            p.wt("tensor", ve, nv)
            if tb > 0 and i == 0:
                p.wt("tensor", ve, h_ve_prev)
            for j in range(8):
                p.op("tensor", lambda e, j=j, par=par, pst=pst: e.transpose(
                    pst[:, j * 128:(j + 1) * 128],
                    ln_bf[:, par * D + j * 128: par * D + (j + 1) * 128], ident[:]),
                    inc=(te if j == 7 else None))
            trv = te.v
            p.wt("vector", te, trv)
            p.op("vector", lambda e, i=i, pst=pst: e.tensor_copy(
                out=xlnT[:].rearrange("p (k c) -> p k c", c=512)[:, :, i * 128:(i + 1) * 128],
                in_=pst[:].rearrange("p (k c) -> p k c", c=128)), inc=ve)
        xln_ve = ve.v
        xlnT_te_prev = te.v

        p.wt("tensor", ve, xln_ve)
        p.wt("tensor", ve, max(scale_ve, ltT_ve))
        for k in range(8):
            p.op("tensor", lambda e, k=k: e.matmul(
                ps_sm[0:4, 0:512], ltT[:, k * 16 + 1: k * 16 + 9: 2],
                xlnT[:, k * 512:(k + 1) * 512], start=(k == 0), stop=(k == 7)),
                inc=(te if k == 7 else None))
        dgs_te = te.v
        p.wt("vector", te, dgs_te)
        p.op("vector", lambda e: e.tensor_scalar(out=dgs_bf[:], in0=ps_sm[0:4, 0:512],
                                                 scalar1=scale_bc[0:4, 0:1], scalar2=None,
                                                 op0=ALU.mult), inc=ve)
        dgs_ve = ve.v

        h_banks = [ps_h, ps_h2]
        gelu_scs = []
        hh_ve = []
        for hhg in range(4):
            gpar = 0
            if tb > 0 or hhg >= 1:
                p.wt("sync", te)
            p.op("sync", lambda e, hhg=hhg: e.dma_start(
                out=w1g[:, 0:8192],
                in_=w1t_d.rearrange("(hhg cc p) c -> hhg cc p c", hhg=4, p=128)[hhg].transpose((1, 0, 2))),
                inc=ldwg[0], delta=16)
            w1g_ld[0] = ldwg[0].v
            p.wt("tensor", ldwg[0], w1g_ld[0])
            p.wt("tensor", stw[0], wprep_stw[0])
            p.wt("tensor", stw[1], wprep_stw[1])
            if hhg == 0:
                p.wt("tensor", ve, max(dgs_ve, mt_ve))
                p.wt("tensor", sc, bias_sc)
                if tb == 0:
                    p.wt("tensor", stl, ewig_st)
            for hc in range(8):
                hhc = hhg * 8 + hc
                bank = h_banks[hhc % 2]
                if hhc >= 2:
                    p.wt("tensor", sc, gelu_scs[hhc - 2])
                for k in range(8):
                    p.op("tensor", lambda e, k=k, hc=hc, bank=bank: e.matmul(
                        bank[:, :], w1g[:, k * 1024 + hc * 128: k * 1024 + (hc + 1) * 128],
                        xlnT[:, k * 512:(k + 1) * 512], start=(k == 0), stop=False))
                p.op("tensor", lambda e, hhc=hhc, bank=bank: e.matmul(
                    bank[:, :], e1[:, hhc * 128:(hhc + 1) * 128], dgs_bf[:, :],
                    start=False, stop=False))
                mmv = p.op("tensor", lambda e, hhc=hhc, bank=bank: e.matmul(
                    bank[:, :], b1_bf[:, hhc * 128:(hhc + 1) * 128], ones_bf[:, 0:512],
                    start=False, stop=True), inc=te)
                p.wt("scalar", te, mmv)
                gelu_scs.append(p.op("scalar", lambda e, hhc=hhc, bank=bank: e.activation(
                    out=h_sb[:, hhc * 512:(hhc + 1) * 512], in_=bank[:, :], func=AF.Gelu), inc=sc))
        h_sc = sc.v
        h_ve_prev = xln_ve
        h_te_after = te.v

        p.wt("tensor", sc, h_sc)
        for hhc in range(32):
            p.op("tensor", lambda e, hhc=hhc: e.matmul(
                ps_sm[0:16, 0:512], mt[:, hhc * 16:(hhc + 1) * 16],
                h_sb[:, hhc * 512:(hhc + 1) * 512], start=(hhc == 0), stop=(hhc == 31)),
                inc=(te if hhc == 31 else None))
        dot2_te = te.v
        p.wt("vector", te, dot2_te)
        p.op("vector", lambda e: e.tensor_scalar(out=dot2_bf[:], in0=ps_sm[0:16, 0:512],
                                                 scalar1=scale_bc[0:16, 0:1], scalar2=None,
                                                 op0=ALU.mult), inc=ve)
        dot2_ve = ve.v

        stt_last = [0] * 4
        prev_tb_st = (sty[1 - tbp].v if tb > 0 else 0)
        for oc4 in range(4):
            p.wt("sync", te)
            p.op("sync", lambda e, oc4=oc4: e.dma_start(
                out=w2g[:, 0:8192],
                in_=w2t_d.rearrange("(oc4 hhc p) c -> oc4 hhc p c", oc4=4, p=128)[oc4].transpose((1, 0, 2))),
                inc=ldwg[1], delta=16)
            w2g_ld[1] = ldwg[1].v
            p.wt("tensor", ldwg[1], w2g_ld[1])
            if oc4 == 0:
                p.wt("tensor", ve, dot2_ve)
            else:
                p.wt("tensor", ve, stt_last[oc4 - 1])
            omm = 0
            for ti in range(4):
                bank = ps_o4[:, ti * 256:(ti + 1) * 256]
                for hhc in range(32):
                    p.op("tensor", lambda e, hhc=hhc, ti=ti, bank=bank: e.matmul(
                        bank, h_sb[:, hhc * 512 + ti * 128: hhc * 512 + (ti + 1) * 128],
                        w2g[:, hhc * 256:(hhc + 1) * 256],
                        start=(hhc == 0), stop=False))
                p.op("tensor", lambda e, ti=ti, oc4=oc4, bank=bank: e.matmul(
                    bank, dot2_bf[:, ti * 128:(ti + 1) * 128],
                    wtil[:, oc4 * 256:(oc4 + 1) * 256], start=False, stop=False))
                omm = p.op("tensor", lambda e, oc4=oc4, bank=bank: e.matmul(
                    bank, ones_bf[:, 0:128], b2_bf[:, oc4 * 256:(oc4 + 1) * 256],
                    start=False, stop=True), inc=te)
            p.wt("vector", te, omm)
            if tb > 0 and oc4 == 0:
                p.wt("vector", sty[1 - tbp], prev_tb_st)
            for ti in range(4):
                sl = tiles[ti] % 8
                p.op("vector", lambda e, ti=ti, oc4=oc4, sl=sl: e.scalar_tensor_tensor(
                    out=y_sb[:, ti * D + oc4 * 256: ti * D + (oc4 + 1) * 256],
                    in0=ps_o4[:, ti * 256:(ti + 1) * 256], scalar=0.5,
                    in1=x_sb[:, sl * D + oc4 * 256: sl * D + (oc4 + 1) * 256],
                    op0=ALU.mult, op1=ALU.add), inc=ve)
            stt_last[oc4] = ve.v
        for ti in range(4):
            t = tiles[ti]
            y_done[t] = stt_last[3]
            p.wt("scalar", ve, stt_last[3])
            p.op("scalar", lambda e, t=t, ti=ti: e.dma_start(
                out=y_h[t * 128:(t + 1) * 128, :], in_=y_sb[:, ti * D:(ti + 1) * D]),
                inc=sty[tbp], delta=16)
            st_done[t] = sty[tbp].v
    p.wt("vector", sty[0], sty[0].v)
    p.wt("vector", sty[1], sty[1].v)
    return p


def build_l0f():
    p = P()
    x_h = p.inp("x", [R, D])
    gate_h = p.inp("gate", [D, 8])
    w1_h = p.inp("w1", [8, D, H2])
    w2_h = p.inp("w2", [8, H2, D])
    w3_h = p.inp("w3", [8, D, H2])
    ident_h = p.inp("ident", [128, 128])
    y_h = p.out("y", [R, D])
    w1s_d = p.dram("w1s_d", [8, 8, 128, 4096], BF16)
    w3s_d = p.dram("w3s_d", [8, 8, 128, 4096], BF16)
    w2s_d = p.dram("w2s_d", [8, 8, 128, 4096], BF16)

    x_sb = p.sbuf("x_sb", [128, 4 * D], F32)
    ident = p.sbuf("ident_sb", [128, 128], BF16)
    gate_bf = p.sbuf("gate_bf", [128, 64], BF16)
    st6 = p.sbuf("st6", [128, 12], F32)
    mvbuf = p.sbuf("mvbuf", [128, 2 * NT], F32)
    nmean = p.sbuf("nmean", [128, NT], F32)
    rstd = p.sbuf("rstd", [128, NT], F32)
    ln_bf = p.sbuf("ln_bf", [128, 2 * D], BF16)
    xlnT = p.sbuf("xlnT", [128, 8 * 512], BF16)
    logit = p.sbuf("logit", [128, 32], F32)
    tmpa = p.sbuf("tmpa", [128, 32], F32)
    tmpb = p.sbuf("tmpb", [128, 32], F32)
    m1s = p.sbuf("m1s", [128, 4], F32)
    m2s = p.sbuf("m2s", [128, 4], F32)
    s1s = p.sbuf("s1s", [128, 4], F32)
    s2s = p.sbuf("s2s", [128, 4], F32)
    we = p.sbuf("we", [128, 8 * NT], F32)
    w1b = p.sbuf("w1b", [128, 2 * 8 * 512], BF16)
    w3b = p.sbuf("w3b", [128, 2 * 8 * 512], BF16)
    w2b = p.sbuf("w2b", [128, 2 * 8 * 512], BF16)
    silu_sb = p.sbuf("silu_sb", [128, 512], F32)
    hid = p.sbuf("hid", [128, 32 * 512], BF16)
    acc = p.sbuf("acc", [128, 4 * D], F32)
    y_sb = p.sbuf("y_sb", [128, 4 * D], F32)

    ps_t = p.psum("ps_t", [128, 1024], BF16)
    ps_t2 = ps_t
    ps_g = p.psum("ps_g", [128, 1024], F32)
    ps_u = p.psum("ps_u", [128, 512], F32)
    ps_o4 = p.psum("ps_o4", [128, 2048], F32)

    ldx = [p.sem("ldx%d" % i) for i in range(4)]
    ldi = p.sem("ldi")
    ldg = p.sem("ldg")
    ldw1 = [p.sem("ldw1a"), p.sem("ldw1b")]
    ldw3 = [p.sem("ldw3a"), p.sem("ldw3b")]
    ldw2 = [p.sem("ldw2a"), p.sem("ldw2b")]
    lds1 = [p.sem("lds1a"), p.sem("lds1b")]
    lds3 = [p.sem("lds3a"), p.sem("lds3b")]
    lds2 = [p.sem("lds2a"), p.sem("lds2b")]
    stw1 = [p.sem("stw1a"), p.sem("stw1b")]
    stw2 = [p.sem("stw2a"), p.sem("stw2b")]
    sc = p.sem("sc")
    te = p.sem("te")
    ve = p.sem("ve")
    sty = [p.sem("sty0"), p.sem("sty1")]

    p.op("gpsimd", lambda e: e.dma_start(out=ident[:], in_=ident_h), inc=ldi, delta=16)
    p.op("gpsimd", lambda e: e.dma_start(
        out=gate_bf[:], in_=gate_h.rearrange("(cc p) e -> p cc e", p=128),
        allow_slow_non_contiguous=True), inc=ldg, delta=16)
    p.q["tensor"].append(lambda e: e.wait_ge(ldi.h, 16))

    stats_done = [0] * NT
    xld = [0] * NT
    for t in range(NT):
        sl = t % 4
        if t >= 4:
            p.wt("sync", ve, stats_done[t - 4])
        p.op("sync", lambda e, t=t, sl=sl: e.dma_start(
            out=x_sb[:, sl * D:(sl + 1) * D], in_=x_h[t * 128:(t + 1) * 128, :]),
            inc=ldx[sl], delta=16)
        xld[t] = ldx[sl].v
        p.wt("vector", ldx[sl], xld[t])
        p.op("vector", lambda e, sl=sl: e.bn_stats(out=st6[:, 0:6], in_=x_sb[:, sl * D:sl * D + 512]), inc=ve)
        p.wt("vector", ve)
        p.op("vector", lambda e, sl=sl: e.bn_stats(out=st6[:, 6:12], in_=x_sb[:, sl * D + 512:sl * D + 1024]), inc=ve)
        p.wt("vector", ve)
        p.op("vector", lambda e, t=t: e.bn_aggr(out=mvbuf[:, 2 * t:2 * t + 2], in_=st6[:, 0:12]), inc=ve)
        p.wt("vector", ve)
        stats_done[t] = ve.v
    p.op("vector", lambda e: e.tensor_scalar(out=rstd[:, 0:NT], in0=mvbuf[:, 1:2 * NT:2],
                                             scalar1=float(EPS), scalar2=None, op0=ALU.add), inc=ve)
    p.wt("scalar", ve)
    p.op("scalar", lambda e: e.sqrt(out=rstd[:, 0:NT], in_=rstd[:, 0:NT]), inc=sc)
    p.wt("vector", sc)
    p.op("vector", lambda e: e.reciprocal(out=rstd[:, 0:NT], in_=rstd[:, 0:NT]), inc=ve)
    p.op("vector", lambda e: e.tensor_scalar(out=nmean[:, 0:NT], in0=mvbuf[:, 0:2 * NT:2],
                                             scalar1=-1.0, scalar2=None, op0=ALU.mult), inc=ve)
    p.wt("vector", ve)
    stats_all = ve.v

    NTB = NT // 4

    def norm_and_transpose(tb, wait_te_xlnT):
        tiles = [tb * 4 + i for i in range(4)]
        for i, t in enumerate(tiles):
            sl = t % 4
            p.wt("sync", ve, xfree[t])
            p.op("sync", lambda e, t=t, sl=sl: e.dma_start(
                out=x_sb[:, sl * D:(sl + 1) * D], in_=x_h[t * 128:(t + 1) * 128, :]),
                inc=ldx[sl], delta=16)
            xld[t] = ldx[sl].v
        for i, t in enumerate(tiles):
            sl = t % 4
            par = i % 2
            p.wt("vector", ldx[sl], xld[t])
            p.wt("vector", te)
            emit_norm_tile(p, x_sb[:, sl * D:(sl + 1) * D], ln_bf[:, par * D:(par + 1) * D],
                           nmean[:, t:t + 1], rstd[:, t:t + 1], ve)
            nv = ve.v
            pst = ps_t if par == 0 else ps_t2
            p.wt("tensor", ve, nv)
            if i == 0 and wait_te_xlnT is not None:
                p.wt("tensor", te, wait_te_xlnT)
            for j in range(8):
                p.op("tensor", lambda e, j=j, par=par, pst=pst: e.transpose(
                    pst[:, j * 128:(j + 1) * 128],
                    ln_bf[:, par * D + j * 128: par * D + (j + 1) * 128], ident[:]),
                    inc=(te if j == 7 else None))
            trv = te.v
            p.wt("vector", te, trv)
            if i == 0 and wait_te_xlnT is not None:
                p.wt("vector", te, wait_te_xlnT)
            p.op("vector", lambda e, i=i, pst=pst: e.tensor_copy(
                out=xlnT[:].rearrange("p (k c) -> p k c", c=512)[:, :, i * 128:(i + 1) * 128],
                in_=pst[:].rearrange("p (k c) -> p k c", c=128)), inc=ve)
        return ve.v, te.v

    xfree = [stats_done[NT - 1]] * NT
    p.wt("tensor", ldg, 16)
    we_ve = ve.v

    st_done = [0] * NT
    prev_xlnT_te = te.v
    hid_prev = 0
    svu_prev = 0
    svu_prev2 = [0, 0]
    acc_done_prev = 0
    prev_tb_st = 0
    for tb in range(NTB):
        tbp = tb % 2
        tiles = [tb * 4 + i for i in range(4)]
        xln_ve, _ = norm_and_transpose(tb, prev_xlnT_te)
        p.wt("tensor", ve, xln_ve)
        p.wt("tensor", sc)
        p.wt("tensor", ve)
        for i in range(4):
            for k in range(8):
                p.op("tensor", lambda e, k=k, i=i: e.matmul(
                    ps_u[:, i * 32:i * 32 + 8],
                    xlnT[:, k * 512 + i * 128: k * 512 + (i + 1) * 128],
                    gate_bf[:, k * 8:(k + 1) * 8], start=(k == 0), stop=(k == 7)),
                    inc=(te if (i, k) == (3, 7) else None))
        lg_te = te.v
        p.wt("vector", te, lg_te)
        wecol = we[:, tb * 32:(tb + 1) * 32]
        l3 = wecol.rearrange("p (t e) -> p t e", e=8) if False else None
        lgs = logit[:, 0:32]
        for i in range(4):
            p.op("vector", lambda e, i=i: e.tensor_copy(
                out=lgs[:, i * 8:(i + 1) * 8], in_=ps_u[:, i * 32:i * 32 + 8]),
                inc=(ve if i == 3 else None))
        p.wt("vector", ve)
        l3v = lgs.rearrange("p (t e) -> p t e", e=8)
        p.op("vector", lambda e, l3v=l3v: e.tensor_reduce(
            out=m1s[:, 0:4], in_=l3v, op=ALU.max, axis=AX.X), inc=ve)
        p.wt("vector", ve)
        p.op("vector", lambda e, l3v=l3v: e.tensor_tensor(
            out=tmpa[:, 0:32].rearrange("p (t e) -> p t e", e=8), in0=l3v,
            in1=m1s[:, 0:4].unsqueeze(2).broadcast_to((128, 4, 8)), op=ALU.is_ge), inc=ve)
        p.wt("vector", ve)
        p.op("vector", lambda e: e.scalar_tensor_tensor(
            out=tmpb[:, 0:32], in0=tmpa[:, 0:32], scalar=-1e30, in1=lgs,
            op0=ALU.mult, op1=ALU.add), inc=ve)
        p.wt("vector", ve)
        p.op("vector", lambda e: e.tensor_reduce(
            out=m2s[:, 0:4], in_=tmpb[:, 0:32].rearrange("p (t e) -> p t e", e=8),
            op=ALU.max, axis=AX.X), inc=ve)
        p.wt("vector", ve)
        p.op("vector", lambda e: e.tensor_tensor(out=s1s[:, 0:4], in0=m1s[:, 0:4],
                                                 in1=m2s[:, 0:4], op=ALU.subtract), inc=ve)
        p.wt("scalar", ve)
        sgm = p.op("scalar", lambda e: e.activation(out=s1s[:, 0:4], in_=s1s[:, 0:4],
                                                    func=AF.Sigmoid), inc=sc)
        p.wt("vector", sc, sgm)
        p.op("vector", lambda e: e.tensor_scalar(out=s2s[:, 0:4], in0=s1s[:, 0:4],
                                                 scalar1=-1.0, scalar2=1.0,
                                                 op0=ALU.mult, op1=ALU.add), inc=ve)
        p.op("vector", lambda e, l3v=l3v: e.tensor_tensor(
            out=tmpb[:, 0:32].rearrange("p (t e) -> p t e", e=8), in0=l3v,
            in1=m2s[:, 0:4].unsqueeze(2).broadcast_to((128, 4, 8)), op=ALU.is_ge), inc=ve)
        p.wt("vector", ve)
        p.op("vector", lambda e: e.tensor_tensor(out=tmpb[:, 0:32], in0=tmpb[:, 0:32],
                                                 in1=tmpa[:, 0:32], op=ALU.subtract), inc=ve)
        p.wt("vector", ve)
        p.op("vector", lambda e: e.tensor_tensor(
            out=tmpa[:, 0:32].rearrange("p (t e) -> p t e", e=8),
            in0=tmpa[:, 0:32].rearrange("p (t e) -> p t e", e=8),
            in1=s1s[:, 0:4].unsqueeze(2).broadcast_to((128, 4, 8)), op=ALU.mult), inc=ve)
        p.op("vector", lambda e: e.tensor_tensor(
            out=tmpb[:, 0:32].rearrange("p (t e) -> p t e", e=8),
            in0=tmpb[:, 0:32].rearrange("p (t e) -> p t e", e=8),
            in1=s2s[:, 0:4].unsqueeze(2).broadcast_to((128, 4, 8)), op=ALU.mult), inc=ve)
        p.wt("vector", ve)
        p.op("vector", lambda e, wecol=wecol: e.tensor_tensor(
            out=wecol, in0=tmpa[:, 0:32], in1=tmpb[:, 0:32], op=ALU.add), inc=ve)
        we_ve = ve.v
        hid_prev = max(hid_prev, we_ve)
        acc_done = 0
        s1_stream = [(ei, ig) for ei in range(8) for ei2 in [0] for ig in range(8)]
        s1_cons = {}
        s1_ld = {}
        s1_stw = [0, 0]
        s2_stw = [0, 0]

        def s1_load(idx):
            ei, ig = s1_stream[idx]
            par = idx % 2
            if tb == 0:
                if idx >= 2:
                    p.wt("gpsimd", te, s1_cons[s1_stream[idx - 2]])
                    p.wt("gpsimd", stw1[par], s1_stw[par])
                else:
                    p.wt("gpsimd", te, te.v)
                p.op("gpsimd", lambda e, ei=ei, ig=ig, par=par: e.dma_start(
                    out=w1b[:, par * 4096:(par + 1) * 4096].rearrange("p (cc i) -> p cc i", i=512),
                    in_=w1_h[ei].rearrange("(cc p) i -> p cc i", p=128)[:, :, ig * 512:(ig + 1) * 512]),
                    inc=ldw1[par], delta=16)
                p.op("gpsimd", lambda e, ei=ei, ig=ig, par=par: e.dma_start(
                    out=w3b[:, par * 4096:(par + 1) * 4096].rearrange("p (cc i) -> p cc i", i=512),
                    in_=w3_h[ei].rearrange("(cc p) i -> p cc i", p=128)[:, :, ig * 512:(ig + 1) * 512]),
                    inc=ldw3[par], delta=16)
                s1_ld[(ei, ig)] = ("g", par, ldw1[par].v, ldw3[par].v)
                p.wt("gpsimd", ldw1[par])
                p.wt("gpsimd", ldw3[par])
                p.op("gpsimd", lambda e, ei=ei, ig=ig, par=par: e.dma_start(
                    out=w1s_d[ei, ig], in_=w1b[:, par * 4096:(par + 1) * 4096]), inc=stw1[par], delta=16)
                p.op("gpsimd", lambda e, ei=ei, ig=ig, par=par: e.dma_start(
                    out=w3s_d[ei, ig], in_=w3b[:, par * 4096:(par + 1) * 4096]), inc=stw1[par], delta=16)
                s1_stw[par] = stw1[par].v
            else:
                if idx >= 2:
                    p.wt("sync", te, s1_cons[s1_stream[idx - 2]])
                else:
                    p.wt("sync", te, te.v)
                p.op("sync", lambda e, ei=ei, ig=ig, par=par: e.dma_start(
                    out=w1b[:, par * 4096:(par + 1) * 4096], in_=w1s_d[ei, ig]),
                    inc=lds1[par], delta=16)
                p.op("sync", lambda e, ei=ei, ig=ig, par=par: e.dma_start(
                    out=w3b[:, par * 4096:(par + 1) * 4096], in_=w3s_d[ei, ig]),
                    inc=lds3[par], delta=16)
                s1_ld[(ei, ig)] = ("s", par, lds1[par].v, lds3[par].v)

        s2_stream = [(ei, oc, grp) for ei in range(8) for oc in range(2) for grp in range(4)]
        s2_cons = {}
        s2_ld = {}

        def s2_load(idx):
            ei, oc, grp = s2_stream[idx]
            par = idx % 2
            if tb == 0:
                if idx >= 2:
                    p.wt("gpsimd", te, s2_cons[s2_stream[idx - 2]])
                    p.wt("gpsimd", stw2[par], s2_stw[par])
                else:
                    p.wt("gpsimd", te, te.v)
                p.op("gpsimd", lambda e, ei=ei, oc=oc, grp=grp, par=par: e.dma_start(
                    out=w2b[:, par * 4096:(par + 1) * 4096].rearrange("p (icg o) -> p icg o", o=512),
                    in_=w2_h[ei].rearrange("(grp icg p) o -> grp p icg o", grp=4, p=128)[grp][:, :, oc * 512:(oc + 1) * 512]),
                    inc=ldw2[par], delta=16)
                s2_ld[(ei, oc, grp)] = ("g", par, ldw2[par].v)
                p.wt("gpsimd", ldw2[par])
                p.op("gpsimd", lambda e, ei=ei, oc=oc, grp=grp, par=par: e.dma_start(
                    out=w2s_d[ei, oc * 4 + grp], in_=w2b[:, par * 4096:(par + 1) * 4096]),
                    inc=stw2[par], delta=16)
                s2_stw[par] = stw2[par].v
            else:
                if idx >= 2:
                    p.wt("sync", te, s2_cons[s2_stream[idx - 2]])
                else:
                    p.wt("sync", te, te.v)
                p.op("sync", lambda e, ei=ei, oc=oc, grp=grp, par=par: e.dma_start(
                    out=w2b[:, par * 4096:(par + 1) * 4096], in_=w2s_d[ei, oc * 4 + grp]),
                    inc=lds2[par], delta=16)
                s2_ld[(ei, oc, grp)] = ("s", par, lds2[par].v)

        if tb == 1:
            p.wt("sync", stw1[0], stw1[0].v)
            p.wt("sync", stw1[1], stw1[1].v)
            p.wt("sync", stw2[0], stw2[0].v)
            p.wt("sync", stw2[1], stw2[1].v)
        s1_load(0)
        s1_load(1)
        s2_load(0)
        hid_prev2 = [hid_prev, hid_prev]
        for ei in range(8):
            for ig in range(8):
                idx = ei * 8 + ig
                qk, wpar, l1v, l3v = s1_ld[(ei, ig)]
                p.wt("tensor", (ldw1 if qk == "g" else lds1)[wpar], l1v)
                p.wt("tensor", (ldw3 if qk == "g" else lds3)[wpar], l3v)
                if ei == 0 and ig == 0:
                    p.wt("tensor", ve, xln_ve)
                for ic4 in range(4):
                    ic = ig * 4 + ic4
                    gpar = ic % 2
                    p.wt("tensor", sc, svu_prev2[gpar] if True else 0)
                    p.wt("tensor", ve, hid_prev)
                    for k in range(8):
                        p.op("tensor", lambda e, k=k, ic4=ic4, wpar=wpar, gpar=gpar: e.matmul(
                            ps_g[:, gpar * 512:(gpar + 1) * 512],
                            w1b[:, wpar * 4096 + k * 512 + ic4 * 128: wpar * 4096 + k * 512 + (ic4 + 1) * 128],
                            xlnT[:, k * 512:(k + 1) * 512], start=(k == 0), stop=(k == 7)),
                            inc=(te if k == 7 else None))
                    g_te = te.v
                    for k in range(8):
                        p.op("tensor", lambda e, k=k, ic4=ic4, wpar=wpar: e.matmul(
                            ps_u[:, :], w3b[:, wpar * 4096 + k * 512 + ic4 * 128: wpar * 4096 + k * 512 + (ic4 + 1) * 128],
                            xlnT[:, k * 512:(k + 1) * 512], start=(k == 0), stop=(k == 7)),
                            inc=(te if k == 7 else None))
                    u_te = te.v
                    s1_cons[(ei, ig)] = te.v
                    p.wt("scalar", te, g_te)
                    p.wt("scalar", ve, hid_prev)
                    svu = p.op("scalar", lambda e, gpar=gpar: e.activation(
                        out=silu_sb[:], in_=ps_g[:, gpar * 512:(gpar + 1) * 512],
                        func=AF.Silu), inc=sc)
                    svu_prev2[gpar] = svu
                    svu_prev = svu
                    p.wt("vector", sc, svu)
                    p.wt("vector", te, u_te)
                    p.op("vector", lambda e, ic=ic: e.tensor_tensor(
                        out=hid[:, ic * 512:(ic + 1) * 512], in0=ps_u[:, :], in1=silu_sb[:],
                        op=ALU.mult), inc=ve)
                    hid_prev = ve.v
                if idx + 2 < 64:
                    s1_load(idx + 2)
            hid_all = ve.v
            oc_evac = 0
            for oc in range(2):
                for grp in range(4):
                    idx2 = ei * 8 + oc * 4 + grp
                    qk2, wpar, l2v = s2_ld[(ei, oc, grp)]
                    p.wt("tensor", (ldw2 if qk2 == "g" else lds2)[wpar], l2v)
                    if oc == 0 and grp == 0:
                        p.wt("tensor", ve, hid_all)
                        p.wt("tensor", ve, acc_done_prev)
                    if oc == 1 and grp == 0:
                        p.wt("tensor", ve, oc_evac)
                    for ts in range(4):
                        for icg in range(8):
                            ic = grp * 8 + icg
                            p.op("tensor", lambda e, ts=ts, ic=ic, icg=icg, wpar=wpar: e.matmul(
                                ps_o4[:, ts * 512:(ts + 1) * 512],
                                hid[:, ic * 512 + ts * 128: ic * 512 + (ts + 1) * 128],
                                w2b[:, wpar * 4096 + icg * 512:wpar * 4096 + (icg + 1) * 512],
                                start=(ic == 0), stop=(ic == 31)),
                                inc=(te if (ts == 3 and icg == 7) else None))
                    s2_cons[(ei, oc, grp)] = te.v
                    if idx2 + 1 < 64:
                        s2_load(idx2 + 1)
                ostage_te = te.v
                p.wt("vector", te, ostage_te)
                if tb > 0 and ei == 0 and oc == 0:
                    p.wt("vector", sty[1 - tbp], prev_tb_st)
                for ts in range(4):
                    col = (tiles[ts]) * 8 + ei
                    if ei == 0:
                        p.op("vector", lambda e, ts=ts, oc=oc, col=col: e.tensor_scalar(
                            out=acc[:, ts * D + oc * 512: ts * D + (oc + 1) * 512],
                            in0=ps_o4[:, ts * 512:(ts + 1) * 512],
                            scalar1=we[:, col:col + 1], scalar2=None, op0=ALU.mult),
                            inc=(ve if ts == 3 else None))
                    else:
                        p.op("vector", lambda e, ts=ts, oc=oc, col=col: e.scalar_tensor_tensor(
                            out=acc[:, ts * D + oc * 512: ts * D + (oc + 1) * 512],
                            in0=ps_o4[:, ts * 512:(ts + 1) * 512],
                            scalar=we[:, col:col + 1],
                            in1=acc[:, ts * D + oc * 512: ts * D + (oc + 1) * 512],
                            op0=ALU.mult, op1=ALU.add),
                            inc=(ve if ts == 3 else None))
                oc_evac = ve.v
            acc_done = ve.v
            acc_done_prev = acc_done
        p.wt("vector", ve, acc_done)
        for ts in range(4):
            t = tiles[ts]
            sl = t % 4
            p.op("vector", lambda e, ts=ts, sl=sl: e.scalar_tensor_tensor(
                out=y_sb[:, ts * D:(ts + 1) * D], in0=acc[:, ts * D:(ts + 1) * D],
                scalar=0.5, in1=x_sb[:, sl * D:(sl + 1) * D], op0=ALU.mult, op1=ALU.add),
                inc=(ve if ts == 3 else None))
        ydone = ve.v
        for t in tiles:
            if t + 4 < NT:
                xfree[t + 4] = ydone
        p.wt("scalar", ve, ydone)
        for ts in range(4):
            t = tiles[ts]
            p.op("scalar", lambda e, t=t, ts=ts: e.dma_start(
                out=y_h[t * 128:(t + 1) * 128, :], in_=y_sb[:, ts * D:(ts + 1) * D]),
                inc=sty[tbp], delta=16)
            st_done[t] = sty[tbp].v
        prev_tb_st = sty[tbp].v
        prev_xlnT_te = te.v
    p.wt("vector", sty[0], sty[0].v)
    p.wt("vector", sty[1], sty[1].v)
    return p



RGN = 2176
NRT_ = 17


def build_l0a():
    p = P()
    xh_h = p.inp("xh", [RGN, D])
    x_h = p.inp("x", [R, D])
    wq1_h = p.inp("wq1", [D, D])
    wq2_h = p.inp("wq2", [D, D])
    wk1_h = p.inp("wk1", [D, 128])
    wk2_h = p.inp("wk2", [D, 128])
    wv_h = p.inp("wv", [D, 128])
    wout_h = p.inp("wout", [D, D])
    bout_h = p.inp("bout", [1, D])
    lvec_h = p.inp("lvec", [64, 4])
    cos_h = p.inp("cosr", [RGN, 32])
    sin_h = p.inp("sinr", [RGN, 32])
    ident_h = p.inp("ident", [128, 128])
    y_h = p.out("y", [R, D])

    xr_sb = p.sbuf("xr_sb", [128, 2 * D], F32)
    x_sb = p.sbuf("x_sb", [128, 2 * D], F32)
    ident = p.sbuf("ident_sb", [128, 128], BF16)
    wq1 = p.sbuf("wq1_sb", [128, 8 * D], BF16)
    wq2 = p.sbuf("wq2_sb", [128, 8 * D], BF16)
    wk1 = p.sbuf("wk1_sb", [128, 8 * 128], BF16)
    wk2 = p.sbuf("wk2_sb", [128, 8 * 128], BF16)
    wv = p.sbuf("wv_sb", [128, 8 * 128], BF16)
    wout = p.sbuf("wout_sb", [128, 8 * D], BF16)
    bout = p.sbuf("bout_sb", [1, D], BF16)
    bout_f = p.sbuf("bout_f", [1, D], F32)
    lvec = p.sbuf("lvec_sb", [64, 4], F32)
    lsc = p.sbuf("lsc_sb", [1, 4], F32)
    nlam_bc = p.sbuf("nlam_bc", [128, 1], F32)
    cos_sb = p.sbuf("cos_sb", [128, NRT_ * 32], F32)
    sin_sb = p.sbuf("sin_sb", [128, NRT_ * 32], F32)
    ones_b1 = p.sbuf("ones_b1", [1, 128], BF16)
    st6 = p.sbuf("st6", [128, 12], F32)
    mvbuf = p.sbuf("mvbuf", [128, 2 * NRT_], F32)
    nmean = p.sbuf("nmean", [128, NRT_], F32)
    rstd = p.sbuf("rstd", [128, NRT_], F32)
    ln_bf = p.sbuf("ln_bf", [128, D], BF16)
    rlnT = p.sbuf("rlnT", [128, 2 * D], BF16)
    q_tm = p.sbuf("q_tm", [128, D], F32)
    q_ro = p.sbuf("q_ro", [128, D], BF16)
    q1f = p.sbuf("q1f", [128, 8 * RGN], BF16)
    q2f = p.sbuf("q2f", [128, 8 * RGN], BF16)
    k1f = p.sbuf("k1f", [128, RGN], BF16)
    k2f = p.sbuf("k2f", [128, RGN], BF16)
    v_aug = p.sbuf("v_aug", [128, NRT_ * 130], BF16)
    p_sb = p.sbuf("p_sb", [128, 2 * 512], BF16)
    pT = p.sbuf("pT", [128, 2 * 512], BF16)
    pc = p.sbuf("pc", [128, 512], BF16)
    nmax = p.sbuf("nmax", [128, 4], F32)
    arec = p.sbuf("arec", [128, 2], F32)
    a_tm = p.sbuf("a_tm", [128, 2 * D], F32)
    a_bf = p.sbuf("a_bf", [128, 2 * D], BF16)
    aT = p.sbuf("aT", [128, 2 * D], BF16)
    y_sb = p.sbuf("y_sb", [128, 2 * D], F32)

    ps_t = p.psum("ps_t", [128, 1024], BF16)
    ps_t2 = p.psum("ps_t2", [128, 1024], BF16)
    ps_a = p.psum("ps_a", [128, 512], F32)
    ps_b = p.psum("ps_b", [128, 512], F32)
    ps_f = p.psum("ps_f", [128, 1024], F32)
    ps_v = p.psum("ps_v", [128, 136], F32)
    ps_s = p.psum("ps_s", [1, 8], F32)

    ldh = p.sem("ldh")
    ldr = [p.sem("ldr0"), p.sem("ldr1")]
    ldx = p.sem("ldx")
    ldg = p.sem("ldg")
    sc = p.sem("sc")
    te = p.sem("te")
    ve = p.sem("ve")
    sty = [p.sem("sty0"), p.sem("sty1")]

    p.op("gpsimd", lambda e: e.dma_start(out=ident[:], in_=ident_h), inc=ldg, delta=16)
    for (dst, srch) in ((wq1, wq1_h), (wq2, wq2_h), (wout, wout_h)):
        p.op("gpsimd", lambda e, dst=dst, srch=srch: e.dma_start(
            out=dst[:].rearrange("p (k c) -> p k c", c=D),
            in_=srch.rearrange("(k p) c -> p k c", p=128)), inc=ldg, delta=16)
    for (dst, srch) in ((wk1, wk1_h), (wk2, wk2_h), (wv, wv_h)):
        p.op("gpsimd", lambda e, dst=dst, srch=srch: e.dma_start(
            out=dst[:].rearrange("p (k c) -> p k c", c=128),
            in_=srch.rearrange("(k p) c -> p k c", p=128)), inc=ldg, delta=16)
    p.op("gpsimd", lambda e: e.dma_start(out=bout[:], in_=bout_h), inc=ldg, delta=16)
    p.op("gpsimd", lambda e: e.dma_start(out=bout_f[:], in_=bout_h), inc=ldg, delta=16)
    wld = ldg.v
    p.op("sync", lambda e: e.dma_start(out=lvec[:], in_=lvec_h), inc=ldh, delta=16)
    p.op("sync", lambda e: e.dma_start(
        out=cos_sb[:].rearrange("p (t c) -> p t c", c=32),
        in_=cos_h.rearrange("(t p) c -> p t c", p=128)), inc=ldh, delta=16)
    p.op("sync", lambda e: e.dma_start(
        out=sin_sb[:].rearrange("p (t c) -> p t c", c=32),
        in_=sin_h.rearrange("(t p) c -> p t c", p=128)), inc=ldh, delta=16)
    tbl_ld = ldh.v
    p.q["tensor"].append(lambda e: e.wait_ge(ldg.h, wld))
    p.q["tensor"].append(lambda e: e.wait_ge(ldh.h, tbl_ld))

    p.op("tensor", lambda e: e.matmul(ps_s[0:1, 0:1], lvec[:, 0:1], lvec[:, 1:2],
                                      start=True, stop=True))
    lam_mm = p.op("tensor", lambda e: e.matmul(ps_s[0:1, 4:5], lvec[:, 2:3], lvec[:, 3:4],
                                               start=True, stop=True), inc=te)
    p.wt("scalar", te, lam_mm)
    lam_sc = p.op("scalar", lambda e: e.activation(out=lsc[:, 0:2], in_=ps_s[0:1, 0:8:4],
                                                   func=AF.Exp), inc=sc)
    p.wt("vector", sc, lam_sc)
    p.op("vector", lambda e: e.tensor_tensor(out=lsc[:, 2:3], in0=lsc[:, 0:1],
                                             in1=lsc[:, 1:2], op=ALU.subtract), inc=ve)
    p.wt("vector", ve)
    p.op("vector", lambda e: e.tensor_scalar(out=lsc[:, 2:3], in0=lsc[:, 2:3],
                                             scalar1=0.8, scalar2=0.9, op0=ALU.add, op1=ALU.min), inc=ve)
    p.wt("vector", ve)
    p.op("vector", lambda e: e.tensor_scalar(out=lsc[:, 3:4], in0=lsc[:, 2:3],
                                             scalar1=0.1, scalar2=-1.0, op0=ALU.max, op1=ALU.mult), inc=ve)
    nlam_ve = ve.v
    onesf = p.sbuf("onesf2", [1, 128], F32)
    p.op("gpsimd", lambda e: e.memset(onesf[:], 1.0), inc=ldg, delta=0) if False else None
    gp = p.sem("gp")
    p.op("gpsimd", lambda e: e.memset(onesf[:], 1.0), inc=gp)
    p.op("gpsimd", lambda e: e.memset(ones_b1[:], 1.0), inc=gp)
    p.op("gpsimd", lambda e: e.memset(v_aug[:], 1.0), inc=gp)
    ones_gp = gp.v
    p.wt("tensor", ve, nlam_ve)
    p.wt("tensor", gp, ones_gp)
    nl_mm = p.op("tensor", lambda e: e.matmul(ps_s[0:128, 0:1] if False else ps_v[:, 0:1],
                                              onesf[:, 0:128], lsc[:, 3:4],
                                              start=True, stop=True), inc=te)
    p.wt("vector", te, nl_mm)
    p.op("vector", lambda e: e.tensor_copy(out=nlam_bc[:], in_=ps_v[:, 0:1]), inc=ve)
    nlam_bc_ve = ve.v

    stats_done = [0] * NRT_
    xrld = [0] * NRT_
    for t in range(NRT_):
        sl = t % 2
        if t >= 2:
            p.wt("sync", ve, stats_done[t - 2])
        p.op("sync", lambda e, t=t, sl=sl: e.dma_start(
            out=xr_sb[:, sl * D:(sl + 1) * D], in_=xh_h[t * 128:(t + 1) * 128, :]),
            inc=ldr[sl], delta=16)
        xrld[t] = ldr[sl].v
        p.wt("vector", ldr[sl], xrld[t])
        p.op("vector", lambda e, sl=sl: e.bn_stats(out=st6[:, 0:6], in_=xr_sb[:, sl * D:sl * D + 512]), inc=ve)
        p.wt("vector", ve)
        p.op("vector", lambda e, sl=sl: e.bn_stats(out=st6[:, 6:12], in_=xr_sb[:, sl * D + 512:sl * D + 1024]), inc=ve)
        p.wt("vector", ve)
        p.op("vector", lambda e, t=t: e.bn_aggr(out=mvbuf[:, 2 * t:2 * t + 2], in_=st6[:, 0:12]), inc=ve)
        p.wt("vector", ve)
        stats_done[t] = ve.v
    p.op("vector", lambda e: e.tensor_scalar(out=rstd[:, 0:NRT_], in0=mvbuf[:, 1:2 * NRT_:2],
                                             scalar1=float(EPS), scalar2=None, op0=ALU.add), inc=ve)
    p.wt("scalar", ve)
    p.op("scalar", lambda e: e.sqrt(out=rstd[:, 0:NRT_], in_=rstd[:, 0:NRT_]), inc=sc)
    p.wt("vector", sc)
    p.op("vector", lambda e: e.reciprocal(out=rstd[:, 0:NRT_], in_=rstd[:, 0:NRT_]), inc=ve)
    p.op("vector", lambda e: e.tensor_scalar(out=nmean[:, 0:NRT_], in0=mvbuf[:, 0:2 * NRT_:2],
                                             scalar1=-1.0, scalar2=None, op0=ALU.mult), inc=ve)
    p.wt("vector", ve)
    stats_all = ve.v

    def emit_rope(dst_bf, src_ap, t, nheads, ve, scaled=False):
        cosb = cos_sb[:, 32 * t:32 * (t + 1)].unsqueeze(1).broadcast_to((128, nheads, 32))
        sinb = sin_sb[:, 32 * t:32 * (t + 1)].unsqueeze(1).broadcast_to((128, nheads, 32))
        se = src_ap.rearrange("p (h two c) -> p h two c", two=2, c=32)[:, :, 0, :]
        so = src_ap.rearrange("p (h two c) -> p h two c", two=2, c=32)[:, :, 1, :]
        de = dst_bf.rearrange("p (h two c) -> p h two c", two=2, c=32)[:, :, 0, :]
        do = dst_bf.rearrange("p (h two c) -> p h two c", two=2, c=32)[:, :, 1, :]
        tmp = q_tm
        p.op("vector", lambda e: e.tensor_tensor(out=vtmp2[:].rearrange(
            "p (h c) -> p h c", c=32)[:, 0:nheads, :], in0=se, in1=cosb, op=ALU.mult), inc=ve)
        p.op("vector", lambda e: e.tensor_tensor(out=vtmp3[:].rearrange(
            "p (h c) -> p h c", c=32)[:, 0:nheads, :], in0=so, in1=sinb, op=ALU.mult), inc=ve)
        p.wt("vector", ve)
        p.op("vector", lambda e: e.tensor_tensor(
            out=de, in0=vtmp2[:].rearrange("p (h c) -> p h c", c=32)[:, 0:nheads, :],
            in1=vtmp3[:].rearrange("p (h c) -> p h c", c=32)[:, 0:nheads, :], op=ALU.subtract), inc=ve)
        p.wt("vector", ve)
        p.op("vector", lambda e: e.tensor_tensor(out=vtmp2[:].rearrange(
            "p (h c) -> p h c", c=32)[:, 0:nheads, :], in0=se, in1=sinb, op=ALU.mult), inc=ve)
        p.op("vector", lambda e: e.tensor_tensor(out=vtmp3[:].rearrange(
            "p (h c) -> p h c", c=32)[:, 0:nheads, :], in0=so, in1=cosb, op=ALU.mult), inc=ve)
        p.wt("vector", ve)
        p.op("vector", lambda e: e.tensor_tensor(
            out=do, in0=vtmp2[:].rearrange("p (h c) -> p h c", c=32)[:, 0:nheads, :],
            in1=vtmp3[:].rearrange("p (h c) -> p h c", c=32)[:, 0:nheads, :], op=ALU.add), inc=ve)

    vtmp2 = p.sbuf("vtmp2", [128, 512], F32)
    vtmp3 = p.sbuf("vtmp3", [128, 512], F32)

    norm_done = [0] * NRT_
    tr_done = [0] * NRT_
    lnt_done = [0] * NRT_
    proj_te = [0] * NRT_
    for t in range(NRT_):
        sl = t % 2
        par = t % 2
        p.wt("sync", ve, stats_all if t < 2 else norm_done[t - 2])
        p.op("sync", lambda e, t=t, sl=sl: e.dma_start(
            out=xr_sb[:, sl * D:(sl + 1) * D], in_=xh_h[t * 128:(t + 1) * 128, :]),
            inc=ldr[sl], delta=16)
        xrld[t] = ldr[sl].v
        p.wt("vector", ldr[sl], xrld[t])
        if t >= 1:
            p.wt("vector", te, tr_done[t - 1])
        emit_norm_tile(p, xr_sb[:, sl * D:(sl + 1) * D], ln_bf[:],
                       nmean[:, t:t + 1], rstd[:, t:t + 1], ve)
        norm_done[t] = ve.v
        pst = ps_t if par == 0 else ps_t2
        p.wt("tensor", ve, norm_done[t])
        if t >= 2:
            p.wt("tensor", ve, lnt_done[t - 2])
        for j in range(8):
            p.op("tensor", lambda e, j=j, pst=pst: e.transpose(
                pst[:, j * 128:(j + 1) * 128], ln_bf[:, j * 128:(j + 1) * 128], ident[:]),
                inc=(te if j == 7 else None))
        tr_done[t] = te.v
        p.wt("vector", te, tr_done[t])
        if t >= 2:
            p.wt("vector", te, proj_te[t - 2])
        p.op("vector", lambda e, par=par, pst=pst: e.tensor_copy(
            out=rlnT[:, par * D:(par + 1) * D], in_=pst[:, :]), inc=ve)
        lnt_done[t] = ve.v
        p.wt("tensor", ve, lnt_done[t])

        def proj(dst_f, w_sb, width, do_rope, scaled, t=t, par=par, do_transpose=True):
            nspan = width // 512 if width >= 512 else 1
            spw = 512 if width >= 512 else width
            for spi in range(nspan):
                ps = ps_a if spi % 2 == 0 else ps_b
                p.wt("tensor", ve)
                for k in range(8):
                    p.op("tensor", lambda e, k=k, spi=spi, ps=ps, w_sb=w_sb, spw=spw: e.matmul(
                        ps[:, 0:spw], rlnT[:, par * D + k * 128:par * D + (k + 1) * 128],
                        w_sb[:, k * width + spi * spw: k * width + spi * spw + spw],
                        start=(k == 0), stop=(k == 7)),
                        inc=(te if k == 7 else None))
                mmv = te.v
                p.wt("vector", te, mmv)
                p.op("vector", lambda e, spi=spi, ps=ps, spw=spw: e.tensor_copy(
                    out=q_tm[:, spi * 512:spi * 512 + spw], in_=ps[:, 0:spw]), inc=ve)
            evv = ve.v
            p.wt("vector", ve, evv)
            nh = width // 64
            if do_rope:
                emit_rope(q_ro[:, 0:width], q_tm[:, 0:width], t, nh, ve, scaled)
            else:
                p.op("vector", lambda e, width=width: e.tensor_copy(
                    out=q_ro[:, 0:width], in_=q_tm[:, 0:width]), inc=ve)
            rv = ve.v
            if not do_transpose:
                p.wt("vector", ve, rv)
                p.op("vector", lambda e: dst_f(e, q_ro), inc=ve)
                p.wt("tensor", ve)
                return ve.v
            p.wt("tensor", ve, rv)
            pst2 = ps_t2 if par == 0 else ps_t
            nj = width // 128
            for j in range(nj):
                p.op("tensor", lambda e, j=j, pst2=pst2: e.transpose(
                    pst2[:, j * 128:(j + 1) * 128], q_ro[:, j * 128:(j + 1) * 128], ident[:]),
                    inc=(te if j == nj - 1 else None))
            trv = te.v
            p.wt("vector", te, trv)
            p.op("vector", lambda e, pst2=pst2: dst_f(e, pst2), inc=ve)
            p.wt("tensor", ve)
            return ve.v

        proj(lambda e, ps, t=t: e.tensor_copy(
            out=q1f[:].rearrange("p (g n) -> p g n", n=RGN)[:, :, t * 128:(t + 1) * 128],
            in_=ps[:].rearrange("p (g c) -> p g c", c=128)), wq1, D, True, True)
        proj(lambda e, ps, t=t: e.tensor_copy(
            out=q2f[:].rearrange("p (g n) -> p g n", n=RGN)[:, :, t * 128:(t + 1) * 128],
            in_=ps[:].rearrange("p (g c) -> p g c", c=128)), wq2, D, True, True)
        proj(lambda e, ps, t=t: e.tensor_copy(
            out=k1f[:, t * 128:(t + 1) * 128], in_=ps[:, 0:128]), wk1, 128, True, False)
        proj(lambda e, ps, t=t: e.tensor_copy(
            out=k2f[:, t * 128:(t + 1) * 128], in_=ps[:, 0:128]), wk2, 128, True, False)
        proj(lambda e, ps, t=t: e.tensor_copy(
            out=v_aug[:].rearrange("p (n c) -> p n c", c=130)[:, t, :].rearrange(
                "p (kv c) -> p kv c", c=65)[:, :, 0:64],
            in_=ps[:, 0:128].rearrange("p (kv c) -> p kv c", c=64)[:, 0:2, :]),
            wv, 128, False, False, do_transpose=False)
        proj_te[t] = te.v
    phase1_ve = ve.v
    phase1_te = te.v

    st_done = [0, 0]
    pf_free = phase1_ve
    psb_free = phase1_te
    pt_free = phase1_ve
    pc_free = phase1_te
    pv_free = phase1_ve
    atm_free = sc.v
    aT_free = phase1_te
    exp_prev = sc.v
    xld2 = [0, 0]
    for w in range(16):
        wp = w % 2
        p.wt("sync", sty[0], st_done[0])
        p.wt("sync", sty[1], st_done[1])
        for ti in range(2):
            p.op("sync", lambda e, w=w, ti=ti: e.dma_start(
                out=x_sb[:, ti * D:(ti + 1) * D],
                in_=x_h[(2 * w + ti) * 128:(2 * w + ti + 1) * 128, :]),
                inc=ldx, delta=16)
        xw_ld = ldx.v
        for h in range(16):
            g = h // 2
            rb = (h % 2) * 64
            kv = h % 2
            p.wt("tensor", ve, pf_free)
            p.wt("tensor", sc, exp_prev)
            for b, qf, kf in ((0, q1f, k1f), (1, q2f, k2f)):
                for qh in range(2):
                    p.op("tensor", lambda e, b=b, qh=qh, qf=qf, kf=kf, g=g, rb=rb, kv=kv, w=w: e.matmul(
                        ps_f[:, (b * 2 + qh) * 256:(b * 2 + qh + 1) * 256],
                        qf[rb:rb + 64, g * RGN + 128 * w + qh * 128: g * RGN + 128 * w + (qh + 1) * 128],
                        kf[kv * 64:kv * 64 + 64, 128 * w: 128 * w + 256],
                        start=True, stop=True),
                        inc=(te if (b, qh) == (1, 1) else None))
            sc_te = te.v
            p.wt("vector", te, sc_te)
            p.wt("vector", sc, exp_prev)
            for i in range(4):
                p.op("vector", lambda e, i=i: e.reduce_max(
                    out=nmax[:, i:i + 1], in_=ps_f[:, i * 256:(i + 1) * 256],
                    axis=AX.X, negate=True), inc=(ve if i == 3 else None))
            p.wt("vector", ve)
            p.op("vector", lambda e: e.tensor_scalar(out=nmax[:, 0:4], in0=nmax[:, 0:4],
                                                     scalar1=0.125, scalar2=None, op0=ALU.mult), inc=ve)
            nm_ve = ve.v
            p.wt("scalar", ve, nm_ve)
            p.wt("scalar", te, psb_free)
            for i in range(4):
                p.op("scalar", lambda e, i=i: e.activation(
                    out=p_sb[:, i * 256:(i + 1) * 256], in_=ps_f[:, i * 256:(i + 1) * 256],
                    func=AF.Exp, bias=nmax[:, i:i + 1], scale=0.125),
                    inc=(sc if i == 3 else None))
            exp_prev = sc.v
            p.wt("vector", sc, exp_prev)
            for i in range(4):
                p.op("vector", lambda e, i=i: e.reduce_sum(
                    out=arec[:, 2:3] if False else nmax[:, i:i + 1],
                    in_=p_sb[:, i * 256:(i + 1) * 256], axis=AX.X),
                    inc=(ve if i == 3 else None))
            p.wt("vector", ve)
            p.op("vector", lambda e: e.reciprocal(out=nmax[:, 0:1], in_=nmax[:, 0:1]))
            p.op("vector", lambda e: e.reciprocal(out=nmax[:, 1:2], in_=nmax[:, 1:2]))
            p.op("vector", lambda e: e.reciprocal(out=nmax[:, 2:3], in_=nmax[:, 2:3]))
            p.op("vector", lambda e: e.reciprocal(out=nmax[:, 3:4], in_=nmax[:, 3:4]), inc=ve)
            p.wt("vector", ve)
            for i in range(4):
                p.op("vector", lambda e, i=i: e.tensor_scalar(
                    out=p_sb[:, i * 256:(i + 1) * 256], in0=p_sb[:, i * 256:(i + 1) * 256],
                    scalar1=nmax[:, i:i + 1], scalar2=None, op0=ALU.mult),
                    inc=(ve if i == 3 else None))
            pnorm_ve = ve.v
            p.wt("tensor", ve, pnorm_ve)
            p.wt("tensor", sc, exp_prev)
            p.wt("tensor", ve, pt_free)
            for idx in range(8):
                p.op("tensor", lambda e, idx=idx: e.transpose(
                    ps_t[:, idx * 128:(idx + 1) * 128], p_sb[:, idx * 128:(idx + 1) * 128],
                    ident[:]), inc=(te if idx == 7 else None))
            ptr_te = te.v
            psb_free = ptr_te
            p.wt("vector", te, ptr_te)
            p.wt("vector", te, pc_free)
            p.op("vector", lambda e: e.tensor_copy(
                out=pT[:].rearrange("p (b kh qh c) -> p b kh qh c", b=2, kh=2, qh=2),
                in_=ps_t[:].rearrange("p (b qh kh c) -> p b kh qh c", b=2, qh=2, kh=2)), inc=ve)
            ptv = ve.v
            pt_free = ptv
            p.wt("vector", ve, max(ptv, nlam_bc_ve))
            p.op("vector", lambda e: e.scalar_tensor_tensor(
                out=pc[:], in0=pT[:, 512:1024], scalar=nlam_bc[:, 0:1],
                in1=pT[:, 0:512], op0=ALU.mult, op1=ALU.add), inc=ve)
            pc_ve = ve.v
            p.wt("tensor", ve, pc_ve)
            p.wt("tensor", ve, pv_free)
            for qh in range(2):
                for kh in range(2):
                    p.op("tensor", lambda e, qh=qh, kh=kh, w=w, kv=kv: e.matmul(
                        ps_v[:, qh * 68: qh * 68 + 64],
                        pc[:, kh * 256 + qh * 128: kh * 256 + (qh + 1) * 128],
                        v_aug[:, (w + kh) * 130 + kv * 65: (w + kh) * 130 + kv * 65 + 64],
                        start=(kh == 0), stop=(kh == 1)),
                        inc=(te if (qh, kh) == (1, 1) else None))
            pv_te = te.v
            pc_free = pv_te
            p.wt("vector", te, pv_te)
            if h == 0:
                p.wt("vector", sc, atm_free)
            for qh in range(2):
                p.op("vector", lambda e, qh=qh, h=h: e.tensor_copy(
                    out=a_tm[:, qh * D + h * 64: qh * D + h * 64 + 64],
                    in_=ps_v[:, qh * 68: qh * 68 + 64]),
                    inc=(ve if qh == 1 else None))
            pv_free = ve.v
        heads_ve = ve.v
        p.wt("scalar", ve, heads_ve)
        p.wt("scalar", te, aT_free)
        cast_sc = p.op("scalar", lambda e: e.copy(out=a_bf[:], in_=a_tm[:]), inc=sc)
        atm_free = cast_sc
        p.wt("tensor", sc, cast_sc)
        p.wt("tensor", ve, pt_free)
        for ti in range(2):
            for j in range(8):
                p.op("tensor", lambda e, ti=ti, j=j: e.transpose(
                    ps_t[:, (j if ti == 0 else j) * 128:(j + 1) * 128] if False else
                    (ps_t if ti == 0 else ps_t2)[:, j * 128:(j + 1) * 128],
                    a_bf[:, ti * D + j * 128: ti * D + (j + 1) * 128], ident[:]),
                    inc=(te if j == 7 else None))
        aTr_te = te.v
        psb_free = max(psb_free, aTr_te) if False else psb_free
        p.wt("vector", te, aTr_te)
        p.wt("vector", te, phase1_te if w == 0 else wout_te)
        for ti in range(2):
            pstx = ps_t if ti == 0 else ps_t2
            p.op("vector", lambda e, ti=ti, pstx=pstx: e.tensor_copy(
                out=aT[:].rearrange("p (j two c) -> p j two c", two=2, c=128)[:, :, ti, :],
                in_=pstx[:].rearrange("p (j c) -> p j c", c=128)), inc=(ve if ti == 1 else None))
        aT_ve = ve.v
        pt_free = max(pt_free, aT_ve)
        aT_free = aTr_te
        p.wt("tensor", ve, aT_ve)
        p.wt("tensor", sc, exp_prev)
        p.wt("vector", ldx, xw_ld)
        for ti in range(2):
            p.wt("tensor", ve, pf_free)
            for sp in range(2):
                for j in range(8):
                    p.op("tensor", lambda e, ti=ti, sp=sp, j=j: e.matmul(
                        ps_f[:, sp * 512:(sp + 1) * 512],
                        aT[:, (j * 2 + ti) * 128:(j * 2 + ti + 1) * 128],
                        wout[:, j * D + sp * 512: j * D + (sp + 1) * 512],
                        start=(j == 0), stop=False))
                p.op("tensor", lambda e, sp=sp: e.matmul(
                    ps_f[:, sp * 512:(sp + 1) * 512],
                    ones_b1[:, 0:128], bout[:, sp * 512:(sp + 1) * 512],
                    start=False, stop=True),
                    inc=(te if sp == 1 else None))
            wout_te = te.v
            p.wt("vector", te, wout_te)
            for sp in range(2):
                p.op("vector", lambda e, ti=ti, sp=sp: e.tensor_tensor(
                    out=y_sb[:, ti * D + sp * 512: ti * D + (sp + 1) * 512],
                    in0=ps_f[:, sp * 512:(sp + 1) * 512],
                    in1=x_sb[:, ti * D + sp * 512: ti * D + (sp + 1) * 512], op=ALU.add),
                    inc=(ve if sp == 1 else None))
            pf_free = ve.v
        yv = ve.v
        p.wt("scalar", ve, yv)
        for ti in range(2):
            p.op("scalar", lambda e, w=w, ti=ti: e.dma_start(
                out=y_h[(2 * w + ti) * 128:(2 * w + ti + 1) * 128, :],
                in_=y_sb[:, ti * D:(ti + 1) * D]), inc=sty[wp], delta=16)
        st_done[wp] = sty[wp].v
    p.wt("vector", sty[0], sty[0].v)
    p.wt("vector", sty[1], sty[1].v)
    return p



SEG = 1024
NSEG = 4


def build_l1a():
    p = P()
    x_h = p.inp("x", [R, D])
    xh_h = p.inp("xh", [SEG, D])
    wq_h = p.inp("wq", [D, D])
    wkv_h = p.inp("wkv", [D, D])
    wo_h = p.inp("wo", [D, D])
    wof_h = p.inp("wof", [D, D])
    bo_h = p.inp("bo", [1, D])
    bof_h = p.inp("bof", [1, D])
    cos_h = p.inp("cosr", [SEG + R, 32])
    sin_h = p.inp("sinr", [SEG + R, 32])
    msk0_h = p.inp("msk0", [1, 1])
    tri_h = p.inp("trimask", [128, 128])
    ident_h = p.inp("ident", [128, 128])
    y_h = p.out("y", [R, D])

    NTA = NSEG * 8 + 8

    x_sb = p.sbuf("x_sb", [128, 2 * D], F32)
    ident = p.sbuf("ident_sb", [128, 128], BF16)
    wq = p.sbuf("wq_sb", [128, 8 * D], BF16)
    wkv = p.sbuf("wkv_sb", [128, 8 * D], BF16)
    wo = p.sbuf("wo_sb", [128, 8 * D], BF16)
    wof = p.sbuf("wof_sb", [128, 8 * D], BF16)
    bo = p.sbuf("bo_sb", [1, 2 * D], BF16)
    tri = p.sbuf("tri_sb", [128, 128], F32)
    msk0 = p.sbuf("msk0_sb", [1, 1], F32)
    msk0_bc = p.sbuf("msk0_bc", [128, 1], F32)
    onesf = p.sbuf("onesf_sb", [1, 128], F32)
    ones_b1 = p.sbuf("ones_b1s", [1, 128], BF16)
    cos_sb = p.sbuf("cos_sb", [128, (NTA) * 32], BF16)
    sin_sb = p.sbuf("sin_sb", [128, (NTA) * 32], BF16)
    st6 = p.sbuf("st6", [128, 12], F32)
    mvbuf = p.sbuf("mvbuf", [128, 2 * NTA], F32)
    nmean = p.sbuf("nmean", [128, NTA], F32)
    rstd = p.sbuf("rstd", [128, NTA], F32)
    ln_bf = p.sbuf("ln_bf", [128, D], BF16)
    rlnT = p.sbuf("rlnT", [128, 2 * D], BF16)
    q_tm = p.sbuf("q_tm", [128, D], BF16)
    q_ro = p.sbuf("q_ro", [128, D], BF16)
    vtmp2 = p.sbuf("vtmp2", [128, 512], F32)
    vtmp3 = p.sbuf("vtmp3", [128, 512], F32)
    qf = p.sbuf("qf", [128, 8 * SEG], BF16)
    kvf = p.sbuf("kvf", [128, 2 * 8 * SEG], BF16)
    kvt = p.sbuf("kvt", [128, 2 * 8 * D], BF16)
    p_sb = p.sbuf("p_sb", [128, 2048], BF16)
    pT = p.sbuf("pT", [128, 2048], BF16)
    nmax = p.sbuf("nmax", [128, 2], F32)
    a_tm = p.sbuf("a_tm", [128, 8 * D], BF16)
    aT = p.sbuf("aT", [128, D], BF16)
    y_sb = p.sbuf("y_sb", [128, D], F32)

    ps_t = p.psum("ps_t", [128, 1024], BF16)
    ps_t2 = p.psum("ps_t2", [128, 1024], BF16)
    ps_f = p.psum("ps_f", [128, 2048], F32)
    ps_v = p.psum("ps_v", [128, 128], F32)
    ps_a = ps_f[:, 1024:1536]
    ps_b = ps_f[:, 1536:2048]

    ldh = p.sem("ldh")
    ldr = [p.sem("ldr0"), p.sem("ldr1")]
    ldy = [p.sem("ldy0"), p.sem("ldy1")]
    ldg = p.sem("ldg")
    gp = p.sem("gp")
    sc = p.sem("sc")
    te = p.sem("te")
    ve = p.sem("ve")
    sty = [p.sem("sty0"), p.sem("sty1")]

    p.op("gpsimd", lambda e: e.dma_start(out=ident[:], in_=ident_h), inc=ldg, delta=16)
    for (dst, srch) in ((wq, wq_h), (wkv, wkv_h), (wo, wo_h), (wof, wof_h)):
        p.op("gpsimd", lambda e, dst=dst, srch=srch: e.dma_start(
            out=dst[:].rearrange("p (k c) -> p k c", c=D),
            in_=srch.rearrange("(k p) c -> p k c", p=128)), inc=ldg, delta=16)
    p.op("gpsimd", lambda e: e.dma_start(out=bo[:, 0:D], in_=bo_h), inc=ldg, delta=16)
    p.op("gpsimd", lambda e: e.dma_start(out=bo[:, D:2 * D], in_=bof_h), inc=ldg, delta=16)
    wld = ldg.v
    p.op("gpsimd", lambda e: e.memset(onesf[:], 1.0), inc=gp)
    p.op("gpsimd", lambda e: e.memset(ones_b1[:], 1.0), inc=gp)
    gp_done = gp.v
    p.op("sync", lambda e: e.dma_start(out=tri[:], in_=tri_h), inc=ldh, delta=16)
    p.op("sync", lambda e: e.dma_start(out=msk0[:], in_=msk0_h), inc=ldh, delta=16)
    p.op("gpsimd", lambda e: e.dma_start(
        out=cos_sb[:].rearrange("p (t c) -> p t c", c=32),
        in_=cos_h.rearrange("(t p) c -> p t c", p=128)), inc=ldg, delta=16)
    p.op("gpsimd", lambda e: e.dma_start(
        out=sin_sb[:].rearrange("p (t c) -> p t c", c=32),
        in_=sin_h.rearrange("(t p) c -> p t c", p=128)), inc=ldg, delta=16)
    wld = ldg.v
    tbl_ld = ldh.v
    p.q["tensor"].append(lambda e: e.wait_ge(ldg.h, wld))
    p.q["tensor"].append(lambda e: e.wait_ge(ldh.h, tbl_ld))
    p.q["tensor"].append(lambda e: e.wait_ge(gp.h, gp_done))
    p.q["vector"].append(lambda e: e.wait_ge(ldg.h, wld))

    mb_mm = p.op("tensor", lambda e: e.matmul(ps_v[:, 0:1], onesf[:, 0:128], msk0[:, 0:1],
                                              start=True, stop=True), inc=te)
    p.wt("vector", te, mb_mm)
    p.op("vector", lambda e: e.tensor_copy(out=msk0_bc[:], in_=ps_v[:, 0:1]), inc=ve)
    msk0_ve = ve.v

    def rope_l1(dst_bf, src_ap, ti, nheads):
        cosb = cos_sb[:, 32 * ti:32 * (ti + 1)].unsqueeze(1).broadcast_to((128, nheads, 32))
        sinb = sin_sb[:, 32 * ti:32 * (ti + 1)].unsqueeze(1).broadcast_to((128, nheads, 32))
        se = src_ap.rearrange("p (h two c) -> p h two c", two=2, c=32)[:, :, 0, :]
        so = src_ap.rearrange("p (h two c) -> p h two c", two=2, c=32)[:, :, 1, :]
        de = dst_bf.rearrange("p (h two c) -> p h two c", two=2, c=32)[:, :, 0, :]
        do = dst_bf.rearrange("p (h two c) -> p h two c", two=2, c=32)[:, :, 1, :]
        v2 = vtmp2[:].rearrange("p (h c) -> p h c", c=32)[:, 0:nheads, :]
        v3 = vtmp3[:].rearrange("p (h c) -> p h c", c=32)[:, 0:nheads, :]
        p.op("vector", lambda e: e.tensor_tensor(out=v2, in0=se, in1=cosb, op=ALU.mult), inc=ve)
        p.op("vector", lambda e: e.tensor_tensor(out=v3, in0=so, in1=sinb, op=ALU.mult), inc=ve)
        p.wt("vector", ve)
        p.op("vector", lambda e: e.tensor_tensor(out=de, in0=v2, in1=v3, op=ALU.subtract), inc=ve)
        p.wt("vector", ve)
        p.op("vector", lambda e: e.tensor_tensor(out=v2, in0=so, in1=cosb, op=ALU.mult), inc=ve)
        p.op("vector", lambda e: e.tensor_tensor(out=v3, in0=se, in1=sinb, op=ALU.mult), inc=ve)
        p.wt("vector", ve)
        p.op("vector", lambda e: e.tensor_tensor(out=do, in0=v2, in1=v3, op=ALU.add), inc=ve)

    stats_done = [0] * NTA
    xld = [0] * NTA
    for ti in range(NTA):
        sl = ti % 2
        srch = xh_h if ti < 8 else x_h
        roff = ti * 128 if ti < 8 else (ti - 8) * 128
        if ti >= 2:
            p.wt("sync", ve, stats_done[ti - 2])
        p.op("sync", lambda e, sl=sl, srch=srch, roff=roff: e.dma_start(
            out=x_sb[:, sl * D:(sl + 1) * D], in_=srch[roff:roff + 128, :]),
            inc=ldr[sl], delta=16)
        xld[ti] = ldr[sl].v
        p.wt("vector", ldr[sl], xld[ti])
        p.op("vector", lambda e, sl=sl: e.bn_stats(out=st6[:, 0:6], in_=x_sb[:, sl * D:sl * D + 512]), inc=ve)
        p.wt("vector", ve)
        p.op("vector", lambda e, sl=sl: e.bn_stats(out=st6[:, 6:12], in_=x_sb[:, sl * D + 512:sl * D + 1024]), inc=ve)
        p.wt("vector", ve)
        p.op("vector", lambda e, ti=ti: e.bn_aggr(out=mvbuf[:, 2 * ti:2 * ti + 2], in_=st6[:, 0:12]), inc=ve)
        p.wt("vector", ve)
        stats_done[ti] = ve.v
    p.op("vector", lambda e: e.tensor_scalar(out=rstd[:, 0:NTA], in0=mvbuf[:, 1:2 * NTA:2],
                                             scalar1=float(EPS), scalar2=None, op0=ALU.add), inc=ve)
    p.wt("scalar", ve)
    p.op("scalar", lambda e: e.sqrt(out=rstd[:, 0:NTA], in_=rstd[:, 0:NTA]), inc=sc)
    p.wt("vector", sc)
    p.op("vector", lambda e: e.reciprocal(out=rstd[:, 0:NTA], in_=rstd[:, 0:NTA]), inc=ve)
    p.op("vector", lambda e: e.tensor_scalar(out=nmean[:, 0:NTA], in0=mvbuf[:, 0:2 * NTA:2],
                                             scalar1=-1.0, scalar2=None, op0=ALU.mult), inc=ve)
    p.wt("vector", ve)
    stats_all = ve.v

    norm_prev = [stats_all]
    tr_prev = [0]
    lnt_prev = [0, 0]
    proj_prev = [0, 0]

    def prep_tile(ti, make_q, kv_ring_slot, seg_tok):
        sl = ti % 2
        par = ti % 2
        srch = xh_h if ti < 8 else x_h
        roff = ti * 128 if ti < 8 else (ti - 8) * 128
        p.wt("sync", ve, norm_prev[0])
        p.op("sync", lambda e, sl=sl, srch=srch, roff=roff: e.dma_start(
            out=x_sb[:, sl * D:(sl + 1) * D], in_=srch[roff:roff + 128, :]),
            inc=ldr[sl], delta=16)
        xldv = ldr[sl].v
        p.wt("vector", ldr[sl], xldv)
        p.wt("vector", te, tr_prev[0])
        emit_norm_tile(p, x_sb[:, sl * D:(sl + 1) * D], ln_bf[:],
                       nmean[:, ti:ti + 1], rstd[:, ti:ti + 1], ve)
        norm_prev[0] = ve.v
        pst = ps_t if par == 0 else ps_t2
        p.wt("tensor", ve, norm_prev[0])
        p.wt("tensor", ve, lnt_prev[par])
        for j in range(8):
            p.op("tensor", lambda e, j=j, pst=pst: e.transpose(
                pst[:, j * 128:(j + 1) * 128], ln_bf[:, j * 128:(j + 1) * 128], ident[:]),
                inc=(te if j == 7 else None))
        tr_prev[0] = te.v
        p.wt("vector", te, tr_prev[0])
        p.wt("vector", te, proj_prev[par])
        p.op("vector", lambda e, par=par, pst=pst: e.tensor_copy(
            out=rlnT[:, par * D:(par + 1) * D], in_=pst[:, :]), inc=ve)
        lnt_prev[par] = ve.v
        p.wt("tensor", ve, lnt_prev[par])

        def one_proj(w_sb, dst_fm, dst_tm):
            for spi in range(2):
                ps = ps_a if spi == 0 else ps_b
                p.wt("tensor", ve)
                p.wt("tensor", sc)
                for k in range(8):
                    p.op("tensor", lambda e, k=k, spi=spi, ps=ps, w_sb=w_sb, par=par: e.matmul(
                        ps, rlnT[:, par * D + k * 128:par * D + (k + 1) * 128],
                        w_sb[:, k * D + spi * 512: k * D + spi * 512 + 512],
                        start=(k == 0), stop=(k == 7)),
                        inc=(te if k == 7 else None))
                mmv = te.v
                p.wt("vector", te, mmv)
                p.op("vector", lambda e, spi=spi, ps=ps: e.tensor_copy(
                    out=q_tm[:, spi * 512:(spi + 1) * 512], in_=ps), inc=ve)
            p.wt("vector", ve)
            rope_l1(q_ro[:], q_tm[:], ti, 16)
            rv = ve.v
            if dst_tm is not None:
                p.wt("vector", ve, rv)
                p.op("vector", lambda e, dst_tm=dst_tm: e.tensor_copy(out=dst_tm, in_=q_ro[:]), inc=ve)
            rv2 = ve.v
            p.wt("tensor", ve, rv2)
            pst2 = ps_t2 if par == 0 else ps_t
            for j in range(8):
                p.op("tensor", lambda e, j=j, pst2=pst2: e.transpose(
                    pst2[:, j * 128:(j + 1) * 128], q_ro[:, j * 128:(j + 1) * 128], ident[:]),
                    inc=(te if j == 7 else None))
            trv = te.v
            p.wt("vector", te, trv)
            p.op("vector", lambda e, pst2=pst2, dst_fm=dst_fm: e.tensor_copy(
                out=dst_fm, in_=pst2[:].rearrange("p (g c) -> p g c", c=128)), inc=ve)
            p.wt("tensor", ve)

        sp = kv_ring_slot
        one_proj(wkv,
                 kvf[:].rearrange("p (s g n) -> p s g n", s=2, n=SEG)[:, sp, :, seg_tok * 128:(seg_tok + 1) * 128],
                 kvt[:, sp * 8 * D + seg_tok * D: sp * 8 * D + (seg_tok + 1) * D])
        if make_q:
            one_proj(wq,
                     qf[:].rearrange("p (g n) -> p g n", n=SEG)[:, :, seg_tok * 128:(seg_tok + 1) * 128],
                     None)
        proj_prev[par] = te.v

    for ti in range(8):
        prep_tile(ti, False, 0, ti)
    st_done = [0, 0]
    pf_free = ve.v
    psb_free = te.v
    pt_free = ve.v
    atm_te_prev = 0
    exp_prev = sc.v
    aT_te = te.v
    for s in range(NSEG):
        cur = (s + 1) % 2
        prv = s % 2
        for i in range(8):
            prep_tile(8 + s * 8 + i, True, cur, i)
        prep_done_te = te.v
        prep_done_ve = ve.v
        for h in range(16):
            g = h // 2
            rb = (h % 2) * 64
            for qi in range(8):
                nk = 8 + qi + 1
                nkc = nk * 128
                pieces = []
                nsp = (nkc + 511) // 512
                for spi in range(nsp):
                    w0 = spi * 512
                    wlen = min(512, nkc - w0)
                    if w0 < SEG:
                        l1 = min(wlen, SEG - w0)
                        pieces.append((prv, w0, l1, w0))
                    if w0 + wlen > SEG:
                        o2 = max(w0, SEG)
                        pieces.append((cur, o2 - SEG, w0 + wlen - o2, o2))
                p.wt("tensor", ve, pf_free)
                p.wt("tensor", sc, exp_prev)
                for pi, (slot, koff, klen, dsto) in enumerate(pieces):
                    p.op("tensor", lambda e, slot=slot, koff=koff, klen=klen, dsto=dsto,
                         g=g, rb=rb, qi=qi: e.matmul(
                        ps_f[:, dsto: dsto + klen],
                        qf[rb:rb + 64, g * SEG + qi * 128: g * SEG + (qi + 1) * 128],
                        kvf[:].rearrange("p (s g n) -> p s g n", s=2, n=SEG)[rb:rb + 64, slot, g, koff:koff + klen],
                        start=True, stop=True),
                        inc=(te if pi == len(pieces) - 1 else None))
                sc_te = te.v
                p.wt("vector", te, sc_te)
                p.op("vector", lambda e, qi=qi, nkc=nkc: e.tensor_tensor(
                    out=ps_f[:, nkc - 128:nkc], in0=ps_f[:, nkc - 128:nkc],
                    in1=tri[:], op=ALU.add), inc=ve)
                if s == 0:
                    p.op("vector", lambda e: e.tensor_scalar(
                        out=ps_f[:, 0:SEG], in0=ps_f[:, 0:SEG],
                        scalar1=msk0_bc[:, 0:1], scalar2=None, op0=ALU.add), inc=ve)
                p.wt("vector", ve)
                p.op("vector", lambda e, nkc=nkc: e.reduce_max(
                    out=nmax[:, 0:1], in_=ps_f[:, 0:nkc], axis=AX.X, negate=True), inc=ve)
                p.wt("vector", ve)
                p.op("vector", lambda e: e.tensor_scalar(
                    out=nmax[:, 0:1], in0=nmax[:, 0:1], scalar1=0.125, scalar2=None,
                    op0=ALU.mult), inc=ve)
                nm_ve = ve.v
                p.wt("scalar", ve, nm_ve)
                p.op("scalar", lambda e, nkc=nkc: e.activation(
                    out=p_sb[:, 0:nkc], in_=ps_f[:, 0:nkc], func=AF.Exp,
                    bias=nmax[:, 0:1], scale=0.125), inc=sc)
                exp_prev = sc.v
                p.wt("vector", sc, exp_prev)
                p.op("vector", lambda e, nkc=nkc: e.reduce_sum(
                    out=nmax[:, 1:2], in_=p_sb[:, 0:nkc], axis=AX.X), inc=ve)
                p.wt("vector", ve)
                p.op("vector", lambda e: e.reciprocal(out=nmax[:, 1:2], in_=nmax[:, 1:2]), inc=ve)
                p.wt("vector", ve)
                p.op("vector", lambda e, nkc=nkc: e.tensor_scalar(
                    out=p_sb[:, 0:nkc], in0=p_sb[:, 0:nkc], scalar1=nmax[:, 1:2],
                    scalar2=None, op0=ALU.mult), inc=ve)
                pn_ve = ve.v
                pf_free = pn_ve
                p.wt("tensor", ve, pn_ve)
                p.wt("tensor", ve, pt_free)
                for j in range(nk):
                    pstx = ps_t if j < 8 else ps_t2
                    jj = j % 8
                    p.op("tensor", lambda e, j=j, jj=jj, pstx=pstx: e.transpose(
                        pstx[:, jj * 128:(jj + 1) * 128], p_sb[:, j * 128:(j + 1) * 128],
                        ident[:]), inc=(te if j == nk - 1 else None))
                ptr_te = te.v
                psb_free = ptr_te
                p.wt("vector", te, ptr_te)
                p.op("vector", lambda e, nk=nk: e.tensor_copy(
                    out=pT[:, 0:min(nk, 8) * 128], in_=ps_t[:, 0:min(nk, 8) * 128]), inc=ve)
                if nk > 8:
                    p.op("vector", lambda e, nk=nk: e.tensor_copy(
                        out=pT[:, 1024:nk * 128], in_=ps_t2[:, 0:(nk - 8) * 128]), inc=ve)
                pt_ve = ve.v
                pt_free = pt_ve
                p.wt("tensor", ve, pt_ve)
                for j in range(nk):
                    slot = prv if j < 8 else cur
                    i2 = j if j < 8 else j - 8
                    p.op("tensor", lambda e, j=j, slot=slot, i2=i2, h=h, nk=nk: e.matmul(
                        ps_v[:, 0:64], pT[:, j * 128:(j + 1) * 128],
                        kvt[:, slot * 8 * D + i2 * D + h * 64: slot * 8 * D + i2 * D + h * 64 + 64],
                        start=(j == 0), stop=(j == nk - 1)),
                        inc=(te if j == nk - 1 else None))
                pv_te = te.v
                p.wt("vector", te, pv_te)
                if h == 0 and qi == 0:
                    p.wt("vector", te, atm_te_prev)
                p.op("vector", lambda e, qi=qi, h=h: e.tensor_copy(
                    out=a_tm[:, qi * D + h * 64: qi * D + h * 64 + 64],
                    in_=ps_v[:, 0:64]), inc=ve)
        heads_ve = ve.v
        wsel = wof if s == 0 else wo
        boff = D if s == 0 else 0
        for i in range(8):
            p.wt("tensor", ve, heads_ve)
            p.wt("tensor", ve, pt_free)
            for j in range(8):
                p.op("tensor", lambda e, j=j, i=i: e.transpose(
                    ps_t[:, j * 128:(j + 1) * 128], a_tm[:, i * D + j * 128: i * D + (j + 1) * 128],
                    ident[:]), inc=(te if j == 7 else None))
            aTr = te.v
            aT_te = aTr
            p.wt("vector", te, aTr)
            aTe_prev = p.op("vector", lambda e: e.tensor_copy(out=aT[:], in_=ps_t[:, :]), inc=ve)
            pt_free = ve.v
            p.wt("tensor", ve, pt_free)
            ypar = 0
            p.wt("sync", sty[0], st_done[0])
            p.op("sync", lambda e, s=s, i=i: e.dma_start(
                out=y_sb[:, 0:D],
                in_=x_h[(s * 8 + i) * 128:(s * 8 + i + 1) * 128, :]), inc=ldy[0], delta=16)
            yld = ldy[0].v
            for spi in range(2):
                ps = ps_a if spi == 0 else ps_b
                p.wt("tensor", ve)
                p.wt("tensor", sc)
                for j in range(8):
                    p.op("tensor", lambda e, j=j, spi=spi, ps=ps, wsel=wsel: e.matmul(
                        ps, aT[:, j * 128:(j + 1) * 128],
                        wsel[:, j * D + spi * 512: j * D + spi * 512 + 512],
                        start=(j == 0), stop=False))
                p.op("tensor", lambda e, spi=spi, ps=ps, boff=boff: e.matmul(
                    ps, ones_b1[:, 0:128], bo[:, boff + spi * 512: boff + spi * 512 + 512],
                    start=False, stop=True), inc=te)
                wmv = te.v
                p.wt("vector", te, wmv)
                p.wt("vector", ldy[ypar], yld)
                p.op("vector", lambda e, spi=spi, ps=ps: e.tensor_tensor(
                    out=y_sb[:, spi * 512:(spi + 1) * 512],
                    in0=ps, in1=y_sb[:, spi * 512:(spi + 1) * 512],
                    op=ALU.add), inc=ve)
            yv = ve.v
            p.wt("scalar", ve, yv)
            p.op("scalar", lambda e, s=s, i=i: e.dma_start(
                out=y_h[(s * 8 + i) * 128:(s * 8 + i + 1) * 128, :],
                in_=y_sb[:, 0:D]), inc=sty[0], delta=16)
            st_done[0] = sty[0].v
        atm_te_prev = te.v
    p.wt("vector", sty[0], sty[0].v)
    p.wt("vector", sty[1], sty[1].v)
    return p



def build_l2a():
    p = P()
    x_h = p.inp("x", [R, D])
    xkv_h = p.inp("xkv", [256, D])
    xf_h = p.inp("xfirst", [R, D])
    wq_h = p.inp("wq", [D, D])
    wk_h = p.inp("wk", [D, D])
    wv_h = p.inp("wv", [D, D])
    wgq_h = p.inp("wgq", [D, 512])
    wgk_h = p.inp("wgk", [D, 512])
    wgv_h = p.inp("wgv", [D, 512])
    row_h = p.inp("r_out_w", [512, D])
    wo_h = p.inp("w_out", [D, D])
    glwb_h = p.inp("glwb", [D, 16])
    grwb_h = p.inp("grwb", [D, 16])
    tria_h = p.inp("tri_a", [128, 128])
    trib_h = p.inp("tri_b", [128, 128])
    mfirst_h = p.inp("mfirst", [1, 1])
    lmask_h = p.inp("lmask", [1, 32 * 64])
    ident_h = p.inp("ident", [128, 128])
    y_h = p.out("y", [R, D])

    NTL = 2 + 32 + 32

    x_sb = p.sbuf("x_sb", [128, 2 * D], F32)
    ident = p.sbuf("ident_sb", [128, 128], BF16)
    wq = p.sbuf("wq_sb", [128, 8 * D], BF16)
    wk = p.sbuf("wk_sb", [128, 8 * D], BF16)
    wv = p.sbuf("wv_sb", [128, 8 * D], BF16)
    wo = p.sbuf("wo_sb", [128, 8 * D], BF16)
    wgq = p.sbuf("wgq_sb", [128, 8 * 512], BF16)
    wgk = p.sbuf("wgk_sb", [128, 8 * 512], BF16)
    wgv = p.sbuf("wgv_sb", [128, 8 * 512], BF16)
    row_w = p.sbuf("row_sb", [128, 4 * D], BF16)
    glwb = p.sbuf("glwb_sb", [128, 8 * 32], BF16)
    tria = p.sbuf("tria_sb", [128, 128], F32)
    trib = p.sbuf("trib_sb", [128, 128], F32)
    mfirst = p.sbuf("mfirst_sb", [1, 1], F32)
    mf_bc = p.sbuf("mf_bc", [128, 1], F32)
    lmask = p.sbuf("lmask_sb", [1, 32 * 64], BF16)
    onesf = p.sbuf("onesf_sb", [1, 128], F32)
    ones_b1 = p.sbuf("ones_b1s", [1, 128], BF16)
    odiv = p.sbuf("odiv", [128, 1], BF16)
    st6 = p.sbuf("st6", [128, 12], F32)
    mvbuf = p.sbuf("mvbuf", [128, 2 * NTL], F32)
    nmean = p.sbuf("nmean", [128, NTL], F32)
    rstd = p.sbuf("rstd", [128, NTL], F32)
    ln_bf = p.sbuf("ln_bf", [128, D], BF16)
    rlnT = p.sbuf("rlnT", [128, 2 * D], BF16)
    pooledT = p.sbuf("pooledT", [128, 8 * 64], BF16)
    gkf = p.sbuf("gkf", [128, 4 * 64], BF16)
    gvl = p.sbuf("gvl", [64, 512], BF16)
    kf = p.sbuf("kf", [128, 4 * 8 * 128], BF16)
    vt = p.sbuf("vt", [128, 4 * D], BF16)
    qf = p.sbuf("qf", [128, 8 * 128], BF16)
    gqf = p.sbuf("gqf", [128, 4 * 128], BF16)
    gat = p.sbuf("gat", [128, 32], F32)
    q_tm = p.sbuf("q_tm", [128, D], BF16)
    p_sb = p.sbuf("p_sb", [128, 384], BF16)
    pT = p.sbuf("pT", [128, 384], BF16)
    lp = p.sbuf("lp", [128, 64], BF16)
    lpT = p.sbuf("lpT", [64, 128], BF16)
    nmax = p.sbuf("nmax", [128, 2], F32)
    loc = p.sbuf("loc", [128, D], F32)
    rem = p.sbuf("rem", [128, 512], F32)
    remT = p.sbuf("remT", [128, 512], BF16)
    remh = p.sbuf("remh", [128, D], F32)
    comb = p.sbuf("comb", [128, D], F32)
    comb_bf = p.sbuf("comb_bf", [128, D], BF16)
    combT = p.sbuf("combT", [128, D], BF16)
    y_sb = p.sbuf("y_sb", [128, D], F32)

    ps_t = p.psum("ps_t", [128, 1024], BF16)
    ps_t2 = p.psum("ps_t2", [128, 1024], BF16)
    ps_a = p.psum("ps_a", [128, 512], F32)
    ps_b = p.psum("ps_b", [128, 512], F32)
    ps_s = p.psum("ps_s", [128, 512], F32)
    ps_v = p.psum("ps_v", [128, 128], F32)

    ldh = p.sem("ldh")
    ldr = [p.sem("ldr0"), p.sem("ldr1")]
    ldy = [p.sem("ldy0")]
    ldg = p.sem("ldg")
    gp = p.sem("gp")
    sc = p.sem("sc")
    te = p.sem("te")
    ve = p.sem("ve")
    sty = [p.sem("sty0")]

    p.op("gpsimd", lambda e: e.dma_start(out=ident[:], in_=ident_h), inc=ldg, delta=16)
    for (dst, srch, cw) in ((wq, wq_h, D), (wk, wk_h, D), (wv, wv_h, D), (wo, wo_h, D),
                            (wgq, wgq_h, 512), (wgk, wgk_h, 512), (wgv, wgv_h, 512)):
        p.op("gpsimd", lambda e, dst=dst, srch=srch, cw=cw: e.dma_start(
            out=dst[:].rearrange("p (k c) -> p k c", c=cw),
            in_=srch.rearrange("(k p) c -> p k c", p=128)), inc=ldg, delta=16)
    p.op("gpsimd", lambda e: e.dma_start(
        out=row_w[:].rearrange("p (k c) -> p k c", c=D),
        in_=row_h.rearrange("(k p) c -> p k c", p=128)), inc=ldg, delta=16)
    p.op("gpsimd", lambda e: e.dma_start(
        out=glwb[:].rearrange("p (k c) -> p k c", c=32)[:, :, 0:16],
        in_=glwb_h.rearrange("(k p) c -> p k c", p=128)), inc=ldg, delta=16)
    p.op("gpsimd", lambda e: e.dma_start(
        out=glwb[:].rearrange("p (k c) -> p k c", c=32)[:, :, 16:32],
        in_=grwb_h.rearrange("(k p) c -> p k c", p=128)), inc=ldg, delta=16)
    p.op("gpsimd", lambda e: e.dma_start(out=lmask[:], in_=lmask_h), inc=ldg, delta=16)
    wld = ldg.v
    p.op("gpsimd", lambda e: e.memset(onesf[:], 1.0), inc=gp)
    p.op("gpsimd", lambda e: e.memset(ones_b1[:], 1.0), inc=gp)
    p.op("gpsimd", lambda e: e.memset(odiv[:], 1.0 / 128.0), inc=gp)
    gp_done = gp.v
    p.op("sync", lambda e: e.dma_start(out=tria[:], in_=tria_h), inc=ldh, delta=16)
    p.op("sync", lambda e: e.dma_start(out=trib[:], in_=trib_h), inc=ldh, delta=16)
    p.op("sync", lambda e: e.dma_start(out=mfirst[:], in_=mfirst_h), inc=ldh, delta=16)
    tbl_ld = ldh.v
    p.q["tensor"].append(lambda e: e.wait_ge(ldg.h, wld))
    p.q["tensor"].append(lambda e: e.wait_ge(ldh.h, tbl_ld))
    p.q["tensor"].append(lambda e: e.wait_ge(gp.h, gp_done))
    p.q["vector"].append(lambda e: e.wait_ge(ldg.h, wld))

    mb = p.op("tensor", lambda e: e.matmul(ps_v[:, 0:1], onesf[:, 0:128], mfirst[:, 0:1],
                                           start=True, stop=True), inc=te)
    p.wt("vector", te, mb)
    p.op("vector", lambda e: e.tensor_copy(out=mf_bc[:], in_=ps_v[:, 0:1]), inc=ve)
    mf_ve = ve.v

    def tile_src(ti):
        if ti < 2:
            return xkv_h, ti * 128
        if ti < 34:
            return xf_h, (ti - 2) * 128
        return x_h, (ti - 34) * 128

    stats_done = [0] * NTL
    for ti in range(NTL):
        sl = ti % 2
        srch, roff = tile_src(ti)
        if ti >= 2:
            p.wt("sync", ve, stats_done[ti - 2])
        p.op("sync", lambda e, sl=sl, srch=srch, roff=roff: e.dma_start(
            out=x_sb[:, sl * D:(sl + 1) * D], in_=srch[roff:roff + 128, :]),
            inc=ldr[sl], delta=16)
        xldv = ldr[sl].v
        p.wt("vector", ldr[sl], xldv)
        p.op("vector", lambda e, sl=sl: e.bn_stats(out=st6[:, 0:6], in_=x_sb[:, sl * D:sl * D + 512]), inc=ve)
        p.wt("vector", ve)
        p.op("vector", lambda e, sl=sl: e.bn_stats(out=st6[:, 6:12], in_=x_sb[:, sl * D + 512:sl * D + 1024]), inc=ve)
        p.wt("vector", ve)
        p.op("vector", lambda e, ti=ti: e.bn_aggr(out=mvbuf[:, 2 * ti:2 * ti + 2], in_=st6[:, 0:12]), inc=ve)
        p.wt("vector", ve)
        stats_done[ti] = ve.v
    p.op("vector", lambda e: e.tensor_scalar(out=rstd[:, 0:NTL], in0=mvbuf[:, 1:2 * NTL:2],
                                             scalar1=float(EPS), scalar2=None, op0=ALU.add), inc=ve)
    p.wt("scalar", ve)
    p.op("scalar", lambda e: e.sqrt(out=rstd[:, 0:NTL], in_=rstd[:, 0:NTL]), inc=sc)
    p.wt("vector", sc)
    p.op("vector", lambda e: e.reciprocal(out=rstd[:, 0:NTL], in_=rstd[:, 0:NTL]), inc=ve)
    p.op("vector", lambda e: e.tensor_scalar(out=nmean[:, 0:NTL], in0=mvbuf[:, 0:2 * NTL:2],
                                             scalar1=-1.0, scalar2=None, op0=ALU.mult), inc=ve)
    p.wt("vector", ve)
    stats_all = ve.v

    norm_prev = [stats_all]
    tr_prev = [0]
    lnt_prev = [0, 0]
    proj_prev = [0, 0]
    pool_done = [0]

    def load_norm(ti):
        sl = ti % 2
        srch, roff = tile_src(ti)
        p.wt("sync", ve, norm_prev[0])
        p.op("sync", lambda e, sl=sl, srch=srch, roff=roff: e.dma_start(
            out=x_sb[:, sl * D:(sl + 1) * D], in_=srch[roff:roff + 128, :]),
            inc=ldr[sl], delta=16)
        xldv = ldr[sl].v
        p.wt("vector", ldr[sl], xldv)
        p.wt("vector", te, tr_prev[0])
        emit_norm_tile(p, x_sb[:, sl * D:(sl + 1) * D], ln_bf[:],
                       nmean[:, ti:ti + 1], rstd[:, ti:ti + 1], ve)
        norm_prev[0] = ve.v

    def emit_pool(lat_col):
        p.wt("tensor", ve, norm_prev[0])
        p.wt("tensor", ve, pool_done[0])
        for cc in range(8):
            p.op("tensor", lambda e, cc=cc: e.matmul(
                ps_v[:, cc * 8:cc * 8 + 1], ln_bf[:, cc * 128:(cc + 1) * 128],
                odiv[:, 0:1], start=True, stop=True),
                inc=(te if cc == 7 else None))
        pl_te = te.v
        p.wt("vector", te, pl_te)
        p.op("vector", lambda e, lat_col=lat_col: e.tensor_copy(
            out=pooledT[:].rearrange("p (cc n) -> p cc n", n=64)[:, :, lat_col:lat_col + 1],
            in_=ps_v[:, 0:64:8].rearrange("p (cc one) -> p cc one", one=1)), inc=ve)
        pool_done[0] = ve.v

    for i in range(32):
        load_norm(2 + i)
        emit_pool(i)
    for i in range(32):
        load_norm(34 + i)
        emit_pool(32 + i)
    poolT_ve = ve.v

    p.wt("tensor", ve, poolT_ve)
    for dc in range(4):
        p.wt("tensor", ve)
        for cc in range(8):
            p.op("tensor", lambda e, dc=dc, cc=cc: e.matmul(
                ps_a[:, 0:64] if False else ps_s[:, 0:64],
                wgk[:, cc * 512 + dc * 128: cc * 512 + (dc + 1) * 128],
                pooledT[:, cc * 64:(cc + 1) * 64], start=(cc == 0), stop=(cc == 7)),
                inc=(te if cc == 7 else None))
        gk_te = te.v
        p.wt("vector", te, gk_te)
        p.op("vector", lambda e, dc=dc: e.tensor_copy(
            out=gkf[:, dc * 64:(dc + 1) * 64], in_=ps_s[:, 0:64]), inc=ve)
    for cc in range(8):
        p.op("tensor", lambda e, cc=cc: e.matmul(
            ps_a[0:64, :], pooledT[:, cc * 64:(cc + 1) * 64],
            wgv[:, cc * 512:(cc + 1) * 512], start=(cc == 0), stop=(cc == 7)),
            inc=(te if cc == 7 else None))
    gv_te = te.v
    p.wt("vector", te, gv_te)
    p.op("vector", lambda e: e.tensor_copy(out=gvl[:], in_=ps_a[0:64, :]), inc=ve)
    lat_ve = ve.v

    def project(w_sb, width, dst_fm_fn, dst_tm_ap, par):
        nspan = (width + 511) // 512
        for spi in range(nspan):
            spw = min(512, width - spi * 512)
            ps = ps_a if spi % 2 == 0 else ps_b
            p.wt("tensor", ve)
            for k in range(8):
                p.op("tensor", lambda e, k=k, spi=spi, ps=ps, w_sb=w_sb, par=par,
                     spw=spw, width=width: e.matmul(
                    ps[:, 0:spw], rlnT[:, par * D + k * 128:par * D + (k + 1) * 128],
                    w_sb[:, k * width + spi * 512: k * width + spi * 512 + spw],
                    start=(k == 0), stop=(k == 7)),
                    inc=(te if k == 7 else None))
            mmv = te.v
            p.wt("vector", te, mmv)
            p.op("vector", lambda e, spi=spi, ps=ps, spw=spw: e.tensor_copy(
                out=q_tm[:, spi * 512:spi * 512 + spw], in_=ps[:, 0:spw]), inc=ve)
        ev = ve.v
        if dst_tm_ap is not None:
            p.wt("vector", ve, ev)
            p.op("vector", lambda e, dst_tm_ap=dst_tm_ap, width=width: e.tensor_copy(
                out=dst_tm_ap, in_=q_tm[:, 0:width]), inc=ve)
        if dst_fm_fn is not None:
            p.wt("tensor", ve, ve.v)
            pst2 = ps_t2 if par == 0 else ps_t
            nj = width // 128
            for j in range(nj):
                p.op("tensor", lambda e, j=j, pst2=pst2: e.transpose(
                    pst2[:, j * 128:(j + 1) * 128], q_tm[:, j * 128:(j + 1) * 128], ident[:]),
                    inc=(te if j == nj - 1 else None))
            trv = te.v
            p.wt("vector", te, trv)
            p.op("vector", lambda e, pst2=pst2: dst_fm_fn(e, pst2), inc=ve)
            p.wt("tensor", ve)

    def prep_kv(ti, slot):
        par = ti % 2
        load_norm(ti)
        pst = ps_t if par == 0 else ps_t2
        p.wt("tensor", ve, norm_prev[0])
        p.wt("tensor", ve, lnt_prev[par])
        for j in range(8):
            p.op("tensor", lambda e, j=j, pst=pst: e.transpose(
                pst[:, j * 128:(j + 1) * 128], ln_bf[:, j * 128:(j + 1) * 128], ident[:]),
                inc=(te if j == 7 else None))
        tr_prev[0] = te.v
        p.wt("vector", te, tr_prev[0])
        p.wt("vector", te, proj_prev[par])
        p.op("vector", lambda e, par=par, pst=pst: e.tensor_copy(
            out=rlnT[:, par * D:(par + 1) * D], in_=pst[:, :]), inc=ve)
        lnt_prev[par] = ve.v
        p.wt("tensor", ve, lnt_prev[par])
        project(wk, D, lambda e, ps, slot=slot: e.tensor_copy(
            out=kf[:].rearrange("p (s g c) -> p s g c", s=4, c=128)[:, slot, :, :],
            in_=ps[:].rearrange("p (g c) -> p g c", c=128)), None, par)
        project(wv, D, None, vt[:, slot * D:(slot + 1) * D], par)
        proj_prev[par] = te.v

    prep_kv(0, 2)
    prep_kv(1, 3)

    st_done = [0]
    pf_free = ve.v
    exp_prev = sc.v
    psb_te = te.v
    pt_free = ve.v
    comb_te_prev = te.v
    for gt in range(32):
        ti = 34 + gt
        par = ti % 2
        slot = gt % 4
        prep_kv(ti, slot)
        project(wq, D, lambda e, ps: e.tensor_copy(
            out=qf[:].rearrange("p (g c) -> p g c", c=128),
            in_=ps[:].rearrange("p (g c) -> p g c", c=128)), q_tm[:, 0:D] if False else None, par)
        project(wgq, 512, lambda e, ps: e.tensor_copy(
            out=gqf[:].rearrange("p (g c) -> p g c", c=128),
            in_=ps[:, 0:512].rearrange("p (g c) -> p g c", c=128)), None, par)
        proj_prev[par] = te.v
        qf_ve = ve.v
        p.wt("tensor", ve, qf_ve)
        for g in range(8):
            p.op("tensor", lambda e, g=g: e.matmul(
                ps_v[:, 0:32], qf[:, g * 128:(g + 1) * 128], glwb[:, g * 32:(g + 1) * 32],
                start=(g == 0), stop=(g == 7)), inc=(te if g == 7 else None))
        gat_te = te.v
        p.wt("scalar", te, gat_te)
        p.wt("scalar", ve)
        gat_sc = p.op("scalar", lambda e: e.activation(out=gat[:], in_=ps_v[:, 0:32],
                                                       func=AF.Sigmoid), inc=sc)
        exp_prev = gat_sc
        p.wt("vector", sc, gat_sc)
        for h in range(16):
            g = h // 2
            rb = (h % 2) * 64
            ks = [(gt - 2) % 4 + (4 if gt < 2 else 0), (gt - 1) % 4 + (4 if gt < 1 else 0), slot]
            s0 = (gt - 2) + 2
            ksl = [((gt - 2) % 4) if gt >= 2 else (2 + gt), ((gt - 1) % 4) if gt >= 1 else (3 if gt == 0 else None), slot]
            ksl[1] = ((gt - 1) % 4) if gt >= 1 else 3
            p.wt("tensor", ve, pf_free)
            p.wt("tensor", sc, exp_prev)
            for j, sl2 in enumerate(ksl):
                p.op("tensor", lambda e, j=j, sl2=sl2, g=g, rb=rb: e.matmul(
                    ps_s[:, j * 128:(j + 1) * 128],
                    qf[rb:rb + 64, g * 128:(g + 1) * 128],
                    kf[:].rearrange("p (s g c) -> p s g c", s=4, c=128)[rb:rb + 64, sl2, g, :],
                    start=True, stop=True), inc=(te if j == 2 else None))
            sc_te = te.v
            p.wt("vector", te, sc_te)
            p.op("vector", lambda e: e.tensor_tensor(out=ps_s[:, 0:128], in0=ps_s[:, 0:128],
                                                     in1=tria[:], op=ALU.add), inc=ve)
            p.op("vector", lambda e: e.tensor_tensor(out=ps_s[:, 256:384], in0=ps_s[:, 256:384],
                                                     in1=trib[:], op=ALU.add), inc=ve)
            if gt < 2:
                nneg = 2 - gt
                p.wt("vector", ve)
                p.op("vector", lambda e, nneg=nneg: e.tensor_scalar(
                    out=ps_s[:, 0:nneg * 128], in0=ps_s[:, 0:nneg * 128],
                    scalar1=mf_bc[:, 0:1], scalar2=None, op0=ALU.add), inc=ve)
            p.wt("vector", ve)
            p.op("vector", lambda e: e.reduce_max(out=nmax[:, 0:1], in_=ps_s[:, 0:384],
                                                  axis=AX.X, negate=True), inc=ve)
            p.wt("vector", ve)
            p.op("vector", lambda e: e.tensor_scalar(out=nmax[:, 0:1], in0=nmax[:, 0:1],
                                                     scalar1=0.125, scalar2=None, op0=ALU.mult), inc=ve)
            nm_ve = ve.v
            p.wt("scalar", ve, nm_ve)
            p.wt("scalar", te, psb_te)
            ex = p.op("scalar", lambda e: e.activation(out=p_sb[:], in_=ps_s[:, 0:384],
                                                       func=AF.Exp, bias=nmax[:, 0:1],
                                                       scale=0.125), inc=sc)
            exp_prev = ex
            p.wt("vector", sc, ex)
            p.op("vector", lambda e: e.reduce_sum(out=nmax[:, 1:2], in_=p_sb[:], axis=AX.X), inc=ve)
            p.wt("vector", ve)
            p.op("vector", lambda e: e.reciprocal(out=nmax[:, 1:2], in_=nmax[:, 1:2]), inc=ve)
            p.wt("vector", ve)
            p.op("vector", lambda e: e.tensor_scalar(out=p_sb[:], in0=p_sb[:],
                                                     scalar1=nmax[:, 1:2], scalar2=None,
                                                     op0=ALU.mult), inc=ve)
            pn_ve = ve.v
            pf_free = pn_ve
            p.wt("tensor", ve, pn_ve)
            p.wt("tensor", ve, pt_free)
            for j in range(3):
                p.op("tensor", lambda e, j=j: e.transpose(
                    ps_t[:, j * 128:(j + 1) * 128], p_sb[:, j * 128:(j + 1) * 128], ident[:]),
                    inc=(te if j == 2 else None))
            ptr = te.v
            psb_te = ptr
            p.wt("vector", te, ptr)
            p.op("vector", lambda e: e.tensor_copy(out=pT[:], in_=ps_t[:, 0:384]), inc=ve)
            pt_ve = ve.v
            pt_free = pt_ve
            p.wt("tensor", ve, pt_ve)
            for j, sl2 in enumerate(ksl):
                p.op("tensor", lambda e, j=j, sl2=sl2, h=h: e.matmul(
                    ps_v[:, 64:128], pT[:, j * 128:(j + 1) * 128],
                    vt[:, sl2 * D + h * 64: sl2 * D + h * 64 + 64],
                    start=(j == 0), stop=(j == 2)), inc=(te if j == 2 else None))
            pv_te = te.v
            p.wt("vector", te, pv_te)
            p.op("vector", lambda e, h=h: e.tensor_copy(
                out=loc[:, h * 64:(h + 1) * 64], in_=ps_v[:, 64:128]), inc=ve)
        loc_ve = ve.v
        for lh in range(8):
            dc = lh // 2
            rb = (lh % 2) * 64
            p.wt("tensor", ve, max(pf_free, lat_ve))
            p.wt("tensor", sc, exp_prev)
            p.op("tensor", lambda e, dc=dc, rb=rb: e.matmul(
                ps_s[:, 0:64], gqf[rb:rb + 64, dc * 128:(dc + 1) * 128],
                gkf[rb:rb + 64, dc * 64 * 0 + 0:64] if False else gkf[rb:rb + 64, dc * 64:(dc + 1) * 64],
                start=True, stop=False))
            p.op("tensor", lambda e, gt=gt: e.matmul(
                ps_s[:, 0:64], ones_b1[:, 0:128], lmask[0:1, gt * 64:(gt + 1) * 64],
                start=False, stop=True), inc=te)
            lsc_te = te.v
            p.wt("vector", te, lsc_te)
            p.op("vector", lambda e: e.reduce_max(out=nmax[:, 0:1], in_=ps_s[:, 0:64],
                                                  axis=AX.X, negate=True), inc=ve)
            p.wt("vector", ve)
            p.op("vector", lambda e: e.tensor_scalar(out=nmax[:, 0:1], in0=nmax[:, 0:1],
                                                     scalar1=0.125, scalar2=None, op0=ALU.mult), inc=ve)
            nm2 = ve.v
            p.wt("scalar", ve, nm2)
            ex2 = p.op("scalar", lambda e: e.activation(out=lp[:], in_=ps_s[:, 0:64],
                                                        func=AF.Exp, bias=nmax[:, 0:1],
                                                        scale=0.125), inc=sc)
            exp_prev = ex2
            p.wt("vector", sc, ex2)
            p.op("vector", lambda e: e.reduce_sum(out=nmax[:, 1:2], in_=lp[:], axis=AX.X), inc=ve)
            p.wt("vector", ve)
            p.op("vector", lambda e: e.reciprocal(out=nmax[:, 1:2], in_=nmax[:, 1:2]), inc=ve)
            p.wt("vector", ve)
            p.op("vector", lambda e: e.tensor_scalar(out=lp[:], in0=lp[:], scalar1=nmax[:, 1:2],
                                                     scalar2=None, op0=ALU.mult), inc=ve)
            lpn = ve.v
            pf_free = lpn
            p.wt("tensor", ve, lpn)
            p.wt("tensor", ve, pt_free)
            tr2 = p.op("tensor", lambda e: e.transpose(ps_t[0:64, 0:128], lp[:], ident[:]), inc=te)
            psb_te = tr2
            p.wt("vector", te, tr2)
            p.op("vector", lambda e: e.tensor_copy(out=lpT[:], in_=ps_t[0:64, 0:128]), inc=ve)
            lpt_ve = ve.v
            pt_free = lpt_ve
            p.wt("tensor", ve, lpt_ve)
            rpv = p.op("tensor", lambda e, lh=lh: e.matmul(
                ps_v[:, 64:128], lpT[:, 0:128], gvl[:, lh * 64:(lh + 1) * 64],
                start=True, stop=True), inc=te)
            p.wt("vector", te, rpv)
            p.op("vector", lambda e, lh=lh: e.tensor_copy(
                out=rem[:, lh * 64:(lh + 1) * 64], in_=ps_v[:, 64:128]), inc=ve)
        rem_ve = ve.v
        p.wt("vector", ve, rem_ve)
        p.op("vector", lambda e: e.tensor_copy(out=remT[:] if False else comb_bf[:, 0:512], in_=rem[:]), inc=ve)
        rc_ve = ve.v
        p.wt("tensor", ve, rc_ve)
        p.wt("tensor", ve, pt_free)
        for j in range(4):
            p.op("tensor", lambda e, j=j: e.transpose(
                ps_t[:, j * 128:(j + 1) * 128], comb_bf[:, j * 128:(j + 1) * 128], ident[:]),
                inc=(te if j == 3 else None))
        rT_te = te.v
        psb_te = rT_te
        p.wt("vector", te, rT_te)
        p.op("vector", lambda e: e.tensor_copy(out=remT[:], in_=ps_t[:, 0:512]), inc=ve)
        rT_ve = ve.v
        pt_free = rT_ve
        p.wt("tensor", ve, rT_ve)
        for spi in range(2):
            ps = ps_a if spi == 0 else ps_b
            p.wt("tensor", ve)
            for j in range(4):
                p.op("tensor", lambda e, j=j, spi=spi, ps=ps: e.matmul(
                    ps[:, :], remT[:, j * 128:(j + 1) * 128],
                    row_w[:, j * D + spi * 512: j * D + spi * 512 + 512],
                    start=(j == 0), stop=(j == 3)), inc=(te if j == 3 else None))
            rh_te = te.v
            p.wt("vector", te, rh_te)
            p.op("vector", lambda e, spi=spi, ps=ps: e.tensor_copy(
                out=remh[:, spi * 512:(spi + 1) * 512], in_=ps[:, :]), inc=ve)
        remh_ve = ve.v
        p.wt("vector", ve, remh_ve)
        p.op("vector", lambda e: e.tensor_tensor(
            out=comb[:].rearrange("p (h c) -> p h c", c=64),
            in0=loc[:].rearrange("p (h c) -> p h c", c=64),
            in1=gat[:, 0:16].unsqueeze(2).broadcast_to((128, 16, 64)), op=ALU.mult), inc=ve)
        p.op("vector", lambda e: e.tensor_tensor(
            out=remh[:].rearrange("p (h c) -> p h c", c=64),
            in0=remh[:].rearrange("p (h c) -> p h c", c=64),
            in1=gat[:, 16:32].unsqueeze(2).broadcast_to((128, 16, 64)), op=ALU.mult), inc=ve)
        p.wt("vector", ve)
        p.op("vector", lambda e: e.tensor_tensor(out=comb_bf[:], in0=comb[:], in1=remh[:],
                                                 op=ALU.add), inc=ve)
        comb_ve = ve.v
        p.wt("tensor", ve, comb_ve)
        p.wt("tensor", ve, pt_free)
        for j in range(8):
            p.op("tensor", lambda e, j=j: e.transpose(
                ps_t[:, j * 128:(j + 1) * 128], comb_bf[:, j * 128:(j + 1) * 128], ident[:]),
                inc=(te if j == 7 else None))
        cT_te = te.v
        psb_te = cT_te
        p.wt("vector", te, cT_te)
        p.op("vector", lambda e: e.tensor_copy(out=combT[:], in_=ps_t[:, :]), inc=ve)
        cT_ve = ve.v
        pt_free = cT_ve
        p.wt("sync", sty[0], st_done[0])
        p.op("sync", lambda e, gt=gt: e.dma_start(
            out=y_sb[:], in_=x_h[gt * 128:(gt + 1) * 128, :]), inc=ldy[0], delta=16)
        yld = ldy[0].v
        p.wt("tensor", ve, cT_ve)
        for spi in range(2):
            ps = ps_a if spi == 0 else ps_b
            p.wt("tensor", ve)
            for j in range(8):
                p.op("tensor", lambda e, j=j, spi=spi, ps=ps: e.matmul(
                    ps[:, :], combT[:, j * 128:(j + 1) * 128],
                    wo[:, j * D + spi * 512: j * D + spi * 512 + 512],
                    start=(j == 0), stop=(j == 7)), inc=(te if j == 7 else None))
            om_te = te.v
            p.wt("vector", te, om_te)
            p.wt("vector", ldy[0], yld)
            p.op("vector", lambda e, spi=spi, ps=ps: e.tensor_tensor(
                out=y_sb[:, spi * 512:(spi + 1) * 512], in0=ps[:, :],
                in1=y_sb[:, spi * 512:(spi + 1) * 512], op=ALU.add), inc=ve)
        yv = ve.v
        p.wt("scalar", ve, yv)
        p.op("scalar", lambda e, gt=gt: e.dma_start(
            out=y_h[gt * 128:(gt + 1) * 128, :], in_=y_sb[:]), inc=sty[0], delta=16)
        st_done[0] = sty[0].v
    p.wt("vector", sty[0], sty[0].v)
    return p



RL3 = 8192
NTL3 = RL3 // 128


L3A_STAGE = 0


def build_l3a():
    p = P()
    x_h = p.inp("x", [RL3, D])
    wkt_h = p.inp("wkt", [D, 512])
    wvt_h = p.inp("wvt", [D, 512])
    wrt_h = p.inp("wrt", [D, 512])
    wot_h = p.inp("wot", [512, D])
    dec_h = p.inp("decay2", [128, 4])
    tf_h = p.inp("tf2", [128, 4])
    maskT_h = p.inp("maskT", [16, 16])
    ident_h = p.inp("ident", [128, 128])
    y_h = p.out("ypart", [RL3, D])

    x_sb = p.sbuf("x_sb", [128, 2 * D], F32)
    ident = p.sbuf("ident_sb", [128, 128], BF16)
    wkt = p.sbuf("wkt_sb", [128, 8 * 512], BF16)
    wvt = p.sbuf("wvt_sb", [128, 8 * 512], BF16)
    wrt = p.sbuf("wrt_sb", [128, 8 * 512], BF16)
    wot = p.sbuf("wot_sb", [64, 8 * D], BF16)
    dec = p.sbuf("dec_sb", [128, 4], F32)
    tf = p.sbuf("tf_sb", [128, 4], F32)
    dinv = p.sbuf("dinv_sb", [128, 4], F32)
    dsq = p.sbuf("dsq_sb", [128, 4], F32)
    tpw = p.sbuf("tpw", [128, 4 * 16], F32)
    tpu = p.sbuf("tpu", [128, 4 * 16], BF16)
    tpuf = p.sbuf("tpuf", [128, 4 * 16], F32)
    tp1 = p.sbuf("tp1", [128, 4 * 16], F32)
    tpm = p.sbuf("tpm", [128, 4 * 16], F32)
    tdc8 = p.sbuf("tdc8", [64, 8], F32)
    maskT = p.sbuf("maskT_sb", [16, 16], F32)
    st6 = p.sbuf("st6", [128, 12], F32)
    mvbuf = p.sbuf("mvbuf", [128, 2 * NTL3], F32)
    nmean = p.sbuf("nmean", [128, NTL3], F32)
    rstd = p.sbuf("rstd", [128, NTL3], F32)
    ln_bf = p.sbuf("ln_bf", [128, D], BF16)
    rlnT = p.sbuf("rlnT", [128, 2 * D], BF16)
    k_tm = p.sbuf("k_tm", [128, 512], BF16)
    v_tm = p.sbuf("v_tm", [128, 512], BF16)
    r_tm = p.sbuf("r_tm", [128, 512], BF16)
    kf = p.sbuf("kf", [128, 4 * 128], BF16)
    vf = p.sbuf("vf", [128, 4 * 128], BF16)
    rf = p.sbuf("rf", [128, 4 * 128], BF16)
    wtl = p.sbuf("wtl", [64, 256], BF16)
    utl = p.sbuf("utl", [64, 256], BF16)
    rtl = p.sbuf("rtl", [64, 2 * 128], BF16)
    ktl = p.sbuf("ktl", [128, 128], BF16)
    vsub = p.sbuf("vsub", [16, 512], BF16)
    ktsub = p.sbuf("ktsub", [16, 512], BF16)
    a_sb = p.sbuf("a_sb", [16, 128], BF16)
    s_st = p.sbuf("s_st", [64, 512], F32)
    s_bf = p.sbuf("s_bf", [64, 512], BF16)
    s_tmp = p.sbuf("s_tmp", [64, 512], F32)
    y_fmt = p.sbuf("y_fmt", [64, 8 * 128], BF16)
    y_sb = p.sbuf("y_sb", [128, D], F32)

    ps_t = p.psum("ps_t", [128, 1024], BF16)
    ps_t2 = p.psum("ps_t2", [128, 512], BF16)
    ps_a = p.psum("ps_a", [128, 512], F32)
    ps_b = p.psum("ps_b", [128, 512], F32)
    psA = p.psum("psA", [16, 128], F32)
    psVT = p.psum("psVT", [16, 1024], BF16)
    psY = p.psum("psY", [64, 128], F32)
    psM = p.psum("psM", [64, 512], F32)

    ldh = p.sem("ldh")
    ldr = [p.sem("ldr0"), p.sem("ldr1")]
    ldg = p.sem("ldg")
    gp = p.sem("gp")
    sc = p.sem("sc")
    te = p.sem("te")
    ve = p.sem("ve")
    sty = [p.sem("sty0")]

    p.op("gpsimd", lambda e: e.dma_start(out=ident[:], in_=ident_h), inc=ldg, delta=16)
    for (dst, srch) in ((wkt, wkt_h), (wvt, wvt_h), (wrt, wrt_h)):
        p.op("gpsimd", lambda e, dst=dst, srch=srch: e.dma_start(
            out=dst[:].rearrange("p (k c) -> p k c", c=512),
            in_=srch.rearrange("(k p) c -> p k c", p=128)), inc=ldg, delta=16)
    p.op("gpsimd", lambda e: e.dma_start(
        out=wot[:].rearrange("p (k c) -> p k c", c=D),
        in_=wot_h.rearrange("(k p) c -> p k c", p=64)), inc=ldg, delta=16)
    wld = ldg.v
    p.op("sync", lambda e: e.dma_start(out=dec[:], in_=dec_h), inc=ldh, delta=16)
    p.op("sync", lambda e: e.dma_start(out=tf[:], in_=tf_h), inc=ldh, delta=16)
    p.op("sync", lambda e: e.dma_start(out=maskT[:], in_=maskT_h), inc=ldh, delta=16)
    tbl_ld = ldh.v
    p.op("gpsimd", lambda e: e.memset(s_st[:], 0.0), inc=gp)
    p.op("gpsimd", lambda e: e.memset(s_bf[:], 0.0), inc=gp)
    p.op("gpsimd", lambda e: e.memset(y_fmt[:], 0.0), inc=gp)
    gp_done = gp.v
    p.q["tensor"].append(lambda e: e.wait_ge(ldg.h, wld))
    p.q["tensor"].append(lambda e: e.wait_ge(ldh.h, tbl_ld))
    p.q["vector"].append(lambda e: e.wait_ge(ldh.h, tbl_ld))
    p.q["vector"].append(lambda e: e.wait_ge(gp.h, gp_done))
    p.q["scalar"].append(lambda e: e.wait_ge(ldh.h, tbl_ld))

    p.op("vector", lambda e: e.reciprocal(out=dinv[:], in_=dec[:]), inc=ve)
    p.op("scalar", lambda e: e.sqrt(out=dsq[:], in_=dec[:]), inc=sc)
    p.wt("vector", sc)
    p.wt("vector", ve)
    p.op("vector", lambda e: e.tensor_copy(out=tpw[:, 15:64:16], in_=dsq[:]), inc=ve)
    for it in range(7):
        p.wt("vector", ve)
        p.op("vector", lambda e: e.tensor_tensor(out=tpw[:, 15:64:16], in0=tpw[:, 15:64:16],
                                                 in1=dec[:], op=ALU.mult), inc=ve)
    for j in range(14, -1, -1):
        p.wt("vector", ve)
        p.op("vector", lambda e, j=j: e.tensor_tensor(
            out=tpw[:, j:64:16], in0=tpw[:, j + 1:64:16], in1=dinv[:], op=ALU.mult), inc=ve)
    p.op("vector", lambda e: e.tensor_copy(out=tp1[:, 0:64:16], in_=dec[:]), inc=ve)
    for j in range(1, 16):
        p.wt("vector", ve)
        p.op("vector", lambda e, j=j: e.tensor_tensor(
            out=tp1[:, j:64:16], in0=tp1[:, j - 1:64:16], in1=dec[:], op=ALU.mult), inc=ve)
    p.op("vector", lambda e: e.tensor_copy(out=tpm[:, 15:64:16], in_=tf[:]), inc=ve)
    for j in range(14, -1, -1):
        p.wt("vector", ve)
        p.op("vector", lambda e, j=j: e.tensor_tensor(
            out=tpm[:, j:64:16], in0=tpm[:, j + 1:64:16], in1=dec[:], op=ALU.mult), inc=ve)
    p.wt("vector", ve)
    p.op("vector", lambda e: e.tensor_tensor(
        out=tpuf[:].rearrange("p (pp j) -> p pp j", j=16),
        in0=tpm[:].rearrange("p (pp j) -> p pp j", j=16),
        in1=tpw[:, 0:64:16].unsqueeze(2).broadcast_to((128, 4, 16)), op=ALU.mult), inc=ve)
    p.wt("vector", ve)
    p.op("vector", lambda e: e.tensor_copy(out=tpu[:], in_=tpuf[:]), inc=ve)
    p.wt("vector", ve)
    p.op("vector", lambda e: e.tensor_tensor(out=dsq[:, 0:4], in0=tp1[:, 15:64:16],
                                             in1=dec[:], op=ALU.mult), inc=ve)
    p.wt("vector", ve)
    for h in range(8):
        pp, rb = h // 2, (h % 2) * 64
        p.op("vector", lambda e, h=h, pp=pp, rb=rb: e.tensor_copy(
            out=tdc8[:, h:h + 1], in_=dsq[rb:rb + 64, pp:pp + 1]),
            inc=(ve if h == 7 else None))
    tab_ve = ve.v

    stats_done = [0] * NTL3
    for ti in range(NTL3):
        sl = ti % 2
        if ti >= 2:
            p.wt("sync", ve, stats_done[ti - 2])
        p.op("sync", lambda e, ti=ti, sl=sl: e.dma_start(
            out=x_sb[:, sl * D:(sl + 1) * D], in_=x_h[ti * 128:(ti + 1) * 128, :]),
            inc=ldr[sl], delta=16)
        xldv = ldr[sl].v
        p.wt("vector", ldr[sl], xldv)
        p.op("vector", lambda e, sl=sl: e.bn_stats(out=st6[:, 0:6], in_=x_sb[:, sl * D:sl * D + 512]), inc=ve)
        p.wt("vector", ve)
        p.op("vector", lambda e, sl=sl: e.bn_stats(out=st6[:, 6:12], in_=x_sb[:, sl * D + 512:sl * D + 1024]), inc=ve)
        p.wt("vector", ve)
        p.op("vector", lambda e, ti=ti: e.bn_aggr(out=mvbuf[:, 2 * ti:2 * ti + 2], in_=st6[:, 0:12]), inc=ve)
        p.wt("vector", ve)
        stats_done[ti] = ve.v
    p.op("vector", lambda e: e.tensor_scalar(out=rstd[:, 0:NTL3], in0=mvbuf[:, 1:2 * NTL3:2],
                                             scalar1=float(EPS), scalar2=None, op0=ALU.add), inc=ve)
    p.wt("scalar", ve)
    p.op("scalar", lambda e: e.sqrt(out=rstd[:, 0:NTL3], in_=rstd[:, 0:NTL3]), inc=sc)
    p.wt("vector", sc)
    p.op("vector", lambda e: e.reciprocal(out=rstd[:, 0:NTL3], in_=rstd[:, 0:NTL3]), inc=ve)
    p.op("vector", lambda e: e.tensor_scalar(out=nmean[:, 0:NTL3], in0=mvbuf[:, 0:2 * NTL3:2],
                                             scalar1=-1.0, scalar2=None, op0=ALU.mult), inc=ve)
    p.wt("vector", ve)
    stats_all = ve.v

    norm_prev = stats_all
    tr_prev = 0
    lnt_prev = [0, 0]
    proj_prev = [0, 0]
    sig_prev = sc.v
    ktmf = [0, 0, 0]
    fm_free = 0
    st_done = [0]
    y_done = 0
    yfmt_free = te.v
    subs_free = te.v
    for ti in range(NTL3):
        sl = ti % 2
        par = ti % 2
        p.wt("sync", ve, norm_prev)
        p.op("sync", lambda e, ti=ti, sl=sl: e.dma_start(
            out=x_sb[:, sl * D:(sl + 1) * D], in_=x_h[ti * 128:(ti + 1) * 128, :]),
            inc=ldr[sl], delta=16)
        xldv = ldr[sl].v
        p.wt("vector", ldr[sl], xldv)
        p.wt("vector", te, tr_prev)
        emit_norm_tile(p, x_sb[:, sl * D:(sl + 1) * D], ln_bf[:],
                       nmean[:, ti:ti + 1], rstd[:, ti:ti + 1], ve)
        norm_prev = ve.v
        pst = ps_t if par == 0 else ps_t2
        p.wt("tensor", ve, norm_prev)
        p.wt("tensor", ve, lnt_prev[par])
        for j in range(8):
            p.op("tensor", lambda e, j=j: e.transpose(
                ps_t[:, j * 128:(j + 1) * 128], ln_bf[:, j * 128:(j + 1) * 128], ident[:]),
                inc=(te if j == 7 else None))
        tr_prev = te.v
        p.wt("vector", te, tr_prev)
        p.wt("vector", te, proj_prev[par])
        p.op("vector", lambda e, par=par: e.tensor_copy(
            out=rlnT[:, par * D:(par + 1) * D], in_=ps_t[:, :]), inc=ve)
        lnt_prev[par] = ve.v
        p.wt("tensor", ve, lnt_prev[par])
        for (w_sb, dst, use_sig, psx) in ((wkt, k_tm, False, ps_a), (wvt, v_tm, False, ps_b),
                                          (wrt, r_tm, True, ps_a)):
            p.wt("tensor", ve)
            p.wt("tensor", sc, sig_prev)
            for k in range(8):
                p.op("tensor", lambda e, k=k, w_sb=w_sb, par=par, psx=psx: e.matmul(
                    psx[:, :], rlnT[:, par * D + k * 128:par * D + (k + 1) * 128],
                    w_sb[:, k * 512:(k + 1) * 512], start=(k == 0), stop=(k == 7)),
                    inc=(te if k == 7 else None))
            mmv = te.v
            if use_sig:
                p.wt("scalar", te, mmv)
                p.wt("scalar", ve)
                sig_prev = p.op("scalar", lambda e, dst=dst, psx=psx: e.activation(
                    out=dst[:], in_=psx[:, :], func=AF.Sigmoid), inc=sc)
            else:
                p.wt("vector", te, mmv)
                p.op("vector", lambda e, dst=dst, psx=psx: e.tensor_copy(out=dst[:], in_=psx[:, :]), inc=ve)
        kv_ve = ve.v
        p.wt("tensor", ve, kv_ve)
        p.wt("tensor", sc, sig_prev)
        p.wt("tensor", ve, fm_free)
        for (src, dstf) in ((k_tm, kf), (v_tm, vf), (r_tm, rf)):
            for j in range(4):
                p.op("tensor", lambda e, j=j, src=src: e.transpose(
                    ps_t2[:, j * 128:(j + 1) * 128], src[:, j * 128:(j + 1) * 128], ident[:]),
                    inc=(te if j == 3 else None))
            trv = te.v
            p.wt("vector", te, trv)
            p.op("vector", lambda e, dstf=dstf: e.tensor_copy(out=dstf[:], in_=ps_t2[:, :]), inc=ve)
            p.wt("tensor", ve)
        fm_ve = ve.v
        y_fmt_ve = 0
        for pp in range(4):
            p.wt("vector", ve, fm_ve)
            p.wt("vector", te)
            p.op("vector", lambda e, pp=pp: e.tensor_tensor(
                out=ktl[:].rearrange("p (cc j) -> p cc j", j=16),
                in0=kf[:, pp * 128:(pp + 1) * 128].rearrange("p (cc j) -> p cc j", j=16),
                in1=tpm[:, pp * 16:(pp + 1) * 16].unsqueeze(1).broadcast_to((128, 8, 16)),
                op=ALU.mult), inc=ve)
            for hh in range(2):
                rb = hh * 64
                p.op("vector", lambda e, hh=hh, rb=rb, pp=pp: e.tensor_tensor(
                    out=wtl[:, hh * 128:(hh + 1) * 128].rearrange("p (cc j) -> p cc j", j=16),
                    in0=rf[rb:rb + 64, pp * 128:(pp + 1) * 128].rearrange("p (cc j) -> p cc j", j=16),
                    in1=tpw[rb:rb + 64, pp * 16:(pp + 1) * 16].unsqueeze(1).broadcast_to((64, 8, 16)),
                    op=ALU.mult), inc=ve)
                p.op("vector", lambda e, hh=hh, rb=rb, pp=pp: e.tensor_tensor(
                    out=utl[:, hh * 128:(hh + 1) * 128].rearrange("p (cc j) -> p cc j", j=16),
                    in0=kf[rb:rb + 64, pp * 128:(pp + 1) * 128].rearrange("p (cc j) -> p cc j", j=16),
                    in1=tpuf[rb:rb + 64, pp * 16:(pp + 1) * 16].unsqueeze(1).broadcast_to((64, 8, 16)),
                    op=ALU.mult), inc=ve)
            for hh in range(2):
                rb = hh * 64
                p.op("vector", lambda e, hh=hh, rb=rb, pp=pp: e.tensor_tensor(
                    out=rtl[:, hh * 128:(hh + 1) * 128].rearrange("p (cc j) -> p cc j", j=16),
                    in0=rf[rb:rb + 64, pp * 128:(pp + 1) * 128].rearrange("p (cc j) -> p cc j", j=16),
                    in1=tp1[rb:rb + 64, pp * 16:(pp + 1) * 16].unsqueeze(1).broadcast_to((64, 8, 16)),
                    op=ALU.mult), inc=ve)
            tl_ve = ve.v
            for cc in range(8 if L3A_STAGE < 3 else 0):
                p.wt("tensor", ve, tl_ve)
                p.wt("tensor", ve, subs_free)
                p.op("tensor", lambda e, pp=pp, cc=cc: e.transpose(
                    psVT[0:16, pp * 128 if False else 0:128],
                    vf[:, pp * 128 + cc * 16: pp * 128 + cc * 16 + 16], ident[:]),
                    inc=None) if False else None
                p.op("tensor", lambda e, pp=pp, cc=cc: e.transpose(
                    psVT[0:16, 0:128], vf[:, pp * 128 + cc * 16: pp * 128 + (cc) * 16 + 16],
                    ident[:]), inc=None)
                p.op("tensor", lambda e, pp=pp, cc=cc: e.transpose(
                    psVT[0:16, 128:256], ktl[:, cc * 16:(cc + 1) * 16], ident[:]), inc=te)
                vs_te = te.v
                p.wt("vector", te, vs_te)
                p.op("vector", lambda e, pp=pp: e.tensor_copy(
                    out=vsub[:, pp * 128:(pp + 1) * 128], in_=psVT[0:16, 0:128]), inc=ve)
                p.op("vector", lambda e, pp=pp: e.tensor_copy(
                    out=ktsub[:, pp * 128:(pp + 1) * 128], in_=psVT[0:16, 128:256]), inc=ve)
                vs_ve = ve.v
                subs_free = vs_ve
                if L3A_STAGE >= 2:
                    continue
                p.wt("tensor", ve, vs_ve)
                for hh in range(2):
                    p.op("tensor", lambda e, hh=hh, cc=cc: e.matmul(
                        psA[0:16, hh * 16:(hh + 1) * 16],
                        utl[:, hh * 128 + cc * 16: hh * 128 + (cc + 1) * 16],
                        wtl[:, hh * 128 + cc * 16: hh * 128 + (cc + 1) * 16],
                        start=True, stop=True), inc=(te if hh == 1 else None))
                at_te = te.v
                p.wt("vector", te, at_te)
                if L3A_STAGE >= 1.5:
                    continue
                p.op("vector", lambda e: e.tensor_tensor(
                    out=a_sb[:, 0:32].rearrange("p (hh j) -> p hh j", j=16),
                    in0=psA[0:16, 0:32].rearrange("p (hh j) -> p hh j", j=16),
                    in1=maskT[:].unsqueeze(1).broadcast_to((16, 2, 16)), op=ALU.mult), inc=ve)
                a_ve = ve.v
                if L3A_STAGE >= 1:
                    continue
                p.wt("tensor", ve, a_ve)
                for hh in range(2):
                    h = pp * 2 + hh
                    rb = hh * 64
                    p.op("tensor", lambda e, h=h, hh=hh, cc=cc: e.matmul(
                        psY[0:64, hh * 16:(hh + 1) * 16],
                        s_bf[:, h * 64:(h + 1) * 64],
                        rtl[:, hh * 128 + cc * 16: hh * 128 + (cc + 1) * 16],
                        start=True, stop=False))
                    p.op("tensor", lambda e, h=h, hh=hh, rb=rb, pp=pp: e.matmul(
                        psY[0:64, hh * 16:(hh + 1) * 16],
                        vsub[:, pp * 128 + rb: pp * 128 + rb + 64],
                        a_sb[:, hh * 16:(hh + 1) * 16],
                        start=False, stop=True), inc=(te if hh == 1 else None))
                y_te = te.v
                for hh in range(2):
                    rb = hh * 64
                    p.op("tensor", lambda e, hh=hh, rb=rb, pp=pp: e.matmul(
                        psM[0:64, (pp * 2 + hh) * 64:(pp * 2 + hh + 1) * 64],
                        ktsub[:, pp * 128 + rb: pp * 128 + rb + 64],
                        vsub[:, pp * 128 + rb: pp * 128 + rb + 64],
                        start=True, stop=True), inc=(te if hh == 1 else None))
                m_te = te.v
                p.wt("vector", te, y_te)
                p.op("vector", lambda e, pp=pp, cc=cc: e.tensor_copy(
                    out=y_fmt[:].rearrange("p (h c) -> p h c", c=128)[:, 2 * pp:2 * pp + 2, cc * 16:(cc + 1) * 16],
                    in_=psY[0:64, 0:32].rearrange("p (h c) -> p h c", c=16)), inc=ve)
                p.wt("vector", te, m_te)
                p.op("vector", lambda e, pp=pp: e.tensor_tensor(
                    out=s_tmp[:, pp * 128:(pp + 1) * 128].rearrange("p (h c) -> p h c", c=64),
                    in0=s_st[:, pp * 128:(pp + 1) * 128].rearrange("p (h c) -> p h c", c=64),
                    in1=tdc8[:, 2 * pp:2 * pp + 2].unsqueeze(2).broadcast_to((64, 2, 64)),
                    op=ALU.mult), inc=ve)
                p.wt("vector", ve)
                p.op("vector", lambda e, pp=pp: e.tensor_tensor(
                    out=s_st[:, pp * 128:(pp + 1) * 128], in0=s_tmp[:, pp * 128:(pp + 1) * 128],
                    in1=psM[0:64, pp * 128:(pp + 1) * 128], op=ALU.add), inc=ve)
                p.wt("vector", ve)
                p.op("vector", lambda e, pp=pp: e.tensor_copy(
                    out=s_bf[:, pp * 128:(pp + 1) * 128],
                    in_=s_st[:, pp * 128:(pp + 1) * 128]), inc=ve)
        yf_ve = ve.v
        fm_free = yf_ve
        p.wt("tensor", ve, yf_ve)
        for spi in range(2):
            ps = ps_a if spi == 0 else ps_b
            p.wt("tensor", ve)
            for h in range(8):
                pp, rb = h // 2, (h % 2) * 64
                p.op("tensor", lambda e, h=h, spi=spi, ps=ps: e.matmul(
                    ps[:, :], y_fmt[:, h * 128:(h + 1) * 128],
                    wot[:, h * D + spi * 512: h * D + spi * 512 + 512],
                    start=(h == 0), stop=(h == 7)), inc=(te if h == 7 else None))
            om_te = te.v
            p.wt("vector", te, om_te)
            if spi == 0:
                p.wt("vector", sty[0], st_done[0])
            p.op("vector", lambda e, spi=spi, ps=ps: e.tensor_copy(
                out=y_sb[:, spi * 512:(spi + 1) * 512], in_=ps[:, :]), inc=ve)
        yv = ve.v
        p.wt("scalar", ve, yv)
        p.op("scalar", lambda e, ti=ti: e.dma_start(
            out=y_h[ti * 128:(ti + 1) * 128, :], in_=y_sb[:]), inc=sty[0], delta=16)
        st_done[0] = sty[0].v
    p.wt("vector", sty[0], sty[0].v)
    return p


def _run(p, in_maps):
    nc = p.finalize()
    return run_bass_kernel_spmd(nc, in_maps, core_ids=list(range(NCORES)))


_IDENT = np.eye(128, dtype=np.float32)


def build_add3():
    p = P()
    x_h = p.inp("x", [R, D])
    p0_h = p.inp("p0", [R, D])
    p1_h = p.inp("p1", [R, D])
    y_h = p.out("y", [R, D])
    x_sb = p.sbuf("x_sb", [128, 2 * D], F32)
    p0_sb = p.sbuf("p0_sb", [128, 2 * D], F32)
    p1_sb = p.sbuf("p1_sb", [128, 2 * D], F32)
    y_sb = p.sbuf("y_sb", [128, 2 * D], F32)
    ld = [p.sem("ld0"), p.sem("ld1")]
    ve = p.sem("ve")
    sty = [p.sem("sty0"), p.sem("sty1")]
    st_done = [0, 0]
    for t in range(NT):
        par = t % 2
        if t >= 2:
            p.wt("sync", ve, (t - 1) * 2)
        for (buf, srch) in ((x_sb, x_h), (p0_sb, p0_h), (p1_sb, p1_h)):
            p.op("sync", lambda e, buf=buf, srch=srch, t=t, par=par: e.dma_start(
                out=buf[:, par * D:(par + 1) * D], in_=srch[t * 128:(t + 1) * 128, :]),
                inc=ld[par], delta=16)
        ldv = ld[par].v
        p.wt("vector", ld[par], ldv)
        p.op("vector", lambda e, par=par: e.tensor_tensor(
            out=y_sb[:, par * D:(par + 1) * D], in0=x_sb[:, par * D:(par + 1) * D],
            in1=p0_sb[:, par * D:(par + 1) * D], op=ALU.add), inc=ve)
        p.wt("vector", ve)
        if t >= 2:
            p.wt("vector", sty[par], st_done[par])
        p.op("vector", lambda e, par=par: e.tensor_tensor(
            out=y_sb[:, par * D:(par + 1) * D], in0=y_sb[:, par * D:(par + 1) * D],
            in1=p1_sb[:, par * D:(par + 1) * D], op=ALU.add), inc=ve)
        p.wt("scalar", ve, ve.v)
        p.op("scalar", lambda e, t=t, par=par: e.dma_start(
            out=y_h[t * 128:(t + 1) * 128, :], in_=y_sb[:, par * D:(par + 1) * D]),
            inc=sty[par], delta=16)
        st_done[par] = sty[par].v
    p.wt("vector", sty[0], sty[0].v)
    p.wt("vector", sty[1], sty[1].v)
    return p


_NC_CACHE = {}


def _get_nc(name, builder):
    if name not in _NC_CACHE:
        _NC_CACHE[name] = builder().finalize()
    return _NC_CACHE[name]


def _launch(name, builder, in_maps):
    nc = _get_nc(name, builder)
    res = run_bass_kernel_spmd(nc, in_maps, core_ids=list(range(NCORES)))
    return res.results


def _rope_tables(pos):
    div = np.exp(np.arange(0, 64, 2, dtype=np.float64) * (-np.log(10000.0) / 64.0))
    ang = pos[:, None].astype(np.float64) * div[None, :]
    return np.cos(ang).astype(np.float32), np.sin(ang).astype(np.float32)


def _permute_qk_cols(wmat, nheads):
    perm = np.concatenate([np.arange(0, 64, 2), np.arange(1, 64, 2)])
    idx = np.concatenate([h * 64 + perm for h in range(nheads)])
    return np.ascontiguousarray(wmat[:, idx])


def kernel(**inputs):
    g = {k: np.ascontiguousarray(np.asarray(v, np.float32)) for k, v in inputs.items()}
    x = g["L0_attn_x"].reshape(4, 8192, D)

    def slc(c):
        b, half = c // 2, c % 2
        return b, half

    def xparts(xf):
        return [xf[c // 2, (c % 2) * R:((c % 2) + 1) * R] for c in range(NCORES)]

    def gather(results, key="y"):
        out = np.empty((4, 8192, D), np.float32)
        for c in range(NCORES):
            b, half = slc(c)
            out[b, half * R:(half + 1) * R] = np.asarray(results[c][key])
        return out

    ident = _IDENT

    def run_l0f(xc):
        ims = []
        for c in range(NCORES):
            b, half = slc(c)
            ims.append({"x": xc[b, half * R:(half + 1) * R], "gate": g["L0_ffn_gate"],
                        "w1": g["L0_ffn_experts_w1"], "w2": g["L0_ffn_experts_w2"],
                        "w3": g["L0_ffn_experts_w3"], "ident": ident})
        return gather(_launch("l0f", build_l0f, ims))

    def run_l0a(xc):
        wq1 = _permute_qk_cols(g["L0_attn_Wq1"], 16)
        wq2 = _permute_qk_cols(g["L0_attn_Wq2"], 16)
        wk1 = _permute_qk_cols(g["L0_attn_Wk1"], 2)
        wk2 = _permute_qk_cols(g["L0_attn_Wk2"], 2)
        lvec = np.stack([g["L0_attn_lq1"], g["L0_attn_lk1"],
                         g["L0_attn_lq2"], g["L0_attn_lk2"]], axis=1)
        ims = []
        for c in range(NCORES):
            b, half = slc(c)
            r0 = 0 if half == 0 else 2048
            cos, sin = _rope_tables(np.arange(r0, r0 + RGN))
            ims.append({"xh": xc[b, r0:r0 + RGN], "x": xc[b, half * R:(half + 1) * R],
                        "wq1": wq1, "wq2": wq2, "wk1": wk1, "wk2": wk2,
                        "wv": g["L0_attn_Wv"], "wout": g["L0_attn_Wout"],
                        "bout": g["L0_attn_bout"].reshape(1, D), "lvec": lvec,
                        "cosr": cos, "sinr": sin, "ident": ident})
        return gather(_launch("l0a", build_l0a, ims))

    def run_l1f(xc):
        ims = []
        for c in range(NCORES):
            b, half = slc(c)
            ims.append({"x": xc[b, half * R:(half + 1) * R], "w": g["L1_ffn_weight"],
                        "ident": ident})
        return gather(_launch("l1f", build_l1f, ims))

    def run_l1a(xc):
        i = np.arange(128)
        tri = np.where(i[None, :] > i[:, None], -1e30, 0.0).astype(np.float32)
        ims = []
        for c in range(NCORES):
            b, half = slc(c)
            pos0 = half * R
            cos, sin = _rope_tables(np.arange(pos0 - 1024, pos0 + R))
            ims.append({"x": xc[b, half * R:(half + 1) * R],
                        "xh": xc[b, 3072:4096],
                        "wq": g["L1_attn_Wq"], "wkv": g["L1_attn_Wkv"],
                        "wo": g["L1_attn_Wo"], "bo": g["L1_attn_bo"].reshape(1, D),
                        "wof": g["L1_attn_Wo0"] if half == 0 else g["L1_attn_Wo"],
                        "bof": (g["L1_attn_bo0"] if half == 0 else g["L1_attn_bo"]).reshape(1, D),
                        "cosr": cos, "sinr": sin,
                        "msk0": np.full((1, 1), -1e30 if half == 0 else 0.0, np.float32),
                        "trimask": tri, "ident": ident})
        return gather(_launch("l1a", build_l1a, ims))

    def run_l2f(xc):
        ims = []
        for c in range(NCORES):
            b, half = slc(c)
            ims.append({"x": xc[b, half * R:(half + 1) * R], "W1": g["L2_ffn_W1"],
                        "b1": g["L2_ffn_b1"].reshape(1, 4096), "W2": g["L2_ffn_W2"],
                        "b2": g["L2_ffn_b2"].reshape(1, D),
                        "scale": g["L2_ffn_scale"].reshape(1, 1),
                        "lt_raw": xc[b, 1:17], "ident": ident})
        return gather(_launch("l2f", build_l2f, ims))

    def run_l2a(xc):
        glw = g["L2_attn_gate_local_w"]
        grw = g["L2_attn_gate_remote_w"]
        glwb = np.zeros((1024, 16), np.float32)
        grwb = np.zeros((1024, 16), np.float32)
        for h in range(16):
            glwb[h * 64:(h + 1) * 64, h] = glw[h]
            grwb[h * 64:(h + 1) * 64, h] = grw[h]
        i = np.arange(128)
        tri_a = np.where(i[None, :] > i[:, None], 0.0, -1e30).astype(np.float32)
        tri_b = np.where(i[None, :] <= i[:, None], 0.0, -1e30).astype(np.float32)
        ims = []
        for c in range(NCORES):
            b, half = slc(c)
            lmask = np.full((32, 64), -1e30, np.float32)
            for gt in range(32):
                lmask[gt, :half * 32 + gt + 1] = 0.0
            ims.append({"x": xc[b, half * R:(half + 1) * R], "xkv": xc[b, 3840:4096],
                        "xfirst": xc[b, 0:4096], "wq": g["L2_attn_wq"],
                        "wk": g["L2_attn_wk"], "wv": g["L2_attn_wv"],
                        "wgq": g["L2_attn_wgq"], "wgk": g["L2_attn_wgk"],
                        "wgv": g["L2_attn_wgv"], "r_out_w": g["L2_attn_r_out_w"],
                        "w_out": g["L2_attn_w_out"], "glwb": glwb, "grwb": grwb,
                        "tri_a": tri_a, "tri_b": tri_b,
                        "mfirst": np.full((1, 1), -1e30 if half == 0 else 0.0, np.float32),
                        "lmask": lmask.reshape(1, -1), "ident": ident})
        return gather(_launch("l2a", build_l2a, ims))

    def run_l3f(xc):
        zrow = np.zeros((1, D), np.float32)
        ims = []
        for c in range(NCORES):
            b, half = slc(c)
            ims.append({"x": xc[b, half * R:(half + 1) * R],
                        "wr": g["L3_ffn_W_r"], "wk": g["L3_ffn_W_k"], "wv": g["L3_ffn_W_v"],
                        "mur": g["L3_ffn_mu_r"].reshape(1, D),
                        "muk": g["L3_ffn_mu_k"].reshape(1, D),
                        "xp": (zrow if half == 0 else xc[b, 4095:4096]),
                        "ident": ident})
        return gather(_launch("l3f", build_l3f, ims))

    def run_l3a(xc):
        td = g["L3_attn_time_decay"]
        tfi = g["L3_attn_time_first"]
        ims = []
        for c in range(NCORES):
            b, hg = c // 2, c % 2
            hs = slice(hg * 512, (hg + 1) * 512)
            dec2 = np.zeros((128, 4), np.float32)
            tf2 = np.zeros((128, 4), np.float32)
            for h in range(8):
                pp, rb = h // 2, (h % 2) * 64
                dec2[rb:rb + 64, pp] = td[hg * 8 + h]
                tf2[rb:rb + 64, pp] = tfi[hg * 8 + h]
            i16 = np.arange(16)
            maskT = (i16[:, None] <= i16[None, :]).astype(np.float32)
            ims.append({"x": xc[b], "wkt": np.ascontiguousarray(g["L3_attn_Wk"][hs, :].T),
                        "wvt": np.ascontiguousarray(g["L3_attn_Wv"][hs, :].T),
                        "wrt": np.ascontiguousarray(g["L3_attn_Wr"][hs, :].T),
                        "wot": np.ascontiguousarray(g["L3_attn_Wo"].T[hs, :]),
                        "decay2": dec2, "tf2": tf2, "maskT": maskT, "ident": ident})
        res = _launch("l3a", build_l3a, ims)
        out = np.empty((4, 8192, D), np.float32)
        ims2 = []
        for c in range(NCORES):
            b, half = slc(c)
            p0 = np.asarray(res[2 * b]["ypart"])[half * R:(half + 1) * R]
            p1 = np.asarray(res[2 * b + 1]["ypart"])[half * R:(half + 1) * R]
            ims2.append({"x": xc[b, half * R:(half + 1) * R], "p0": p0, "p1": p1})
        return gather(_launch("add3", build_add3, ims2))

    x = run_l0f(x)
    x = run_l0a(x)
    x = run_l0f(x)
    x = run_l1f(x)
    x = run_l1a(x)
    x = run_l1f(x)
    x = run_l2f(x)
    x = run_l2a(x)
    x = run_l2f(x)
    x = run_l3f(x)
    x = run_l3a(x)
    x = run_l3f(x)
    return x



# version 04 (final)
# speedup vs baseline: 1.1091x; 1.1091x over previous
import contextlib
import numpy as np
import concourse.bass as bass
import concourse.mybir as mybir
from concourse.bass_utils import run_bass_kernel_spmd

F32 = mybir.dt.float32
BF16 = mybir.dt.bfloat16
AF = mybir.ActivationFunctionType
ALU = mybir.AluOpType
AX = mybir.AxisListType

NCORES = 8
R = 4096
D = 1024
NT = R // 128
EPS = 1e-5

ENGINES = ("sync", "scalar", "tensor", "vector", "gpsimd")


class Ctr:
    def __init__(self, h):
        self.h = h
        self.v = 0


class P:

    def __init__(self):
        self.nc = bass.Bass("TRN2", target_bir_lowering=False)
        self.es = contextlib.ExitStack()
        self.q = {e: [] for e in ENGINES}

    def inp(self, name, shape, dt=F32):
        return self.nc.dram_tensor(name, list(shape), dt, kind="ExternalInput").ap()

    def out(self, name, shape, dt=F32):
        return self.nc.dram_tensor(name, list(shape), dt, kind="ExternalOutput").ap()

    def dram(self, name, shape, dt=F32):
        return self.nc.dram_tensor(name, list(shape), dt).ap()

    def sbuf(self, name, shape, dt):
        return self.es.enter_context(self.nc.sbuf_tensor(name, list(shape), dt))

    def psum(self, name, shape, dt=F32):
        return self.es.enter_context(self.nc.psum_tensor(name, list(shape), dt))

    def sem(self, name):
        return Ctr(self.es.enter_context(self.nc.semaphore(name)))

    def op(self, eng, fn, inc=None, delta=1):
        if inc is None:
            self.q[eng].append(fn)
            return None
        inc.v += delta
        self.q[eng].append(lambda e, fn=fn, c=inc, d=delta: fn(e).then_inc(c.h, d))
        return inc.v

    def wt(self, eng, ctr, val=None):
        v = ctr.v if val is None else val
        self.q[eng].append(lambda e, c=ctr, v=v: e.wait_ge(c.h, v))

    def finalize(self):
        blk = self.es.enter_context(self.nc.Block())
        for name in ENGINES:
            ops = self.q[name]
            if not ops:
                continue

            def body(eng, ops=ops):
                for f in ops:
                    f(eng)

            getattr(blk, name)(body)
        return self.nc


def emit_ln(p, x_sb, nmean, rstd, st6, mvbuf, ve, sc, nt=NT, x_off=0, dv=D):
    for t in range(nt):
        xt = x_sb[:, x_off + t * dv: x_off + t * dv + D]
        p.op("vector", lambda e, xt=xt: e.bn_stats(out=st6[:, 0:6], in_=xt[:, 0:512]), inc=ve)
        p.wt("vector", ve)
        p.op("vector", lambda e, xt=xt: e.bn_stats(out=st6[:, 6:12], in_=xt[:, 512:1024]), inc=ve)
        p.wt("vector", ve)
        p.op("vector", lambda e, t=t: e.bn_aggr(out=mvbuf[:, 2 * t:2 * t + 2], in_=st6[:, 0:12]), inc=ve)
        p.wt("vector", ve)
    p.op("vector", lambda e: e.tensor_scalar(out=rstd[:, 0:nt], in0=mvbuf[:, 1:2 * nt:2],
                                             scalar1=float(EPS), scalar2=None, op0=ALU.add), inc=ve)
    p.wt("scalar", ve)
    p.op("scalar", lambda e: e.sqrt(out=rstd[:, 0:nt], in_=rstd[:, 0:nt]), inc=sc)
    p.wt("vector", sc)
    p.op("vector", lambda e: e.reciprocal(out=rstd[:, 0:nt], in_=rstd[:, 0:nt]), inc=ve)
    p.op("vector", lambda e: e.tensor_scalar(out=nmean[:, 0:nt], in0=mvbuf[:, 0:2 * nt:2],
                                             scalar1=-1.0, scalar2=None, op0=ALU.mult), inc=ve)
    p.wt("vector", ve)


def emit_norm_tile(p, x_ap, out_bf, nmean_col, rstd_col, ve):
    p.op("vector", lambda e: e.tensor_scalar(out=out_bf, in0=x_ap, scalar1=nmean_col,
                                             scalar2=rstd_col, op0=ALU.add, op1=ALU.mult), inc=ve)


def emit_transpose_tile(p, src_bf, ps_t, dst, te, ve, n128=8, dst_blocks=None, ident=None):
    for j in range(n128):
        p.op("tensor", lambda e, j=j: e.transpose(ps_t[:, j * 128:(j + 1) * 128],
                                                  src_bf[:, j * 128:(j + 1) * 128], ident[:]),
             inc=(te if j == n128 - 1 else None))
    p.wt("vector", te)
    if dst_blocks is None:
        p.op("vector", lambda e: e.tensor_copy(out=dst, in_=ps_t[:, 0:n128 * 128]), inc=ve)
    else:
        for j, db in enumerate(dst_blocks):
            p.op("vector", lambda e, j=j, db=db: e.tensor_copy(out=db, in_=ps_t[:, j * 128:(j + 1) * 128]),
                 inc=(ve if j == n128 - 1 else None))
    p.wt("tensor", ve)


def build_l1f():
    p = P()
    x_h = p.inp("x", [R, D])
    w_h = p.inp("w", [D, D])
    ident_h = p.inp("ident", [128, 128])
    y_h = p.out("y", [R, D])

    x_sb = p.sbuf("x_sb", [128, NT * D], F32)
    ident = p.sbuf("ident_sb", [128, 128], BF16)
    wf32 = p.sbuf("wf32", [128, 2 * D], F32)
    wsg = p.sbuf("wsg", [128, 2 * D], BF16)
    wsgT = p.sbuf("wsgT", [128, 8 * D], BF16)
    st6 = p.sbuf("st6", [128, 12], F32)
    mvbuf = p.sbuf("mvbuf", [128, 2 * NT], F32)
    nmean = p.sbuf("nmean", [128, NT], F32)
    rstd = p.sbuf("rstd", [128, NT], F32)
    ln_bf = p.sbuf("ln_bf", [128, 2 * D], BF16)
    lnt = p.sbuf("lnt", [128, 2 * D], BF16)
    y_sb = p.sbuf("y_sb", [128, 2 * D], F32)

    ps_t = p.psum("ps_t", [128, 1024], BF16)
    ps_t2 = p.psum("ps_t2", [128, 1024], BF16)
    ps_a = p.psum("ps_a", [128, 512], F32)
    ps_b = p.psum("ps_b", [128, 512], F32)

    ldh = p.sem("ldh")
    ldi = p.sem("ldi")
    ldw = [p.sem("ldw0"), p.sem("ldw1")]
    sc = p.sem("sc")
    te = p.sem("te")
    ve = p.sem("ve")
    sty = [p.sem("sty0"), p.sem("sty1")]

    for t in range(NT):
        p.op("sync", lambda e, t=t: e.dma_start(out=x_sb[:, t * D:(t + 1) * D],
                                                in_=x_h[t * 128:(t + 1) * 128, :]), inc=ldh, delta=16)
    x_loaded = ldh.v
    p.op("gpsimd", lambda e: e.dma_start(out=ident[:], in_=ident_h), inc=ldi, delta=16)
    ident_ld = ldi.v

    sign_done = [0] * 8
    wprep_te = [0] * 8
    for o in range(8):
        par = o % 2
        if o >= 2:
            p.wt("gpsimd", sc, sign_done[o - 2])
        p.op("gpsimd", lambda e, par=par, o=o: e.dma_start(
            out=wf32[:, par * D:(par + 1) * D], in_=w_h[o * 128:(o + 1) * 128, :]), inc=ldw[par], delta=16)
        p.wt("scalar", ldw[par])
        if o >= 2:
            p.wt("scalar", te, wprep_te[o - 2])
        sign_done[o] = p.op("scalar", lambda e, par=par: e.sign(
            out=wsg[:, par * D:(par + 1) * D], in_=wf32[:, par * D:(par + 1) * D]), inc=sc)
        pst = ps_t if par == 0 else ps_t2
        p.wt("tensor", sc, sign_done[o])
        if o >= 2:
            p.wt("tensor", ve)
        for j in range(8):
            p.op("tensor", lambda e, j=j, par=par, pst=pst: e.transpose(
                pst[:, j * 128:(j + 1) * 128], wsg[:, par * D + j * 128: par * D + (j + 1) * 128],
                ident[:]), inc=(te if j == 7 else None))
        wprep_te[o] = te.v
        p.wt("vector", te, wprep_te[o])
        p.op("vector", lambda e, o=o, pst=pst: e.tensor_copy(
            out=wsgT[:].rearrange("p (j od) -> p j od", j=8)[:, :, o * 128:(o + 1) * 128],
            in_=pst[:].rearrange("p (j c) -> p j c", j=8)), inc=ve)
    wprep_ve = ve.v


    p.wt("vector", ldh, x_loaded)
    emit_ln(p, x_sb, nmean, rstd, st6, mvbuf, ve, sc)
    stats_ve = ve.v

    norm_done = [0] * NT
    lnt_done = [0] * NT
    tr_done = [0] * NT
    mm_done = [0] * NT
    y_done = [0] * NT
    st_done = [0] * NT
    for t in range(NT):
        par = t % 2
        if t >= 2:
            p.wt("vector", te, tr_done[t - 2])
        emit_norm_tile(p, x_sb[:, t * D:(t + 1) * D], ln_bf[:, par * D:(par + 1) * D],
                       nmean[:, t:t + 1], rstd[:, t:t + 1], ve)
        norm_done[t] = ve.v
        pst = ps_t if par == 0 else ps_t2
        p.wt("tensor", ve, norm_done[t])
        for j in range(8):
            p.op("tensor", lambda e, j=j, par=par, pst=pst: e.transpose(
                pst[:, j * 128:(j + 1) * 128],
                ln_bf[:, par * D + j * 128: par * D + (j + 1) * 128], ident[:]),
                inc=(te if j == 7 else None))
        tr_done[t] = te.v
        tr_te = te.v
        p.wt("vector", te, tr_te)
        p.op("vector", lambda e, par=par, pst=pst: e.tensor_copy(
            out=lnt[:, par * D:(par + 1) * D], in_=pst[:, :]), inc=ve)
        lnt_done[t] = ve.v
        p.wt("tensor", ve, lnt_done[t])
        if t >= 1:
            p.wt("tensor", ve, y_done[t - 1])
        for h, ps in ((0, ps_a), (1, ps_b)):
            for j in range(8):
                p.op("tensor", lambda e, j=j, h=h, ps=ps, par=par: e.matmul(
                    ps[:, :], lnt[:, par * D + j * 128: par * D + (j + 1) * 128],
                    wsgT[:, j * D + h * 512: j * D + h * 512 + 512],
                    start=(j == 0), stop=(j == 7)),
                    inc=(te if (h, j) == (1, 7) else None))
        mm_done[t] = te.v
        p.wt("vector", te, mm_done[t])
        if t >= 2:
            p.wt("vector", sty[par], st_done[t - 2])
        for h, ps in ((0, ps_a), (1, ps_b)):
            p.op("vector", lambda e, h=h, ps=ps, par=par, t=t: e.scalar_tensor_tensor(
                out=y_sb[:, par * D + h * 512: par * D + h * 512 + 512], in0=ps[:, :],
                scalar=0.5, in1=x_sb[:, t * D + h * 512: t * D + h * 512 + 512],
                op0=ALU.mult, op1=ALU.add), inc=(ve if h == 1 else None))
        y_done[t] = ve.v
        p.wt("scalar", ve, y_done[t])
        p.op("scalar", lambda e, t=t, par=par: e.dma_start(
            out=y_h[t * 128:(t + 1) * 128, :], in_=y_sb[:, par * D:(par + 1) * D]),
            inc=sty[par], delta=16)
        st_done[t] = sty[par].v
    p.wt("vector", sty[0], sty[0].v)
    p.wt("vector", sty[1], sty[1].v)

    p.q["tensor"].insert(0, lambda e: e.wait_ge(ldi.h, ident_ld))
    return p



def emit_wT_prep(p, w_h, dst, wf32, wbf, ps_pair, ident, ldw_pair, sc, te, ve,
                 rows=D, cols=D, op="copy"):
    nrow = rows // 128
    ncol = cols // 128
    sc_base = sc.v
    te_base = te.v
    sign_done = [0] * nrow
    tdone = [0] * nrow
    for o in range(nrow):
        par = o % 2
        if o >= 2:
            p.wt("gpsimd", sc, sign_done[o - 2])
        else:
            p.wt("gpsimd", sc, sc_base)
        p.op("gpsimd", lambda e, par=par, o=o: e.dma_start(
            out=wf32[:, par * cols:par * cols + cols],
            in_=w_h[o * 128:(o + 1) * 128, :]), inc=ldw_pair[par], delta=16)
        p.wt("scalar", ldw_pair[par])
        p.wt("scalar", te, tdone[o - 2] if o >= 2 else te_base)
        fn = (lambda e, par=par: e.sign(out=wbf[:, par * cols:par * cols + cols],
                                        in_=wf32[:, par * cols:par * cols + cols])) if op == "sign" else              (lambda e, par=par: e.copy(out=wbf[:, par * cols:par * cols + cols],
                                        in_=wf32[:, par * cols:par * cols + cols]))
        sign_done[o] = p.op("scalar", fn, inc=sc)
        pst = ps_pair[par]
        p.wt("tensor", sc, sign_done[o])
        p.wt("tensor", ve)
        for j in range(ncol):
            p.op("tensor", lambda e, j=j, par=par, pst=pst: e.transpose(
                pst[:, j * 128:(j + 1) * 128],
                wbf[:, par * cols + j * 128: par * cols + (j + 1) * 128], ident[:]),
                inc=(te if j == ncol - 1 else None))
        tdone[o] = te.v
        p.wt("vector", te, tdone[o])
        p.op("vector", lambda e, o=o, pst=pst: e.tensor_copy(
            out=dst[:].rearrange("p (j od) -> p j od", j=ncol)[:, :, o * 128:(o + 1) * 128],
            in_=pst[:].rearrange("p (j c) -> p j c", j=ncol)[:, :, 0:128]), inc=ve)


def build_l3f():
    p = P()
    x_h = p.inp("x", [R, D])
    wr_h = p.inp("wr", [D, D])
    wk_h = p.inp("wk", [D, D])
    wv_h = p.inp("wv", [D, D])
    mur_h = p.inp("mur", [1, D])
    muk_h = p.inp("muk", [1, D])
    xp_h = p.inp("xp", [1, D])
    ident_h = p.inp("ident", [128, 128])
    y_h = p.out("y", [R, D])

    x_sb = p.sbuf("x_sb", [128, 4 * D], F32)
    ident = p.sbuf("ident_sb", [128, 128], BF16)
    wf32 = p.sbuf("wf32", [128, 2 * D], F32)
    wbf = p.sbuf("wbf", [128, 2 * D], BF16)
    wrT = p.sbuf("wrT", [128, 8 * D], BF16)
    wkT = p.sbuf("wkT", [128, 8 * D], BF16)
    wvT = p.sbuf("wvT", [128, 8 * D], BF16)
    mu_row = p.sbuf("mu_row", [1, 2 * D], F32)
    mu_fm = p.sbuf("mu_fm", [128, 4 * 8], F32)
    xp_sb = p.sbuf("xp_sb", [1, D], F32)
    xpln = p.sbuf("xpln", [1, D], BF16)
    st6 = p.sbuf("st6", [128, 12], F32)
    mvbuf = p.sbuf("mvbuf", [128, 2 * NT], F32)
    nmean = p.sbuf("nmean", [128, NT], F32)
    rstd = p.sbuf("rstd", [128, NT], F32)
    pst6 = p.sbuf("pst6", [1, 12], F32)
    pmv = p.sbuf("pmv", [1, 2], F32)
    ln_bf = p.sbuf("ln_bf", [128, 2 * D], BF16)
    lnt = p.sbuf("lnt", [128, 2 * 8 * 129], BF16)
    xrT = p.sbuf("xrT", [128, D], BF16)
    xkT = p.sbuf("xkT", [128, D], BF16)
    r_sb = p.sbuf("r_sb", [128, D], F32)
    kact = p.sbuf("kact", [128, D], F32)
    kact_bf = p.sbuf("kact_bf", [128, D], BF16)
    kactT = p.sbuf("kactT", [128, D], BF16)
    vtmp = p.sbuf("vtmp", [128, D], F32)
    y_sb = p.sbuf("y_sb", [128, 2 * D], F32)

    ps_t = p.psum("ps_t", [128, 1024], BF16)
    ps_t2 = p.psum("ps_t2", [128, 1024], BF16)
    ps_a = p.psum("ps_a", [128, 512], F32)
    ps_b = p.psum("ps_b", [128, 512], F32)
    ps_c = p.psum("ps_c", [128, 512], F32)
    ps_d = p.psum("ps_d", [128, 512], F32)

    ldh = p.sem("ldh")
    ldx = [p.sem("ldx%d" % i) for i in range(4)]
    ldi = p.sem("ldi")
    ldw = [p.sem("ldw0"), p.sem("ldw1")]
    sc = p.sem("sc")
    te = p.sem("te")
    ve = p.sem("ve")
    sty = [p.sem("sty0"), p.sem("sty1")]

    p.op("sync", lambda e: e.dma_start(out=mu_row[:, 0:D], in_=mur_h), inc=ldh, delta=16)
    p.op("sync", lambda e: e.dma_start(out=mu_row[:, D:2 * D], in_=muk_h), inc=ldh, delta=16)
    p.op("sync", lambda e: e.dma_start(out=xp_sb[:], in_=xp_h), inc=ldh, delta=16)
    all_ld = ldh.v
    p.op("gpsimd", lambda e: e.dma_start(out=ident[:], in_=ident_h), inc=ldi, delta=16)
    p.q["tensor"].append(lambda e: e.wait_ge(ldi.h, 16))

    emit_wT_prep(p, wr_h, wrT, wf32, wbf, (ps_t, ps_t2), ident, ldw, sc, te, ve)
    emit_wT_prep(p, wk_h, wkT, wf32, wbf, (ps_t, ps_t2), ident, ldw, sc, te, ve)
    emit_wT_prep(p, wv_h, wvT, wf32, wbf, (ps_t, ps_t2), ident, ldw, sc, te, ve)

    stats_done = [0] * NT
    xld = [0] * NT
    for t in range(NT):
        sl = t % 4
        if t >= 4:
            p.wt("sync", ve, stats_done[t - 4])
        p.op("sync", lambda e, t=t, sl=sl: e.dma_start(
            out=x_sb[:, sl * D:(sl + 1) * D], in_=x_h[t * 128:(t + 1) * 128, :]),
            inc=ldx[sl], delta=16)
        xld[t] = ldx[sl].v
        p.wt("vector", ldx[sl], xld[t])
        p.op("vector", lambda e, sl=sl: e.bn_stats(out=st6[:, 0:6], in_=x_sb[:, sl * D:sl * D + 512]), inc=ve)
        p.wt("vector", ve)
        p.op("vector", lambda e, sl=sl: e.bn_stats(out=st6[:, 6:12], in_=x_sb[:, sl * D + 512:sl * D + 1024]), inc=ve)
        p.wt("vector", ve)
        p.op("vector", lambda e, t=t: e.bn_aggr(out=mvbuf[:, 2 * t:2 * t + 2], in_=st6[:, 0:12]), inc=ve)
        p.wt("vector", ve)
        stats_done[t] = ve.v
    p.op("vector", lambda e: e.tensor_scalar(out=rstd[:, 0:NT], in0=mvbuf[:, 1:2 * NT:2],
                                             scalar1=float(EPS), scalar2=None, op0=ALU.add), inc=ve)
    p.wt("scalar", ve)
    p.op("scalar", lambda e: e.sqrt(out=rstd[:, 0:NT], in_=rstd[:, 0:NT]), inc=sc)
    p.wt("vector", sc)
    p.op("vector", lambda e: e.reciprocal(out=rstd[:, 0:NT], in_=rstd[:, 0:NT]), inc=ve)
    p.op("vector", lambda e: e.tensor_scalar(out=nmean[:, 0:NT], in0=mvbuf[:, 0:2 * NT:2],
                                             scalar1=-1.0, scalar2=None, op0=ALU.mult), inc=ve)
    p.wt("vector", ve)
    p.wt("vector", ldh, all_ld)
    p.op("vector", lambda e: e.bn_stats(out=pst6[:, 0:6], in_=xp_sb[:, 0:512]), inc=ve)
    p.wt("vector", ve)
    p.op("vector", lambda e: e.bn_stats(out=pst6[:, 6:12], in_=xp_sb[:, 512:1024]), inc=ve)
    p.wt("vector", ve)
    p.op("vector", lambda e: e.bn_aggr(out=pmv[:, 0:2], in_=pst6[:, 0:12]), inc=ve)
    p.wt("vector", ve)
    p.op("vector", lambda e: e.tensor_scalar(out=pmv[:, 1:2], in0=pmv[:, 1:2],
                                             scalar1=float(EPS), scalar2=None, op0=ALU.add), inc=ve)
    p.wt("scalar", ve)
    p.op("scalar", lambda e: e.sqrt(out=pmv[:, 1:2], in_=pmv[:, 1:2]), inc=sc)
    p.wt("vector", sc)
    p.op("vector", lambda e: e.reciprocal(out=pmv[:, 1:2], in_=pmv[:, 1:2]), inc=ve)
    p.op("vector", lambda e: e.tensor_scalar(out=pmv[:, 0:1], in0=pmv[:, 0:1],
                                             scalar1=-1.0, scalar2=None, op0=ALU.mult), inc=ve)
    p.wt("vector", ve)
    p.op("vector", lambda e: e.tensor_scalar(out=xpln[:], in0=xp_sb[:], scalar1=pmv[:, 0:1],
                                             scalar2=pmv[:, 1:2], op0=ALU.add, op1=ALU.mult), inc=ve)
    xpln_ve = ve.v

    mu_bf = p.sbuf("mu_bf", [1, 2 * D], BF16)
    p.wt("scalar", ldh, all_ld)
    mu_cast = p.op("scalar", lambda e: e.copy(out=mu_bf[:], in_=mu_row[:]), inc=sc)
    p.wt("tensor", sc, mu_cast)
    p.wt("tensor", ve)
    for i in range(16):
        p.op("tensor", lambda e, i=i: e.transpose(
            ps_t[:, i * 64:i * 64 + 1] if False else ps_t[:, i * 128:(i + 1) * 128][:, 0:1],
            mu_bf[:, i * 128:(i + 1) * 128], ident[:]), inc=None) if False else None
    for i in range(16):
        p.op("tensor", lambda e, i=i: e.transpose(
            ps_t2[:, 2 * i:2 * i + 1], mu_bf[:, i * 128:(i + 1) * 128], ident[0:1, 0:1]),
            inc=(te if i == 15 else None))
    mu_t = te.v
    p.wt("vector", te, mu_t)
    p.op("vector", lambda e: e.tensor_copy(out=mu_fm[:, 0:8], in_=ps_t2[:, 0:16:2]), inc=ve)
    p.op("vector", lambda e: e.tensor_copy(out=mu_fm[:, 16:24], in_=ps_t2[:, 16:32:2]), inc=ve)
    p.wt("vector", ve)
    p.op("vector", lambda e: e.tensor_scalar(out=mu_fm[:, 8:16], in0=mu_fm[:, 0:8],
                                             scalar1=-1.0, scalar2=1.0, op0=ALU.mult, op1=ALU.add), inc=ve)
    p.op("vector", lambda e: e.tensor_scalar(out=mu_fm[:, 24:32], in0=mu_fm[:, 16:24],
                                             scalar1=-1.0, scalar2=1.0, op0=ALU.mult, op1=ALU.add), inc=ve)
    mu_ve = ve.v

    p.wt("tensor", ve, mu_ve)
    for k in range(8):
        p.op("tensor", lambda e, k=k: e.transpose(
            ps_t[:, 2 * k:2 * k + 1], xpln[:, k * 128:(k + 1) * 128], ident[0:1, 0:1]),
            inc=(te if k == 7 else None))
    xpt_te = te.v
    p.wt("vector", te, xpt_te)
    p.op("vector", lambda e: e.tensor_copy(
        out=lnt[:].rearrange("p (k c) -> p k c", c=129)[:, 0:8, 0:1],
        in_=ps_t[:, 0:16:2].rearrange("p (k one) -> p k one", one=1)), inc=ve)
    margin0_ve = ve.v

    lnt_w = 8 * 129
    norm_done = [0] * NT
    tr_done = [0] * NT
    lnt_done = [0] * NT
    mix_done = [0] * NT
    mm_rk = [0] * NT
    y_done = [0] * NT
    st_done = [0] * NT
    kT_done = [0] * NT
    for t in range(NT):
        par = t % 2
        sl = t % 4
        if t >= 4:
            p.wt("sync", ve, y_done[t - 4])
        else:
            p.wt("sync", ve, stats_done[NT - 1])
        p.op("sync", lambda e, t=t, sl=sl: e.dma_start(
            out=x_sb[:, sl * D:(sl + 1) * D], in_=x_h[t * 128:(t + 1) * 128, :]),
            inc=ldx[sl], delta=16)
        xld[t] = ldx[sl].v
        if t >= 2:
            p.wt("vector", te, tr_done[t - 2])
        p.wt("vector", ldx[sl], xld[t])
        emit_norm_tile(p, x_sb[:, sl * D:(sl + 1) * D], ln_bf[:, par * D:(par + 1) * D],
                       nmean[:, t:t + 1], rstd[:, t:t + 1], ve)
        norm_done[t] = ve.v
        pst = ps_t if par == 0 else ps_t2
        p.wt("tensor", ve, norm_done[t])
        if t >= 2:
            p.wt("tensor", ve, kT_done[t - 2])
        if t == 0:
            p.wt("tensor", ve, margin0_ve)
        for j in range(8):
            p.op("tensor", lambda e, j=j, par=par, pst=pst: e.transpose(
                pst[:, j * 128:(j + 1) * 128],
                ln_bf[:, par * D + j * 128: par * D + (j + 1) * 128], ident[:]),
                inc=(te if j == 7 else None))
        tr_done[t] = te.v
        p.wt("vector", te, tr_done[t])
        p.op("vector", lambda e, par=par, pst=pst: e.tensor_copy(
            out=lnt[:, par * lnt_w:(par + 1) * lnt_w].rearrange("p (k c) -> p k c", c=129)[:, :, 1:129],
            in_=pst[:].rearrange("p (k c) -> p k c", c=128)), inc=ve)
        p.wt("vector", ve)
        p.op("vector", lambda e, par=par: e.tensor_copy(
            out=lnt[:, (1 - par) * lnt_w:(2 - par) * lnt_w].rearrange("p (k c) -> p k c", c=129)[:, :, 0:1],
            in_=lnt[:, par * lnt_w:(par + 1) * lnt_w].rearrange("p (k c) -> p k c", c=129)[:, :, 128:129]),
            inc=ve)
        lnt_done[t] = ve.v
        p.wt("vector", ve, lnt_done[t])
        lcur = lnt[:, par * lnt_w:(par + 1) * lnt_w].rearrange("p (k c) -> p k c", c=129)[:, :, 1:129]
        lprev = lnt[:, par * lnt_w:(par + 1) * lnt_w].rearrange("p (k c) -> p k c", c=129)[:, :, 0:128]
        xr3 = xrT[:].rearrange("p (k c) -> p k c", c=128)
        xk3 = xkT[:].rearrange("p (k c) -> p k c", c=128)
        p.op("vector", lambda e, lcur=lcur, xr3=xr3: e.tensor_tensor(
            out=xr3, in0=lcur, in1=mu_fm[:, 0:8].unsqueeze(2).broadcast_to((128, 8, 128)),
            op=ALU.mult), inc=ve)
        p.op("vector", lambda e, lprev=lprev: e.tensor_tensor(
            out=vtmp[:].rearrange("p (k c) -> p k c", c=128), in0=lprev,
            in1=mu_fm[:, 8:16].unsqueeze(2).broadcast_to((128, 8, 128)), op=ALU.mult), inc=ve)
        p.wt("vector", ve)
        p.op("vector", lambda e, xr3=xr3: e.tensor_tensor(
            out=xr3, in0=xr3, in1=vtmp[:].rearrange("p (k c) -> p k c", c=128), op=ALU.add), inc=ve)
        p.op("vector", lambda e, lcur=lcur, xk3=xk3: e.tensor_tensor(
            out=xk3, in0=lcur, in1=mu_fm[:, 16:24].unsqueeze(2).broadcast_to((128, 8, 128)),
            op=ALU.mult), inc=ve)
        p.wt("vector", ve)
        p.op("vector", lambda e, lprev=lprev: e.tensor_tensor(
            out=vtmp[:].rearrange("p (k c) -> p k c", c=128), in0=lprev,
            in1=mu_fm[:, 24:32].unsqueeze(2).broadcast_to((128, 8, 128)), op=ALU.mult), inc=ve)
        p.wt("vector", ve)
        p.op("vector", lambda e, xk3=xk3: e.tensor_tensor(
            out=xk3, in0=xk3, in1=vtmp[:].rearrange("p (k c) -> p k c", c=128), op=ALU.add), inc=ve)
        mix_done[t] = ve.v
        p.wt("tensor", ve, mix_done[t])
        if t >= 1:
            p.wt("vector", te, 0) if False else None
        for (ps2, wT, src) in ((( ps_a, ps_b), wrT, xrT), ((ps_c, ps_d), wkT, xkT)):
            for h, ps in ((0, ps2[0]), (1, ps2[1])):
                for j in range(8):
                    p.op("tensor", lambda e, j=j, h=h, ps=ps, wT=wT, src=src: e.matmul(
                        ps[:, :], src[:, j * 128:(j + 1) * 128],
                        wT[:, j * D + h * 512: j * D + h * 512 + 512],
                        start=(j == 0), stop=(j == 7)),
                        inc=(te if j == 7 else None))
        mm_rk[t] = te.v
        p.wt("scalar", te, mm_rk[t])
        if t >= 1:
            p.wt("scalar", ve, y_done[t - 1])
        p.op("scalar", lambda e: e.activation(out=r_sb[:, 0:512], in_=ps_a[:, :], func=AF.Sigmoid), inc=sc)
        p.op("scalar", lambda e: e.activation(out=r_sb[:, 512:1024], in_=ps_b[:, :], func=AF.Sigmoid), inc=sc)
        sig_sc = sc.v
        p.wt("vector", sc, sig_sc)
        p.op("vector", lambda e: e.tensor_scalar(out=kact[:, 0:512], in0=ps_c[:, :],
                                                 scalar1=0.0, scalar2=None, op0=ALU.max), inc=ve)
        p.op("vector", lambda e: e.tensor_scalar(out=kact[:, 512:1024], in0=ps_d[:, :],
                                                 scalar1=0.0, scalar2=None, op0=ALU.max), inc=ve)
        p.wt("vector", ve)
        p.op("vector", lambda e: e.tensor_tensor(out=kact_bf[:], in0=kact[:], in1=kact[:], op=ALU.mult), inc=ve)
        kk_ve = ve.v
        p.wt("tensor", ve, kk_ve)
        pst2 = ps_t if par == 0 else ps_t2
        for j in range(8):
            p.op("tensor", lambda e, j=j, pst2=pst2: e.transpose(
                pst2[:, j * 128:(j + 1) * 128], kact_bf[:, j * 128:(j + 1) * 128], ident[:]),
                inc=(te if j == 7 else None))
        kT_te = te.v
        p.wt("vector", te, kT_te)
        p.op("vector", lambda e, pst2=pst2: e.tensor_copy(out=kactT[:], in_=pst2[:, :]), inc=ve)
        kT_done[t] = ve.v
        p.wt("tensor", ve, kT_done[t])
        for h, ps in ((0, ps_a), (1, ps_b)):
            for j in range(8):
                p.op("tensor", lambda e, j=j, h=h, ps=ps: e.matmul(
                    ps[:, :], kactT[:, j * 128:(j + 1) * 128],
                    wvT[:, j * D + h * 512: j * D + h * 512 + 512],
                    start=(j == 0), stop=(j == 7)),
                    inc=(te if (h, j) == (1, 7) else None))
        mm_v = te.v
        p.wt("vector", te, mm_v)
        if t >= 2:
            p.wt("vector", sty[par], st_done[t - 2])
        for h, ps in ((0, ps_a), (1, ps_b)):
            p.op("vector", lambda e, h=h, ps=ps: e.tensor_tensor(
                out=vtmp[:, h * 512:(h + 1) * 512], in0=ps[:, :],
                in1=r_sb[:, h * 512:(h + 1) * 512], op=ALU.mult), inc=ve)
        p.wt("vector", ve)
        for h in (0, 1):
            p.op("vector", lambda e, h=h, par=par, sl=sl: e.scalar_tensor_tensor(
                out=y_sb[:, par * D + h * 512: par * D + h * 512 + 512],
                in0=vtmp[:, h * 512:(h + 1) * 512], scalar=0.5,
                in1=x_sb[:, sl * D + h * 512: sl * D + h * 512 + 512],
                op0=ALU.mult, op1=ALU.add), inc=(ve if h == 1 else None))
        y_done[t] = ve.v
        p.wt("scalar", ve, y_done[t])
        p.op("scalar", lambda e, t=t, par=par: e.dma_start(
            out=y_h[t * 128:(t + 1) * 128, :], in_=y_sb[:, par * D:(par + 1) * D]),
            inc=sty[par], delta=16)
        st_done[t] = sty[par].v
    p.wt("vector", sty[0], sty[0].v)
    p.wt("vector", sty[1], sty[1].v)
    return p



H2 = 4096


def build_l2f():
    p = P()
    x_h = p.inp("x", [R, D])
    w1_h = p.inp("W1", [H2, D])
    b1_h = p.inp("b1", [1, H2])
    w2_h = p.inp("W2", [D, H2])
    b2_h = p.inp("b2", [1, D])
    scale_h = p.inp("scale", [1, 1])
    lt_h = p.inp("lt_raw", [16, D])
    ident_h = p.inp("ident", [128, 128])
    y_h = p.out("y", [R, D])

    w1t_d = p.dram("w1t_d", [H2, D], BF16)
    w2t_d = p.dram("w2t_d", [4 * 32 * 128, 256], BF16)
    lnlt_d = p.dram("lnlt_d", [16, D], F32)

    x_sb = p.sbuf("x_sb", [128, 8 * D], F32)
    ident = p.sbuf("ident_sb", [128, 128], BF16)
    wf32 = p.sbuf("wf32", [128, 2 * D], F32)
    wbf = p.sbuf("wbf", [128, D], BF16)
    wtmp = p.sbuf("wtmp", [128, D], BF16)
    st6 = p.sbuf("st6", [128, 12], F32)
    mvbuf = p.sbuf("mvbuf", [128, 2 * NT], F32)
    nmean = p.sbuf("nmean", [128, NT], F32)
    rstd = p.sbuf("rstd", [128, NT], F32)
    ln_bf = p.sbuf("ln_bf", [128, 2 * D], BF16)
    xlnT = p.sbuf("xlnT", [128, 8 * 512], BF16)
    lt_sb = p.sbuf("lt_sb", [16, D], F32)
    lt6 = p.sbuf("lt6", [16, 12], F32)
    ltmv = p.sbuf("ltmv", [16, 2], F32)
    lnlt = p.sbuf("lnlt", [16, D], BF16)
    ltT = p.sbuf("ltT", [128, 8 * 16], BF16)
    e1 = p.sbuf("e1", [4, H2], BF16)
    mt = p.sbuf("mt", [128, 32 * 16], BF16)
    wtil = p.sbuf("wtil", [16, D], BF16)
    b1_sb = p.sbuf("b1_sb", [1, H2], F32)
    b1_bf = p.sbuf("b1_bf", [1, H2], BF16)
    b2_sb = p.sbuf("b2_sb", [1, D], F32)
    b2_bf = p.sbuf("b2_bf", [1, D], BF16)
    ones_bf = p.sbuf("ones_bf", [1, 512], BF16)
    onesf = p.sbuf("onesf", [1, 128], F32)
    scale_sb = p.sbuf("scale_sb", [1, 1], F32)
    scale_bc = p.sbuf("scale_bc", [128, 1], F32)
    w1g = p.sbuf("w1g", [128, 8 * 1024], BF16)
    w2g = p.sbuf("w2g", [128, 32 * 256], BF16)
    h_sb = p.sbuf("h_sb", [128, 32 * 512], BF16)
    dgs_bf = p.sbuf("dgs_bf", [4, 512], BF16)
    dot2_bf = p.sbuf("dot2_bf", [16, 512], BF16)
    y_sb = p.sbuf("y_sb", [128, 4 * D], F32)

    ps_t = p.psum("ps_t", [128, 1024], BF16)
    ps_t2 = p.psum("ps_t2", [128, 1024], BF16)
    ps_h = p.psum("ps_h", [128, 512], F32)
    ps_h2 = p.psum("ps_h2", [128, 512], F32)
    ps_sm = p.psum("ps_sm", [16, 512], F32)
    ps_o4 = p.psum("ps_o4", [128, 1024], F32)

    ldh = p.sem("ldh")
    ldx = [p.sem("ldx%d" % i) for i in range(8)]
    ldi = p.sem("ldi")
    ldw = [p.sem("ldw0"), p.sem("ldw1")]
    ldwg = [p.sem("ldwg0"), p.sem("ldwg1")]
    sc = p.sem("sc")
    te = p.sem("te")
    ve = p.sem("ve")
    gp = p.sem("gp")
    stw = [p.sem("stw0"), p.sem("stw1")]
    sty = [p.sem("sty0"), p.sem("sty1")]
    stl = p.sem("stl")

    p.op("sync", lambda e: e.dma_start(out=lt_sb[:], in_=lt_h), inc=ldh, delta=16)
    p.op("sync", lambda e: e.dma_start(out=b1_sb[:], in_=b1_h), inc=ldh, delta=16)
    p.op("sync", lambda e: e.dma_start(out=b2_sb[:], in_=b2_h), inc=ldh, delta=16)
    p.op("sync", lambda e: e.dma_start(out=scale_sb[:], in_=scale_h), inc=ldh, delta=16)
    small_ld = ldh.v
    p.op("gpsimd", lambda e: e.dma_start(out=ident[:], in_=ident_h), inc=ldi, delta=16)
    p.op("gpsimd", lambda e: e.memset(ones_bf[:], 1.0), inc=gp)
    p.op("gpsimd", lambda e: e.memset(onesf[:], 1.0), inc=gp)
    p.op("gpsimd", lambda e: e.memset(mt[:], 0.0), inc=gp)
    memset_gp = gp.v
    p.q["tensor"].append(lambda e: e.wait_ge(ldi.h, 16))

    p.wt("tensor", ldh, small_ld)
    p.wt("tensor", gp, memset_gp)
    sb_mm = p.op("tensor", lambda e: e.matmul(ps_sm[0:128, 0:1] if False else ps_h[0:128, 0:1],
                                              onesf[:, 0:128], scale_sb[:, 0:1],
                                              start=True, stop=True), inc=te)
    p.wt("vector", te, sb_mm)
    p.op("vector", lambda e: e.tensor_copy(out=scale_bc[:], in_=ps_h[:, 0:1]), inc=ve)
    scale_ve = ve.v

    p.wt("scalar", ldh, small_ld)
    p.op("scalar", lambda e: e.copy(out=b1_bf[:], in_=b1_sb[:]), inc=sc)
    p.op("scalar", lambda e: e.copy(out=b2_bf[:], in_=b2_sb[:]), inc=sc)
    bias_sc = sc.v

    p.wt("vector", ldh, small_ld)
    p.op("vector", lambda e: e.bn_stats(out=lt6[:, 0:6], in_=lt_sb[:, 0:512]), inc=ve)
    p.wt("vector", ve)
    p.op("vector", lambda e: e.bn_stats(out=lt6[:, 6:12], in_=lt_sb[:, 512:1024]), inc=ve)
    p.wt("vector", ve)
    p.op("vector", lambda e: e.bn_aggr(out=ltmv[:, 0:2], in_=lt6[:, 0:12]), inc=ve)
    p.wt("vector", ve)
    p.op("vector", lambda e: e.tensor_scalar(out=ltmv[:, 1:2], in0=ltmv[:, 1:2],
                                             scalar1=float(EPS), scalar2=None, op0=ALU.add), inc=ve)
    p.wt("scalar", ve)
    p.wt("scalar", sc, bias_sc)
    p.op("scalar", lambda e: e.sqrt(out=ltmv[:, 1:2], in_=ltmv[:, 1:2]), inc=sc)
    p.wt("vector", sc)
    p.op("vector", lambda e: e.reciprocal(out=ltmv[:, 1:2], in_=ltmv[:, 1:2]), inc=ve)
    p.wt("vector", ve)
    p.op("vector", lambda e: e.tensor_scalar(out=ltmv[:, 0:1], in0=ltmv[:, 0:1],
                                             scalar1=-1.0, scalar2=None, op0=ALU.mult), inc=ve)
    p.wt("vector", ve)
    p.op("vector", lambda e: e.tensor_scalar(out=lnlt[:], in0=lt_sb[:], scalar1=ltmv[:, 0:1],
                                             scalar2=ltmv[:, 1:2], op0=ALU.add, op1=ALU.mult), inc=ve)
    lnlt_ve = ve.v
    p.wt("gpsimd", ve, lnlt_ve)
    p.op("gpsimd", lambda e: e.dma_start(out=lnlt_d, in_=lnlt[:]), inc=stl, delta=16)
    lnlt_st = stl.v
    p.wt("tensor", ve, lnlt_ve)
    for k in range(8):
        p.op("tensor", lambda e, k=k: e.transpose(
            ps_t[:, k * 16:(k + 1) * 16], lnlt[:, k * 128:(k + 1) * 128], ident[0:16, 0:16]),
            inc=(te if k == 7 else None))
    ltT_te = te.v
    p.wt("vector", te, ltT_te)
    p.op("vector", lambda e: e.tensor_copy(out=ltT[:], in_=ps_t[:, 0:128]), inc=ve)
    ltT_ve = ve.v
    p.wt("vector", ve, ltT_ve)
    p.wt("vector", gp, memset_gp)
    for j in range(4):
        for vc in range(8):
            hhc = j * 8 + vc
            p.op("vector", lambda e, j=j, vc=vc, hhc=hhc: e.tensor_copy(
                out=mt[:, hhc * 16 + j: hhc * 16 + 16: 4],
                in_=ltT[:, vc * 16 + 9: vc * 16 + 16: 2]),
                inc=(ve if hhc == 31 else None))
    mt_ve = ve.v
    p.wt("gpsimd", stl, lnlt_st)
    p.op("gpsimd", lambda e: e.memset(e1[:], 0.0), inc=gp)
    p.op("gpsimd", lambda e: e.memset(wtil[:], 0.0), inc=gp)
    p.wt("gpsimd", gp)
    for g in range(4):
        p.op("gpsimd", lambda e, g=g: e.dma_start(
            out=e1[g:g + 1, g * 1024:(g + 1) * 1024], in_=lnlt_d[2 * g:2 * g + 1, :]),
            inc=stl, delta=16)
    for g in range(4):
        p.op("gpsimd", lambda e, g=g: e.dma_start(
            out=wtil[4 * g:4 * g + 4, g * 256:(g + 1) * 256],
            in_=lnlt_d[8 + 2 * g:9 + 2 * g, :].rearrange("one (c j) -> (one j) c", j=4),
            allow_slow_non_contiguous=True), inc=stl, delta=16)
    ewig_st = stl.v

    prep_last_store = [None]

    def prep_w(w_h_, nrow, dst_fn):
        for o in range(nrow):
            par = o % 2
            p.wt("gpsimd", sc)
            p.op("gpsimd", lambda e, par=par, o=o: e.dma_start(
                out=wf32[:, par * D:(par + 1) * D], in_=w_h_[o * 128:(o + 1) * 128, :]),
                inc=ldw[par], delta=16)
            p.wt("scalar", ldw[par])
            p.wt("scalar", te)
            cast_sc = p.op("scalar", lambda e, par=par: e.copy(
                out=wbf[:, 0:D], in_=wf32[:, par * D:(par + 1) * D]), inc=sc)
            pst = ps_t if par == 0 else ps_t2
            p.wt("tensor", sc, cast_sc)
            p.wt("tensor", ve)
            for j in range(8):
                p.op("tensor", lambda e, j=j, pst=pst: e.transpose(
                    pst[:, j * 128:(j + 1) * 128],
                    wbf[:, j * 128:(j + 1) * 128], ident[:]),
                    inc=(te if j == 7 else None))
            tr_v = te.v
            p.wt("vector", te, tr_v)
            if prep_last_store[0] is not None:
                p.wt("vector", prep_last_store[0][0], prep_last_store[0][1])
            p.op("vector", lambda e, pst=pst: e.tensor_copy(out=wtmp[:], in_=pst[:, 0:1024]), inc=ve)
            gat_ve = ve.v
            p.wt("gpsimd", ve, gat_ve)
            p.op("gpsimd", lambda e, o=o, dst=dst_fn(o): e.dma_start(out=dst, in_=wtmp[:]),
                 inc=stw[par], delta=16)
            prep_last_store[0] = (stw[par], stw[par].v)

    def w1_dst(o):
        hhg, g8 = o // 8, o % 8
        v = w1t_d.rearrange("(hhg j p) c -> hhg j p c", j=8, p=128)[o // 8]
        v = v[:, :, (o % 8) * 128:(o % 8) * 128 + 128]
        return v.transpose((1, 0, 2))

    def w2_dst(cb):
        def f(o):
            oc4 = (o * 128) // 256
            v = w2t_d.rearrange("(oc4 hhc p) c -> oc4 hhc p c", oc4=4, hhc=32)[oc4]
            v = v[cb * 8:(cb + 1) * 8, :, (o % 2) * 128:(o % 2) * 128 + 128]
            return v.transpose((1, 0, 2))
        return f

    prep_w(w1_h, 32, w1_dst)
    for cb in range(4):
        prep_w(w2_h[:, cb * 1024:(cb + 1) * 1024], 8, w2_dst(cb))
    wprep_stw = (stw[0].v, stw[1].v)

    stats_done = [0] * NT
    xld = [0] * NT
    for t in range(NT):
        sl = t % 8
        if t >= 8:
            p.wt("sync", ve, stats_done[t - 8])
        p.op("sync", lambda e, t=t, sl=sl: e.dma_start(
            out=x_sb[:, sl * D:(sl + 1) * D], in_=x_h[t * 128:(t + 1) * 128, :]),
            inc=ldx[sl], delta=16)
        xld[t] = ldx[sl].v
        p.wt("vector", ldx[sl], xld[t])
        p.op("vector", lambda e, sl=sl: e.bn_stats(out=st6[:, 0:6], in_=x_sb[:, sl * D:sl * D + 512]), inc=ve)
        p.wt("vector", ve)
        p.op("vector", lambda e, sl=sl: e.bn_stats(out=st6[:, 6:12], in_=x_sb[:, sl * D + 512:sl * D + 1024]), inc=ve)
        p.wt("vector", ve)
        p.op("vector", lambda e, t=t: e.bn_aggr(out=mvbuf[:, 2 * t:2 * t + 2], in_=st6[:, 0:12]), inc=ve)
        p.wt("vector", ve)
        stats_done[t] = ve.v
    p.op("vector", lambda e: e.tensor_scalar(out=rstd[:, 0:NT], in0=mvbuf[:, 1:2 * NT:2],
                                             scalar1=float(EPS), scalar2=None, op0=ALU.add), inc=ve)
    p.wt("scalar", ve)
    p.op("scalar", lambda e: e.sqrt(out=rstd[:, 0:NT], in_=rstd[:, 0:NT]), inc=sc)
    p.wt("vector", sc)
    p.op("vector", lambda e: e.reciprocal(out=rstd[:, 0:NT], in_=rstd[:, 0:NT]), inc=ve)
    p.op("vector", lambda e: e.tensor_scalar(out=nmean[:, 0:NT], in0=mvbuf[:, 0:2 * NT:2],
                                             scalar1=-1.0, scalar2=None, op0=ALU.mult), inc=ve)
    p.wt("vector", ve)
    stats_all = ve.v

    NTB = NT // 4
    y_done = [0] * NT
    st_done = [0] * NT
    h_ve_prev = 0
    out_te_prev = 0
    xlnT_te_prev = 0
    w1g_ld = [0, 0]
    w2g_ld = [0, 0]
    for tb in range(NTB):
        tbp = tb % 2
        tiles = [tb * 4 + i for i in range(4)]
        for i, t in enumerate(tiles):
            sl = t % 8
            if t >= 8:
                p.wt("sync", ve, y_done[t - 8])
            else:
                p.wt("sync", ve, stats_done[NT - 1])
            p.op("sync", lambda e, t=t, sl=sl: e.dma_start(
                out=x_sb[:, sl * D:(sl + 1) * D], in_=x_h[t * 128:(t + 1) * 128, :]),
                inc=ldx[sl], delta=16)
            xld[t] = ldx[sl].v
        for i, t in enumerate(tiles):
            sl = t % 8
            par = i % 2
            p.wt("vector", ldx[sl], xld[t])
            if i >= 2:
                p.wt("vector", te)
            emit_norm_tile(p, x_sb[:, sl * D:(sl + 1) * D], ln_bf[:, par * D:(par + 1) * D],
                           nmean[:, t:t + 1], rstd[:, t:t + 1], ve)
            nv = ve.v
            pst = ps_t if par == 0 else ps_t2
            p.wt("tensor", ve, nv)
            if tb > 0 and i == 0:
                p.wt("tensor", ve, h_ve_prev)
            for j in range(8):
                p.op("tensor", lambda e, j=j, par=par, pst=pst: e.transpose(
                    pst[:, j * 128:(j + 1) * 128],
                    ln_bf[:, par * D + j * 128: par * D + (j + 1) * 128], ident[:]),
                    inc=(te if j == 7 else None))
            trv = te.v
            p.wt("vector", te, trv)
            p.op("vector", lambda e, i=i, pst=pst: e.tensor_copy(
                out=xlnT[:].rearrange("p (k c) -> p k c", c=512)[:, :, i * 128:(i + 1) * 128],
                in_=pst[:].rearrange("p (k c) -> p k c", c=128)), inc=ve)
        xln_ve = ve.v
        xlnT_te_prev = te.v

        p.wt("tensor", ve, xln_ve)
        p.wt("tensor", ve, max(scale_ve, ltT_ve))
        for k in range(8):
            p.op("tensor", lambda e, k=k: e.matmul(
                ps_sm[0:4, 0:512], ltT[:, k * 16 + 1: k * 16 + 9: 2],
                xlnT[:, k * 512:(k + 1) * 512], start=(k == 0), stop=(k == 7)),
                inc=(te if k == 7 else None))
        dgs_te = te.v
        p.wt("vector", te, dgs_te)
        p.op("vector", lambda e: e.tensor_scalar(out=dgs_bf[:], in0=ps_sm[0:4, 0:512],
                                                 scalar1=scale_bc[0:4, 0:1], scalar2=None,
                                                 op0=ALU.mult), inc=ve)
        dgs_ve = ve.v

        h_banks = [ps_h, ps_h2]
        gelu_scs = []
        hh_ve = []
        for hhg in range(4):
            gpar = 0
            if tb > 0 or hhg >= 1:
                p.wt("sync", te)
            p.op("sync", lambda e, hhg=hhg: e.dma_start(
                out=w1g[:, 0:8192],
                in_=w1t_d.rearrange("(hhg cc p) c -> hhg cc p c", hhg=4, p=128)[hhg].transpose((1, 0, 2))),
                inc=ldwg[0], delta=16)
            w1g_ld[0] = ldwg[0].v
            p.wt("tensor", ldwg[0], w1g_ld[0])
            p.wt("tensor", stw[0], wprep_stw[0])
            p.wt("tensor", stw[1], wprep_stw[1])
            if hhg == 0:
                p.wt("tensor", ve, max(dgs_ve, mt_ve))
                p.wt("tensor", sc, bias_sc)
                if tb == 0:
                    p.wt("tensor", stl, ewig_st)
            for hc in range(8):
                hhc = hhg * 8 + hc
                bank = h_banks[hhc % 2]
                if hhc >= 2:
                    p.wt("tensor", sc, gelu_scs[hhc - 2])
                for k in range(8):
                    p.op("tensor", lambda e, k=k, hc=hc, bank=bank: e.matmul(
                        bank[:, :], w1g[:, k * 1024 + hc * 128: k * 1024 + (hc + 1) * 128],
                        xlnT[:, k * 512:(k + 1) * 512], start=(k == 0), stop=False))
                p.op("tensor", lambda e, hhc=hhc, bank=bank: e.matmul(
                    bank[:, :], e1[:, hhc * 128:(hhc + 1) * 128], dgs_bf[:, :],
                    start=False, stop=False))
                mmv = p.op("tensor", lambda e, hhc=hhc, bank=bank: e.matmul(
                    bank[:, :], b1_bf[:, hhc * 128:(hhc + 1) * 128], ones_bf[:, 0:512],
                    start=False, stop=True), inc=te)
                p.wt("scalar", te, mmv)
                gelu_scs.append(p.op("scalar", lambda e, hhc=hhc, bank=bank: e.activation(
                    out=h_sb[:, hhc * 512:(hhc + 1) * 512], in_=bank[:, :], func=AF.Gelu), inc=sc))
        h_sc = sc.v
        h_ve_prev = xln_ve
        h_te_after = te.v

        p.wt("tensor", sc, h_sc)
        for hhc in range(32):
            p.op("tensor", lambda e, hhc=hhc: e.matmul(
                ps_sm[0:16, 0:512], mt[:, hhc * 16:(hhc + 1) * 16],
                h_sb[:, hhc * 512:(hhc + 1) * 512], start=(hhc == 0), stop=(hhc == 31)),
                inc=(te if hhc == 31 else None))
        dot2_te = te.v
        p.wt("vector", te, dot2_te)
        p.op("vector", lambda e: e.tensor_scalar(out=dot2_bf[:], in0=ps_sm[0:16, 0:512],
                                                 scalar1=scale_bc[0:16, 0:1], scalar2=None,
                                                 op0=ALU.mult), inc=ve)
        dot2_ve = ve.v

        stt_last = [0] * 4
        prev_tb_st = (sty[1 - tbp].v if tb > 0 else 0)
        for oc4 in range(4):
            p.wt("sync", te)
            p.op("sync", lambda e, oc4=oc4: e.dma_start(
                out=w2g[:, 0:8192],
                in_=w2t_d.rearrange("(oc4 hhc p) c -> oc4 hhc p c", oc4=4, p=128)[oc4].transpose((1, 0, 2))),
                inc=ldwg[1], delta=16)
            w2g_ld[1] = ldwg[1].v
            p.wt("tensor", ldwg[1], w2g_ld[1])
            if oc4 == 0:
                p.wt("tensor", ve, dot2_ve)
            else:
                p.wt("tensor", ve, stt_last[oc4 - 1])
            omm = 0
            for ti in range(4):
                bank = ps_o4[:, ti * 256:(ti + 1) * 256]
                for hhc in range(32):
                    p.op("tensor", lambda e, hhc=hhc, ti=ti, bank=bank: e.matmul(
                        bank, h_sb[:, hhc * 512 + ti * 128: hhc * 512 + (ti + 1) * 128],
                        w2g[:, hhc * 256:(hhc + 1) * 256],
                        start=(hhc == 0), stop=False))
                p.op("tensor", lambda e, ti=ti, oc4=oc4, bank=bank: e.matmul(
                    bank, dot2_bf[:, ti * 128:(ti + 1) * 128],
                    wtil[:, oc4 * 256:(oc4 + 1) * 256], start=False, stop=False))
                omm = p.op("tensor", lambda e, oc4=oc4, bank=bank: e.matmul(
                    bank, ones_bf[:, 0:128], b2_bf[:, oc4 * 256:(oc4 + 1) * 256],
                    start=False, stop=True), inc=te)
            p.wt("vector", te, omm)
            if tb > 0 and oc4 == 0:
                p.wt("vector", sty[1 - tbp], prev_tb_st)
            for ti in range(4):
                sl = tiles[ti] % 8
                p.op("vector", lambda e, ti=ti, oc4=oc4, sl=sl: e.scalar_tensor_tensor(
                    out=y_sb[:, ti * D + oc4 * 256: ti * D + (oc4 + 1) * 256],
                    in0=ps_o4[:, ti * 256:(ti + 1) * 256], scalar=0.5,
                    in1=x_sb[:, sl * D + oc4 * 256: sl * D + (oc4 + 1) * 256],
                    op0=ALU.mult, op1=ALU.add), inc=ve)
            stt_last[oc4] = ve.v
        for ti in range(4):
            t = tiles[ti]
            y_done[t] = stt_last[3]
            p.wt("scalar", ve, stt_last[3])
            p.op("scalar", lambda e, t=t, ti=ti: e.dma_start(
                out=y_h[t * 128:(t + 1) * 128, :], in_=y_sb[:, ti * D:(ti + 1) * D]),
                inc=sty[tbp], delta=16)
            st_done[t] = sty[tbp].v
    p.wt("vector", sty[0], sty[0].v)
    p.wt("vector", sty[1], sty[1].v)
    return p


def build_l0f():
    p = P()
    x_h = p.inp("x", [R, D])
    gate_h = p.inp("gate", [D, 8])
    w1_h = p.inp("w1", [8, D, H2])
    w2_h = p.inp("w2", [8, H2, D])
    w3_h = p.inp("w3", [8, D, H2])
    ident_h = p.inp("ident", [128, 128])
    y_h = p.out("y", [R, D])
    w1s_d = p.dram("w1s_d", [8, 8, 128, 4096], BF16)
    w3s_d = p.dram("w3s_d", [8, 8, 128, 4096], BF16)
    w2s_d = p.dram("w2s_d", [8, 8, 128, 4096], BF16)

    x_sb = p.sbuf("x_sb", [128, 4 * D], F32)
    ident = p.sbuf("ident_sb", [128, 128], BF16)
    gate_bf = p.sbuf("gate_bf", [128, 64], BF16)
    st6 = p.sbuf("st6", [128, 12], F32)
    mvbuf = p.sbuf("mvbuf", [128, 2 * NT], F32)
    nmean = p.sbuf("nmean", [128, NT], F32)
    rstd = p.sbuf("rstd", [128, NT], F32)
    ln_bf = p.sbuf("ln_bf", [128, 2 * D], BF16)
    xlnT = p.sbuf("xlnT", [128, 8 * 512], BF16)
    logit = p.sbuf("logit", [128, 32], F32)
    tmpa = p.sbuf("tmpa", [128, 32], F32)
    tmpb = p.sbuf("tmpb", [128, 32], F32)
    m1s = p.sbuf("m1s", [128, 4], F32)
    m2s = p.sbuf("m2s", [128, 4], F32)
    s1s = p.sbuf("s1s", [128, 4], F32)
    s2s = p.sbuf("s2s", [128, 4], F32)
    we = p.sbuf("we", [128, 8 * NT], F32)
    w1b = p.sbuf("w1b", [128, 2 * 8 * 512], BF16)
    w3b = p.sbuf("w3b", [128, 2 * 8 * 512], BF16)
    w2b = p.sbuf("w2b", [128, 2 * 8 * 512], BF16)
    silu_sb = p.sbuf("silu_sb", [128, 512], F32)
    hid = p.sbuf("hid", [128, 32 * 512], BF16)
    acc = p.sbuf("acc", [128, 4 * D], F32)
    y_sb = p.sbuf("y_sb", [128, 4 * D], F32)

    ps_t = p.psum("ps_t", [128, 1024], BF16)
    ps_t2 = ps_t
    ps_g = p.psum("ps_g", [128, 1024], F32)
    ps_u = p.psum("ps_u", [128, 512], F32)
    ps_o4 = p.psum("ps_o4", [128, 2048], F32)

    ldx = [p.sem("ldx%d" % i) for i in range(4)]
    ldi = p.sem("ldi")
    ldg = p.sem("ldg")
    ldw1 = [p.sem("ldw1a"), p.sem("ldw1b")]
    ldw3 = [p.sem("ldw3a"), p.sem("ldw3b")]
    ldw2 = [p.sem("ldw2a"), p.sem("ldw2b")]
    lds1 = [p.sem("lds1a"), p.sem("lds1b")]
    lds3 = [p.sem("lds3a"), p.sem("lds3b")]
    lds2 = [p.sem("lds2a"), p.sem("lds2b")]
    stw1 = [p.sem("stw1a"), p.sem("stw1b")]
    stw2 = [p.sem("stw2a"), p.sem("stw2b")]
    sc = p.sem("sc")
    te = p.sem("te")
    ve = p.sem("ve")
    sty = [p.sem("sty0"), p.sem("sty1")]

    p.op("gpsimd", lambda e: e.dma_start(out=ident[:], in_=ident_h), inc=ldi, delta=16)
    p.op("gpsimd", lambda e: e.dma_start(
        out=gate_bf[:], in_=gate_h.rearrange("(cc p) e -> p cc e", p=128),
        allow_slow_non_contiguous=True), inc=ldg, delta=16)
    p.q["tensor"].append(lambda e: e.wait_ge(ldi.h, 16))

    stats_done = [0] * NT
    xld = [0] * NT
    for t in range(NT):
        sl = t % 4
        if t >= 4:
            p.wt("sync", ve, stats_done[t - 4])
        p.op("sync", lambda e, t=t, sl=sl: e.dma_start(
            out=x_sb[:, sl * D:(sl + 1) * D], in_=x_h[t * 128:(t + 1) * 128, :]),
            inc=ldx[sl], delta=16)
        xld[t] = ldx[sl].v
        p.wt("vector", ldx[sl], xld[t])
        p.op("vector", lambda e, sl=sl: e.bn_stats(out=st6[:, 0:6], in_=x_sb[:, sl * D:sl * D + 512]), inc=ve)
        p.wt("vector", ve)
        p.op("vector", lambda e, sl=sl: e.bn_stats(out=st6[:, 6:12], in_=x_sb[:, sl * D + 512:sl * D + 1024]), inc=ve)
        p.wt("vector", ve)
        p.op("vector", lambda e, t=t: e.bn_aggr(out=mvbuf[:, 2 * t:2 * t + 2], in_=st6[:, 0:12]), inc=ve)
        p.wt("vector", ve)
        stats_done[t] = ve.v
    p.op("vector", lambda e: e.tensor_scalar(out=rstd[:, 0:NT], in0=mvbuf[:, 1:2 * NT:2],
                                             scalar1=float(EPS), scalar2=None, op0=ALU.add), inc=ve)
    p.wt("scalar", ve)
    p.op("scalar", lambda e: e.sqrt(out=rstd[:, 0:NT], in_=rstd[:, 0:NT]), inc=sc)
    p.wt("vector", sc)
    p.op("vector", lambda e: e.reciprocal(out=rstd[:, 0:NT], in_=rstd[:, 0:NT]), inc=ve)
    p.op("vector", lambda e: e.tensor_scalar(out=nmean[:, 0:NT], in0=mvbuf[:, 0:2 * NT:2],
                                             scalar1=-1.0, scalar2=None, op0=ALU.mult), inc=ve)
    p.wt("vector", ve)
    stats_all = ve.v

    NTB = NT // 4

    def norm_and_transpose(tb, wait_te_xlnT):
        tiles = [tb * 4 + i for i in range(4)]
        for i, t in enumerate(tiles):
            sl = t % 4
            p.wt("sync", ve, xfree[t])
            p.op("sync", lambda e, t=t, sl=sl: e.dma_start(
                out=x_sb[:, sl * D:(sl + 1) * D], in_=x_h[t * 128:(t + 1) * 128, :]),
                inc=ldx[sl], delta=16)
            xld[t] = ldx[sl].v
        for i, t in enumerate(tiles):
            sl = t % 4
            par = i % 2
            p.wt("vector", ldx[sl], xld[t])
            p.wt("vector", te)
            emit_norm_tile(p, x_sb[:, sl * D:(sl + 1) * D], ln_bf[:, par * D:(par + 1) * D],
                           nmean[:, t:t + 1], rstd[:, t:t + 1], ve)
            nv = ve.v
            pst = ps_t if par == 0 else ps_t2
            p.wt("tensor", ve, nv)
            if i == 0 and wait_te_xlnT is not None:
                p.wt("tensor", te, wait_te_xlnT)
            for j in range(8):
                p.op("tensor", lambda e, j=j, par=par, pst=pst: e.transpose(
                    pst[:, j * 128:(j + 1) * 128],
                    ln_bf[:, par * D + j * 128: par * D + (j + 1) * 128], ident[:]),
                    inc=(te if j == 7 else None))
            trv = te.v
            p.wt("vector", te, trv)
            if i == 0 and wait_te_xlnT is not None:
                p.wt("vector", te, wait_te_xlnT)
            p.op("vector", lambda e, i=i, pst=pst: e.tensor_copy(
                out=xlnT[:].rearrange("p (k c) -> p k c", c=512)[:, :, i * 128:(i + 1) * 128],
                in_=pst[:].rearrange("p (k c) -> p k c", c=128)), inc=ve)
        return ve.v, te.v

    xfree = [stats_done[NT - 1]] * NT
    p.wt("tensor", ldg, 16)
    we_ve = ve.v

    st_done = [0] * NT
    prev_xlnT_te = te.v
    hid_prev = 0
    svu_prev = 0
    svu_prev2 = [0, 0]
    acc_done_prev = 0
    prev_tb_st = 0
    for tb in range(NTB):
        tbp = tb % 2
        tiles = [tb * 4 + i for i in range(4)]
        xln_ve, _ = norm_and_transpose(tb, prev_xlnT_te)
        p.wt("tensor", ve, xln_ve)
        p.wt("tensor", sc)
        p.wt("tensor", ve)
        for i in range(4):
            for k in range(8):
                p.op("tensor", lambda e, k=k, i=i: e.matmul(
                    ps_u[:, i * 32:i * 32 + 8],
                    xlnT[:, k * 512 + i * 128: k * 512 + (i + 1) * 128],
                    gate_bf[:, k * 8:(k + 1) * 8], start=(k == 0), stop=(k == 7)),
                    inc=(te if (i, k) == (3, 7) else None))
        lg_te = te.v
        p.wt("vector", te, lg_te)
        wecol = we[:, tb * 32:(tb + 1) * 32]
        l3 = wecol.rearrange("p (t e) -> p t e", e=8) if False else None
        lgs = logit[:, 0:32]
        for i in range(4):
            p.op("vector", lambda e, i=i: e.tensor_copy(
                out=lgs[:, i * 8:(i + 1) * 8], in_=ps_u[:, i * 32:i * 32 + 8]),
                inc=(ve if i == 3 else None))
        p.wt("vector", ve)
        l3v = lgs.rearrange("p (t e) -> p t e", e=8)
        p.op("vector", lambda e, l3v=l3v: e.tensor_reduce(
            out=m1s[:, 0:4], in_=l3v, op=ALU.max, axis=AX.X), inc=ve)
        p.wt("vector", ve)
        p.op("vector", lambda e, l3v=l3v: e.tensor_tensor(
            out=tmpa[:, 0:32].rearrange("p (t e) -> p t e", e=8), in0=l3v,
            in1=m1s[:, 0:4].unsqueeze(2).broadcast_to((128, 4, 8)), op=ALU.is_ge), inc=ve)
        p.wt("vector", ve)
        p.op("vector", lambda e: e.scalar_tensor_tensor(
            out=tmpb[:, 0:32], in0=tmpa[:, 0:32], scalar=-1e30, in1=lgs,
            op0=ALU.mult, op1=ALU.add), inc=ve)
        p.wt("vector", ve)
        p.op("vector", lambda e: e.tensor_reduce(
            out=m2s[:, 0:4], in_=tmpb[:, 0:32].rearrange("p (t e) -> p t e", e=8),
            op=ALU.max, axis=AX.X), inc=ve)
        p.wt("vector", ve)
        p.op("vector", lambda e: e.tensor_tensor(out=s1s[:, 0:4], in0=m1s[:, 0:4],
                                                 in1=m2s[:, 0:4], op=ALU.subtract), inc=ve)
        p.wt("scalar", ve)
        sgm = p.op("scalar", lambda e: e.activation(out=s1s[:, 0:4], in_=s1s[:, 0:4],
                                                    func=AF.Sigmoid), inc=sc)
        p.wt("vector", sc, sgm)
        p.op("vector", lambda e: e.tensor_scalar(out=s2s[:, 0:4], in0=s1s[:, 0:4],
                                                 scalar1=-1.0, scalar2=1.0,
                                                 op0=ALU.mult, op1=ALU.add), inc=ve)
        p.op("vector", lambda e, l3v=l3v: e.tensor_tensor(
            out=tmpb[:, 0:32].rearrange("p (t e) -> p t e", e=8), in0=l3v,
            in1=m2s[:, 0:4].unsqueeze(2).broadcast_to((128, 4, 8)), op=ALU.is_ge), inc=ve)
        p.wt("vector", ve)
        p.op("vector", lambda e: e.tensor_tensor(out=tmpb[:, 0:32], in0=tmpb[:, 0:32],
                                                 in1=tmpa[:, 0:32], op=ALU.subtract), inc=ve)
        p.wt("vector", ve)
        p.op("vector", lambda e: e.tensor_tensor(
            out=tmpa[:, 0:32].rearrange("p (t e) -> p t e", e=8),
            in0=tmpa[:, 0:32].rearrange("p (t e) -> p t e", e=8),
            in1=s1s[:, 0:4].unsqueeze(2).broadcast_to((128, 4, 8)), op=ALU.mult), inc=ve)
        p.op("vector", lambda e: e.tensor_tensor(
            out=tmpb[:, 0:32].rearrange("p (t e) -> p t e", e=8),
            in0=tmpb[:, 0:32].rearrange("p (t e) -> p t e", e=8),
            in1=s2s[:, 0:4].unsqueeze(2).broadcast_to((128, 4, 8)), op=ALU.mult), inc=ve)
        p.wt("vector", ve)
        p.op("vector", lambda e, wecol=wecol: e.tensor_tensor(
            out=wecol, in0=tmpa[:, 0:32], in1=tmpb[:, 0:32], op=ALU.add), inc=ve)
        we_ve = ve.v
        hid_prev = max(hid_prev, we_ve)
        acc_done = 0
        s1_stream = [(ei, ig) for ei in range(8) for ei2 in [0] for ig in range(8)]
        s1_cons = {}
        s1_ld = {}
        s1_stw = [0, 0]
        s2_stw = [0, 0]

        def s1_load(idx):
            ei, ig = s1_stream[idx]
            par = idx % 2
            if tb == 0:
                if idx >= 2:
                    p.wt("gpsimd", te, s1_cons[s1_stream[idx - 2]])
                    p.wt("gpsimd", stw1[par], s1_stw[par])
                else:
                    p.wt("gpsimd", te, te.v)
                p.op("gpsimd", lambda e, ei=ei, ig=ig, par=par: e.dma_start(
                    out=w1b[:, par * 4096:(par + 1) * 4096].rearrange("p (cc i) -> p cc i", i=512),
                    in_=w1_h[ei].rearrange("(cc p) i -> p cc i", p=128)[:, :, ig * 512:(ig + 1) * 512]),
                    inc=ldw1[par], delta=16)
                p.op("gpsimd", lambda e, ei=ei, ig=ig, par=par: e.dma_start(
                    out=w3b[:, par * 4096:(par + 1) * 4096].rearrange("p (cc i) -> p cc i", i=512),
                    in_=w3_h[ei].rearrange("(cc p) i -> p cc i", p=128)[:, :, ig * 512:(ig + 1) * 512]),
                    inc=ldw3[par], delta=16)
                s1_ld[(ei, ig)] = ("g", par, ldw1[par].v, ldw3[par].v)
                p.wt("gpsimd", ldw1[par])
                p.wt("gpsimd", ldw3[par])
                p.op("gpsimd", lambda e, ei=ei, ig=ig, par=par: e.dma_start(
                    out=w1s_d[ei, ig], in_=w1b[:, par * 4096:(par + 1) * 4096]), inc=stw1[par], delta=16)
                p.op("gpsimd", lambda e, ei=ei, ig=ig, par=par: e.dma_start(
                    out=w3s_d[ei, ig], in_=w3b[:, par * 4096:(par + 1) * 4096]), inc=stw1[par], delta=16)
                s1_stw[par] = stw1[par].v
            else:
                if idx >= 2:
                    p.wt("sync", te, s1_cons[s1_stream[idx - 2]])
                else:
                    p.wt("sync", te, te.v)
                p.op("sync", lambda e, ei=ei, ig=ig, par=par: e.dma_start(
                    out=w1b[:, par * 4096:(par + 1) * 4096], in_=w1s_d[ei, ig]),
                    inc=lds1[par], delta=16)
                p.op("sync", lambda e, ei=ei, ig=ig, par=par: e.dma_start(
                    out=w3b[:, par * 4096:(par + 1) * 4096], in_=w3s_d[ei, ig]),
                    inc=lds3[par], delta=16)
                s1_ld[(ei, ig)] = ("s", par, lds1[par].v, lds3[par].v)

        s2_stream = [(ei, oc, grp) for ei in range(8) for oc in range(2) for grp in range(4)]
        s2_cons = {}
        s2_ld = {}

        def s2_load(idx):
            ei, oc, grp = s2_stream[idx]
            par = idx % 2
            if tb == 0:
                if idx >= 2:
                    p.wt("gpsimd", te, s2_cons[s2_stream[idx - 2]])
                    p.wt("gpsimd", stw2[par], s2_stw[par])
                else:
                    p.wt("gpsimd", te, te.v)
                p.op("gpsimd", lambda e, ei=ei, oc=oc, grp=grp, par=par: e.dma_start(
                    out=w2b[:, par * 4096:(par + 1) * 4096].rearrange("p (icg o) -> p icg o", o=512),
                    in_=w2_h[ei].rearrange("(grp icg p) o -> grp p icg o", grp=4, p=128)[grp][:, :, oc * 512:(oc + 1) * 512]),
                    inc=ldw2[par], delta=16)
                s2_ld[(ei, oc, grp)] = ("g", par, ldw2[par].v)
                p.wt("gpsimd", ldw2[par])
                p.op("gpsimd", lambda e, ei=ei, oc=oc, grp=grp, par=par: e.dma_start(
                    out=w2s_d[ei, oc * 4 + grp], in_=w2b[:, par * 4096:(par + 1) * 4096]),
                    inc=stw2[par], delta=16)
                s2_stw[par] = stw2[par].v
            else:
                if idx >= 2:
                    p.wt("sync", te, s2_cons[s2_stream[idx - 2]])
                else:
                    p.wt("sync", te, te.v)
                p.op("sync", lambda e, ei=ei, oc=oc, grp=grp, par=par: e.dma_start(
                    out=w2b[:, par * 4096:(par + 1) * 4096], in_=w2s_d[ei, oc * 4 + grp]),
                    inc=lds2[par], delta=16)
                s2_ld[(ei, oc, grp)] = ("s", par, lds2[par].v)

        if tb == 1:
            p.wt("sync", stw1[0], stw1[0].v)
            p.wt("sync", stw1[1], stw1[1].v)
            p.wt("sync", stw2[0], stw2[0].v)
            p.wt("sync", stw2[1], stw2[1].v)
        s1_load(0)
        s1_load(1)
        s2_load(0)
        hid_prev2 = [hid_prev, hid_prev]
        for ei in range(8):
            for ig in range(8):
                idx = ei * 8 + ig
                qk, wpar, l1v, l3v = s1_ld[(ei, ig)]
                p.wt("tensor", (ldw1 if qk == "g" else lds1)[wpar], l1v)
                p.wt("tensor", (ldw3 if qk == "g" else lds3)[wpar], l3v)
                if ei == 0 and ig == 0:
                    p.wt("tensor", ve, xln_ve)
                for ic4 in range(4):
                    ic = ig * 4 + ic4
                    gpar = ic % 2
                    p.wt("tensor", sc, svu_prev2[gpar] if True else 0)
                    p.wt("tensor", ve, hid_prev)
                    for k in range(8):
                        p.op("tensor", lambda e, k=k, ic4=ic4, wpar=wpar, gpar=gpar: e.matmul(
                            ps_g[:, gpar * 512:(gpar + 1) * 512],
                            w1b[:, wpar * 4096 + k * 512 + ic4 * 128: wpar * 4096 + k * 512 + (ic4 + 1) * 128],
                            xlnT[:, k * 512:(k + 1) * 512], start=(k == 0), stop=(k == 7)),
                            inc=(te if k == 7 else None))
                    g_te = te.v
                    for k in range(8):
                        p.op("tensor", lambda e, k=k, ic4=ic4, wpar=wpar: e.matmul(
                            ps_u[:, :], w3b[:, wpar * 4096 + k * 512 + ic4 * 128: wpar * 4096 + k * 512 + (ic4 + 1) * 128],
                            xlnT[:, k * 512:(k + 1) * 512], start=(k == 0), stop=(k == 7)),
                            inc=(te if k == 7 else None))
                    u_te = te.v
                    s1_cons[(ei, ig)] = te.v
                    p.wt("scalar", te, g_te)
                    p.wt("scalar", ve, hid_prev)
                    svu = p.op("scalar", lambda e, gpar=gpar: e.activation(
                        out=silu_sb[:], in_=ps_g[:, gpar * 512:(gpar + 1) * 512],
                        func=AF.Silu), inc=sc)
                    svu_prev2[gpar] = svu
                    svu_prev = svu
                    p.wt("vector", sc, svu)
                    p.wt("vector", te, u_te)
                    p.op("vector", lambda e, ic=ic: e.tensor_tensor(
                        out=hid[:, ic * 512:(ic + 1) * 512], in0=ps_u[:, :], in1=silu_sb[:],
                        op=ALU.mult), inc=ve)
                    hid_prev = ve.v
                if idx + 2 < 64:
                    s1_load(idx + 2)
            hid_all = ve.v
            oc_evac = 0
            for oc in range(2):
                for grp in range(4):
                    idx2 = ei * 8 + oc * 4 + grp
                    qk2, wpar, l2v = s2_ld[(ei, oc, grp)]
                    p.wt("tensor", (ldw2 if qk2 == "g" else lds2)[wpar], l2v)
                    if oc == 0 and grp == 0:
                        p.wt("tensor", ve, hid_all)
                        p.wt("tensor", ve, acc_done_prev)
                    if oc == 1 and grp == 0:
                        p.wt("tensor", ve, oc_evac)
                    for ts in range(4):
                        for icg in range(8):
                            ic = grp * 8 + icg
                            p.op("tensor", lambda e, ts=ts, ic=ic, icg=icg, wpar=wpar: e.matmul(
                                ps_o4[:, ts * 512:(ts + 1) * 512],
                                hid[:, ic * 512 + ts * 128: ic * 512 + (ts + 1) * 128],
                                w2b[:, wpar * 4096 + icg * 512:wpar * 4096 + (icg + 1) * 512],
                                start=(ic == 0), stop=(ic == 31)),
                                inc=(te if (ts == 3 and icg == 7) else None))
                    s2_cons[(ei, oc, grp)] = te.v
                    if idx2 + 1 < 64:
                        s2_load(idx2 + 1)
                ostage_te = te.v
                p.wt("vector", te, ostage_te)
                if tb > 0 and ei == 0 and oc == 0:
                    p.wt("vector", sty[1 - tbp], prev_tb_st)
                for ts in range(4):
                    col = (tiles[ts]) * 8 + ei
                    if ei == 0:
                        p.op("vector", lambda e, ts=ts, oc=oc, col=col: e.tensor_scalar(
                            out=acc[:, ts * D + oc * 512: ts * D + (oc + 1) * 512],
                            in0=ps_o4[:, ts * 512:(ts + 1) * 512],
                            scalar1=we[:, col:col + 1], scalar2=None, op0=ALU.mult),
                            inc=(ve if ts == 3 else None))
                    else:
                        p.op("vector", lambda e, ts=ts, oc=oc, col=col: e.scalar_tensor_tensor(
                            out=acc[:, ts * D + oc * 512: ts * D + (oc + 1) * 512],
                            in0=ps_o4[:, ts * 512:(ts + 1) * 512],
                            scalar=we[:, col:col + 1],
                            in1=acc[:, ts * D + oc * 512: ts * D + (oc + 1) * 512],
                            op0=ALU.mult, op1=ALU.add),
                            inc=(ve if ts == 3 else None))
                oc_evac = ve.v
            acc_done = ve.v
            acc_done_prev = acc_done
        p.wt("vector", ve, acc_done)
        for ts in range(4):
            t = tiles[ts]
            sl = t % 4
            p.op("vector", lambda e, ts=ts, sl=sl: e.scalar_tensor_tensor(
                out=y_sb[:, ts * D:(ts + 1) * D], in0=acc[:, ts * D:(ts + 1) * D],
                scalar=0.5, in1=x_sb[:, sl * D:(sl + 1) * D], op0=ALU.mult, op1=ALU.add),
                inc=(ve if ts == 3 else None))
        ydone = ve.v
        for t in tiles:
            if t + 4 < NT:
                xfree[t + 4] = ydone
        p.wt("scalar", ve, ydone)
        for ts in range(4):
            t = tiles[ts]
            p.op("scalar", lambda e, t=t, ts=ts: e.dma_start(
                out=y_h[t * 128:(t + 1) * 128, :], in_=y_sb[:, ts * D:(ts + 1) * D]),
                inc=sty[tbp], delta=16)
            st_done[t] = sty[tbp].v
        prev_tb_st = sty[tbp].v
        prev_xlnT_te = te.v
    p.wt("vector", sty[0], sty[0].v)
    p.wt("vector", sty[1], sty[1].v)
    return p



RGN = 2176
NRT_ = 17


def build_l0a():
    p = P()
    xh_h = p.inp("xh", [RGN, D])
    x_h = p.inp("x", [R, D])
    wq1_h = p.inp("wq1", [D, D])
    wq2_h = p.inp("wq2", [D, D])
    wk1_h = p.inp("wk1", [D, 128])
    wk2_h = p.inp("wk2", [D, 128])
    wv_h = p.inp("wv", [D, 128])
    wout_h = p.inp("wout", [D, D])
    bout_h = p.inp("bout", [1, D])
    lvec_h = p.inp("lvec", [64, 4])
    cos_h = p.inp("cosr", [RGN, 32])
    sin_h = p.inp("sinr", [RGN, 32])
    ident_h = p.inp("ident", [128, 128])
    y_h = p.out("y", [R, D])

    xr_sb = p.sbuf("xr_sb", [128, 2 * D], F32)
    x_sb = p.sbuf("x_sb", [128, 2 * D], F32)
    ident = p.sbuf("ident_sb", [128, 128], BF16)
    wq1 = p.sbuf("wq1_sb", [128, 8 * D], BF16)
    wq2 = p.sbuf("wq2_sb", [128, 8 * D], BF16)
    wk1 = p.sbuf("wk1_sb", [128, 8 * 128], BF16)
    wk2 = p.sbuf("wk2_sb", [128, 8 * 128], BF16)
    wv = p.sbuf("wv_sb", [128, 8 * 128], BF16)
    wout = p.sbuf("wout_sb", [128, 8 * D], BF16)
    bout = p.sbuf("bout_sb", [1, D], BF16)
    bout_f = p.sbuf("bout_f", [1, D], F32)
    lvec = p.sbuf("lvec_sb", [64, 4], F32)
    lsc = p.sbuf("lsc_sb", [1, 4], F32)
    nlam_bc = p.sbuf("nlam_bc", [128, 1], F32)
    cos_sb = p.sbuf("cos_sb", [128, NRT_ * 32], F32)
    sin_sb = p.sbuf("sin_sb", [128, NRT_ * 32], F32)
    ones_b1 = p.sbuf("ones_b1", [1, 128], BF16)
    st6 = p.sbuf("st6", [128, 12], F32)
    mvbuf = p.sbuf("mvbuf", [128, 2 * NRT_], F32)
    nmean = p.sbuf("nmean", [128, NRT_], F32)
    rstd = p.sbuf("rstd", [128, NRT_], F32)
    ln_bf = p.sbuf("ln_bf", [128, D], BF16)
    rlnT = p.sbuf("rlnT", [128, 2 * D], BF16)
    q_tm = p.sbuf("q_tm", [128, D], F32)
    q_ro = p.sbuf("q_ro", [128, D], BF16)
    q1f = p.sbuf("q1f", [128, 8 * RGN], BF16)
    q2f = p.sbuf("q2f", [128, 8 * RGN], BF16)
    k1f = p.sbuf("k1f", [128, RGN], BF16)
    k2f = p.sbuf("k2f", [128, RGN], BF16)
    v_aug = p.sbuf("v_aug", [128, NRT_ * 130], BF16)
    p_sb = p.sbuf("p_sb", [128, 2 * 512], BF16)
    pT = p.sbuf("pT", [128, 2 * 512], BF16)
    pc = p.sbuf("pc", [128, 512], BF16)
    nmax = p.sbuf("nmax", [128, 4], F32)
    asum = p.sbuf("asum", [128, 4], F32)
    arec = p.sbuf("arec", [128, 2], F32)
    a_tm = p.sbuf("a_tm", [128, 2 * D], F32)
    a_bf = p.sbuf("a_bf", [128, 2 * D], BF16)
    aT = p.sbuf("aT", [128, 2 * D], BF16)
    y_sb = p.sbuf("y_sb", [128, 2 * D], F32)

    ps_t = p.psum("ps_t", [128, 1024], BF16)
    ps_t2 = p.psum("ps_t2", [128, 1024], BF16)
    ps_a = p.psum("ps_a", [128, 512], F32)
    ps_b = p.psum("ps_b", [128, 512], F32)
    ps_f = p.psum("ps_f", [128, 1024], F32)
    ps_v = p.psum("ps_v", [128, 136], F32)
    ps_s = p.psum("ps_s", [1, 8], F32)

    ldh = p.sem("ldh")
    ldr = [p.sem("ldr0"), p.sem("ldr1")]
    ldx = p.sem("ldx")
    ldg = p.sem("ldg")
    sc = p.sem("sc")
    te = p.sem("te")
    ve = p.sem("ve")
    sty = [p.sem("sty0"), p.sem("sty1")]

    p.op("gpsimd", lambda e: e.dma_start(out=ident[:], in_=ident_h), inc=ldg, delta=16)
    for (dst, srch) in ((wq1, wq1_h), (wq2, wq2_h), (wout, wout_h)):
        p.op("gpsimd", lambda e, dst=dst, srch=srch: e.dma_start(
            out=dst[:].rearrange("p (k c) -> p k c", c=D),
            in_=srch.rearrange("(k p) c -> p k c", p=128)), inc=ldg, delta=16)
    for (dst, srch) in ((wk1, wk1_h), (wk2, wk2_h), (wv, wv_h)):
        p.op("gpsimd", lambda e, dst=dst, srch=srch: e.dma_start(
            out=dst[:].rearrange("p (k c) -> p k c", c=128),
            in_=srch.rearrange("(k p) c -> p k c", p=128)), inc=ldg, delta=16)
    p.op("gpsimd", lambda e: e.dma_start(out=bout[:], in_=bout_h), inc=ldg, delta=16)
    p.op("gpsimd", lambda e: e.dma_start(out=bout_f[:], in_=bout_h), inc=ldg, delta=16)
    wld = ldg.v
    p.op("sync", lambda e: e.dma_start(out=lvec[:], in_=lvec_h), inc=ldh, delta=16)
    p.op("sync", lambda e: e.dma_start(
        out=cos_sb[:].rearrange("p (t c) -> p t c", c=32),
        in_=cos_h.rearrange("(t p) c -> p t c", p=128)), inc=ldh, delta=16)
    p.op("sync", lambda e: e.dma_start(
        out=sin_sb[:].rearrange("p (t c) -> p t c", c=32),
        in_=sin_h.rearrange("(t p) c -> p t c", p=128)), inc=ldh, delta=16)
    tbl_ld = ldh.v
    p.q["tensor"].append(lambda e: e.wait_ge(ldg.h, wld))
    p.q["tensor"].append(lambda e: e.wait_ge(ldh.h, tbl_ld))
    p.q["vector"].append(lambda e: e.wait_ge(ldg.h, wld))
    p.op("vector", lambda e: e.tensor_scalar(out=wq1[:], in0=wq1[:], scalar1=0.125,
                                             scalar2=None, op0=ALU.mult), inc=ve)
    p.op("vector", lambda e: e.tensor_scalar(out=wq2[:], in0=wq2[:], scalar1=0.125,
                                             scalar2=None, op0=ALU.mult), inc=ve)

    p.op("tensor", lambda e: e.matmul(ps_s[0:1, 0:1], lvec[:, 0:1], lvec[:, 1:2],
                                      start=True, stop=True))
    lam_mm = p.op("tensor", lambda e: e.matmul(ps_s[0:1, 4:5], lvec[:, 2:3], lvec[:, 3:4],
                                               start=True, stop=True), inc=te)
    p.wt("scalar", te, lam_mm)
    lam_sc = p.op("scalar", lambda e: e.activation(out=lsc[:, 0:2], in_=ps_s[0:1, 0:8:4],
                                                   func=AF.Exp), inc=sc)
    p.wt("vector", sc, lam_sc)
    p.op("vector", lambda e: e.tensor_tensor(out=lsc[:, 2:3], in0=lsc[:, 0:1],
                                             in1=lsc[:, 1:2], op=ALU.subtract), inc=ve)
    p.wt("vector", ve)
    p.op("vector", lambda e: e.tensor_scalar(out=lsc[:, 2:3], in0=lsc[:, 2:3],
                                             scalar1=0.8, scalar2=0.9, op0=ALU.add, op1=ALU.min), inc=ve)
    p.wt("vector", ve)
    p.op("vector", lambda e: e.tensor_scalar(out=lsc[:, 3:4], in0=lsc[:, 2:3],
                                             scalar1=0.1, scalar2=-1.0, op0=ALU.max, op1=ALU.mult), inc=ve)
    nlam_ve = ve.v
    onesf = p.sbuf("onesf2", [1, 128], F32)
    p.op("gpsimd", lambda e: e.memset(onesf[:], 1.0), inc=ldg, delta=0) if False else None
    gp = p.sem("gp")
    p.op("gpsimd", lambda e: e.memset(onesf[:], 1.0), inc=gp)
    p.op("gpsimd", lambda e: e.memset(ones_b1[:], 1.0), inc=gp)
    p.op("gpsimd", lambda e: e.memset(v_aug[:], 1.0), inc=gp)
    ones_gp = gp.v
    p.wt("tensor", ve, nlam_ve)
    p.wt("tensor", gp, ones_gp)
    nl_mm = p.op("tensor", lambda e: e.matmul(ps_s[0:128, 0:1] if False else ps_v[:, 0:1],
                                              onesf[:, 0:128], lsc[:, 3:4],
                                              start=True, stop=True), inc=te)
    p.wt("vector", te, nl_mm)
    p.op("vector", lambda e: e.tensor_copy(out=nlam_bc[:], in_=ps_v[:, 0:1]), inc=ve)
    nlam_bc_ve = ve.v

    stats_done = [0] * NRT_
    xrld = [0] * NRT_
    for t in range(NRT_):
        sl = t % 2
        if t >= 2:
            p.wt("sync", ve, stats_done[t - 2])
        p.op("sync", lambda e, t=t, sl=sl: e.dma_start(
            out=xr_sb[:, sl * D:(sl + 1) * D], in_=xh_h[t * 128:(t + 1) * 128, :]),
            inc=ldr[sl], delta=16)
        xrld[t] = ldr[sl].v
        p.wt("vector", ldr[sl], xrld[t])
        p.op("vector", lambda e, sl=sl: e.bn_stats(out=st6[:, 0:6], in_=xr_sb[:, sl * D:sl * D + 512]), inc=ve)
        p.wt("vector", ve)
        p.op("vector", lambda e, sl=sl: e.bn_stats(out=st6[:, 6:12], in_=xr_sb[:, sl * D + 512:sl * D + 1024]), inc=ve)
        p.wt("vector", ve)
        p.op("vector", lambda e, t=t: e.bn_aggr(out=mvbuf[:, 2 * t:2 * t + 2], in_=st6[:, 0:12]), inc=ve)
        p.wt("vector", ve)
        stats_done[t] = ve.v
    p.op("vector", lambda e: e.tensor_scalar(out=rstd[:, 0:NRT_], in0=mvbuf[:, 1:2 * NRT_:2],
                                             scalar1=float(EPS), scalar2=None, op0=ALU.add), inc=ve)
    p.wt("scalar", ve)
    p.op("scalar", lambda e: e.sqrt(out=rstd[:, 0:NRT_], in_=rstd[:, 0:NRT_]), inc=sc)
    p.wt("vector", sc)
    p.op("vector", lambda e: e.reciprocal(out=rstd[:, 0:NRT_], in_=rstd[:, 0:NRT_]), inc=ve)
    p.op("vector", lambda e: e.tensor_scalar(out=nmean[:, 0:NRT_], in0=mvbuf[:, 0:2 * NRT_:2],
                                             scalar1=-1.0, scalar2=None, op0=ALU.mult), inc=ve)
    p.wt("vector", ve)
    stats_all = ve.v

    def emit_rope(dst_bf, src_ap, t, nheads, ve, scaled=False):
        cosb = cos_sb[:, 32 * t:32 * (t + 1)].unsqueeze(1).broadcast_to((128, nheads, 32))
        sinb = sin_sb[:, 32 * t:32 * (t + 1)].unsqueeze(1).broadcast_to((128, nheads, 32))
        se = src_ap.rearrange("p (h two c) -> p h two c", two=2, c=32)[:, :, 0, :]
        so = src_ap.rearrange("p (h two c) -> p h two c", two=2, c=32)[:, :, 1, :]
        de = dst_bf.rearrange("p (h two c) -> p h two c", two=2, c=32)[:, :, 0, :]
        do = dst_bf.rearrange("p (h two c) -> p h two c", two=2, c=32)[:, :, 1, :]
        tmp = q_tm
        p.op("vector", lambda e: e.tensor_tensor(out=vtmp2[:].rearrange(
            "p (h c) -> p h c", c=32)[:, 0:nheads, :], in0=se, in1=cosb, op=ALU.mult), inc=ve)
        p.op("vector", lambda e: e.tensor_tensor(out=vtmp3[:].rearrange(
            "p (h c) -> p h c", c=32)[:, 0:nheads, :], in0=so, in1=sinb, op=ALU.mult), inc=ve)
        p.wt("vector", ve)
        p.op("vector", lambda e: e.tensor_tensor(
            out=de, in0=vtmp2[:].rearrange("p (h c) -> p h c", c=32)[:, 0:nheads, :],
            in1=vtmp3[:].rearrange("p (h c) -> p h c", c=32)[:, 0:nheads, :], op=ALU.subtract), inc=ve)
        p.wt("vector", ve)
        p.op("vector", lambda e: e.tensor_tensor(out=vtmp2[:].rearrange(
            "p (h c) -> p h c", c=32)[:, 0:nheads, :], in0=se, in1=sinb, op=ALU.mult), inc=ve)
        p.op("vector", lambda e: e.tensor_tensor(out=vtmp3[:].rearrange(
            "p (h c) -> p h c", c=32)[:, 0:nheads, :], in0=so, in1=cosb, op=ALU.mult), inc=ve)
        p.wt("vector", ve)
        p.op("vector", lambda e: e.tensor_tensor(
            out=do, in0=vtmp2[:].rearrange("p (h c) -> p h c", c=32)[:, 0:nheads, :],
            in1=vtmp3[:].rearrange("p (h c) -> p h c", c=32)[:, 0:nheads, :], op=ALU.add), inc=ve)

    vtmp2 = p.sbuf("vtmp2", [128, 512], F32)
    vtmp3 = p.sbuf("vtmp3", [128, 512], F32)

    norm_done = [0] * NRT_
    tr_done = [0] * NRT_
    lnt_done = [0] * NRT_
    proj_te = [0] * NRT_
    for t in range(NRT_):
        sl = t % 2
        par = t % 2
        p.wt("sync", ve, stats_all if t < 2 else norm_done[t - 2])
        p.op("sync", lambda e, t=t, sl=sl: e.dma_start(
            out=xr_sb[:, sl * D:(sl + 1) * D], in_=xh_h[t * 128:(t + 1) * 128, :]),
            inc=ldr[sl], delta=16)
        xrld[t] = ldr[sl].v
        p.wt("vector", ldr[sl], xrld[t])
        if t >= 1:
            p.wt("vector", te, tr_done[t - 1])
        emit_norm_tile(p, xr_sb[:, sl * D:(sl + 1) * D], ln_bf[:],
                       nmean[:, t:t + 1], rstd[:, t:t + 1], ve)
        norm_done[t] = ve.v
        pst = ps_t if par == 0 else ps_t2
        p.wt("tensor", ve, norm_done[t])
        if t >= 2:
            p.wt("tensor", ve, lnt_done[t - 2])
        for j in range(8):
            p.op("tensor", lambda e, j=j, pst=pst: e.transpose(
                pst[:, j * 128:(j + 1) * 128], ln_bf[:, j * 128:(j + 1) * 128], ident[:]),
                inc=(te if j == 7 else None))
        tr_done[t] = te.v
        p.wt("vector", te, tr_done[t])
        if t >= 2:
            p.wt("vector", te, proj_te[t - 2])
        p.op("vector", lambda e, par=par, pst=pst: e.tensor_copy(
            out=rlnT[:, par * D:(par + 1) * D], in_=pst[:, :]), inc=ve)
        lnt_done[t] = ve.v
        p.wt("tensor", ve, lnt_done[t])

        def proj(dst_f, w_sb, width, do_rope, scaled, t=t, par=par, do_transpose=True):
            nspan = width // 512 if width >= 512 else 1
            spw = 512 if width >= 512 else width
            for spi in range(nspan):
                ps = ps_a if spi % 2 == 0 else ps_b
                p.wt("tensor", ve)
                for k in range(8):
                    p.op("tensor", lambda e, k=k, spi=spi, ps=ps, w_sb=w_sb, spw=spw: e.matmul(
                        ps[:, 0:spw], rlnT[:, par * D + k * 128:par * D + (k + 1) * 128],
                        w_sb[:, k * width + spi * spw: k * width + spi * spw + spw],
                        start=(k == 0), stop=(k == 7)),
                        inc=(te if k == 7 else None))
                mmv = te.v
                p.wt("vector", te, mmv)
                p.op("vector", lambda e, spi=spi, ps=ps, spw=spw: e.tensor_copy(
                    out=q_tm[:, spi * 512:spi * 512 + spw], in_=ps[:, 0:spw]), inc=ve)
            evv = ve.v
            p.wt("vector", ve, evv)
            nh = width // 64
            if do_rope:
                emit_rope(q_ro[:, 0:width], q_tm[:, 0:width], t, nh, ve, scaled)
            else:
                p.op("vector", lambda e, width=width: e.tensor_copy(
                    out=q_ro[:, 0:width], in_=q_tm[:, 0:width]), inc=ve)
            rv = ve.v
            if not do_transpose:
                p.wt("vector", ve, rv)
                p.op("vector", lambda e: dst_f(e, q_ro), inc=ve)
                p.wt("tensor", ve)
                return ve.v
            p.wt("tensor", ve, rv)
            pst2 = ps_t2 if par == 0 else ps_t
            nj = width // 128
            for j in range(nj):
                p.op("tensor", lambda e, j=j, pst2=pst2: e.transpose(
                    pst2[:, j * 128:(j + 1) * 128], q_ro[:, j * 128:(j + 1) * 128], ident[:]),
                    inc=(te if j == nj - 1 else None))
            trv = te.v
            p.wt("vector", te, trv)
            p.op("vector", lambda e, pst2=pst2: dst_f(e, pst2), inc=ve)
            p.wt("tensor", ve)
            return ve.v

        proj(lambda e, ps, t=t: e.tensor_copy(
            out=q1f[:].rearrange("p (g n) -> p g n", n=RGN)[:, :, t * 128:(t + 1) * 128],
            in_=ps[:].rearrange("p (g c) -> p g c", c=128)), wq1, D, True, True)
        proj(lambda e, ps, t=t: e.tensor_copy(
            out=q2f[:].rearrange("p (g n) -> p g n", n=RGN)[:, :, t * 128:(t + 1) * 128],
            in_=ps[:].rearrange("p (g c) -> p g c", c=128)), wq2, D, True, True)
        proj(lambda e, ps, t=t: e.tensor_copy(
            out=k1f[:, t * 128:(t + 1) * 128], in_=ps[:, 0:128]), wk1, 128, True, False)
        proj(lambda e, ps, t=t: e.tensor_copy(
            out=k2f[:, t * 128:(t + 1) * 128], in_=ps[:, 0:128]), wk2, 128, True, False)
        proj(lambda e, ps, t=t: e.tensor_copy(
            out=v_aug[:].rearrange("p (n c) -> p n c", c=130)[:, t, :].rearrange(
                "p (kv c) -> p kv c", c=65)[:, :, 0:64],
            in_=ps[:, 0:128].rearrange("p (kv c) -> p kv c", c=64)[:, 0:2, :]),
            wv, 128, False, False, do_transpose=False)
        proj_te[t] = te.v
    phase1_ve = ve.v
    phase1_te = te.v

    st_done = [0, 0]
    pf_free = phase1_ve
    psb_free = phase1_te
    pt_free = phase1_ve
    pc_free = phase1_te
    pv_free = phase1_ve
    atm_free = sc.v
    aT_free = phase1_te
    exp_prev = sc.v
    xld2 = [0, 0]
    for w in range(16):
        wp = w % 2
        p.wt("sync", sty[0], st_done[0])
        p.wt("sync", sty[1], st_done[1])
        for ti in range(2):
            p.op("sync", lambda e, w=w, ti=ti: e.dma_start(
                out=x_sb[:, ti * D:(ti + 1) * D],
                in_=x_h[(2 * w + ti) * 128:(2 * w + ti + 1) * 128, :]),
                inc=ldx, delta=16)
        xw_ld = ldx.v
        for h in range(16):
            g = h // 2
            rb = (h % 2) * 64
            kv = h % 2
            p.wt("tensor", ve, pf_free)
            p.wt("tensor", sc, exp_prev)
            for b, qf, kf in ((0, q1f, k1f), (1, q2f, k2f)):
                for qh in range(2):
                    p.op("tensor", lambda e, b=b, qh=qh, qf=qf, kf=kf, g=g, rb=rb, kv=kv, w=w: e.matmul(
                        ps_f[:, (b * 2 + qh) * 256:(b * 2 + qh + 1) * 256],
                        qf[rb:rb + 64, g * RGN + 128 * w + qh * 128: g * RGN + 128 * w + (qh + 1) * 128],
                        kf[kv * 64:kv * 64 + 64, 128 * w: 128 * w + 256],
                        start=True, stop=True),
                        inc=(te if (b, qh) == (1, 1) else None))
            sc_te = te.v
            p.wt("vector", te, sc_te)
            p.wt("vector", sc, exp_prev)
            for i in range(4):
                p.op("vector", lambda e, i=i: e.reduce_max(
                    out=nmax[:, i:i + 1], in_=ps_f[:, i * 256:(i + 1) * 256],
                    axis=AX.X, negate=True), inc=(ve if i == 3 else None))
            nm_ve = ve.v
            p.wt("scalar", ve, nm_ve)
            p.wt("scalar", te, psb_free)
            for i in range(4):
                p.op("scalar", lambda e, i=i: e.activation(
                    out=p_sb[:, i * 256:(i + 1) * 256], in_=ps_f[:, i * 256:(i + 1) * 256],
                    func=AF.Exp, bias=nmax[:, i:i + 1], scale=1.0,
                    accum_out=arec[:, 2 + i:3 + i] if False else asum[:, i:i + 1]),
                    inc=(sc if i == 3 else None))
            exp_prev = sc.v
            p.wt("vector", sc, exp_prev)
            p.op("vector", lambda e: e.reciprocal(out=asum[:, 0:1], in_=asum[:, 0:1]))
            p.op("vector", lambda e: e.reciprocal(out=asum[:, 1:2], in_=asum[:, 1:2]))
            p.op("vector", lambda e: e.reciprocal(out=asum[:, 2:3], in_=asum[:, 2:3]))
            p.op("vector", lambda e: e.reciprocal(out=asum[:, 3:4], in_=asum[:, 3:4]), inc=ve)
            p.wt("vector", ve)
            for i in range(4):
                p.op("vector", lambda e, i=i: e.tensor_scalar(
                    out=p_sb[:, i * 256:(i + 1) * 256], in0=p_sb[:, i * 256:(i + 1) * 256],
                    scalar1=asum[:, i:i + 1], scalar2=None, op0=ALU.mult),
                    inc=(ve if i == 3 else None))
            pnorm_ve = ve.v
            p.wt("tensor", ve, pnorm_ve)
            p.wt("tensor", sc, exp_prev)
            p.wt("tensor", ve, pt_free)
            for idx in range(8):
                p.op("tensor", lambda e, idx=idx: e.transpose(
                    ps_t[:, idx * 128:(idx + 1) * 128], p_sb[:, idx * 128:(idx + 1) * 128],
                    ident[:]), inc=(te if idx == 7 else None))
            ptr_te = te.v
            psb_free = ptr_te
            p.wt("vector", te, ptr_te)
            p.wt("vector", te, pc_free)
            p.op("vector", lambda e: e.tensor_copy(
                out=pT[:].rearrange("p (b kh qh c) -> p b kh qh c", b=2, kh=2, qh=2),
                in_=ps_t[:].rearrange("p (b qh kh c) -> p b kh qh c", b=2, qh=2, kh=2)), inc=ve)
            ptv = ve.v
            pt_free = ptv
            p.wt("vector", ve, max(ptv, nlam_bc_ve))
            p.op("vector", lambda e: e.scalar_tensor_tensor(
                out=pc[:], in0=pT[:, 512:1024], scalar=nlam_bc[:, 0:1],
                in1=pT[:, 0:512], op0=ALU.mult, op1=ALU.add), inc=ve)
            pc_ve = ve.v
            p.wt("tensor", ve, pc_ve)
            p.wt("tensor", ve, pv_free)
            for qh in range(2):
                for kh in range(2):
                    p.op("tensor", lambda e, qh=qh, kh=kh, w=w, kv=kv: e.matmul(
                        ps_v[:, qh * 68: qh * 68 + 64],
                        pc[:, kh * 256 + qh * 128: kh * 256 + (qh + 1) * 128],
                        v_aug[:, (w + kh) * 130 + kv * 65: (w + kh) * 130 + kv * 65 + 64],
                        start=(kh == 0), stop=(kh == 1)),
                        inc=(te if (qh, kh) == (1, 1) else None))
            pv_te = te.v
            pc_free = pv_te
            p.wt("vector", te, pv_te)
            if h == 0:
                p.wt("vector", sc, atm_free)
            for qh in range(2):
                p.op("vector", lambda e, qh=qh, h=h: e.tensor_copy(
                    out=a_tm[:, qh * D + h * 64: qh * D + h * 64 + 64],
                    in_=ps_v[:, qh * 68: qh * 68 + 64]),
                    inc=(ve if qh == 1 else None))
            pv_free = ve.v
        heads_ve = ve.v
        p.wt("scalar", ve, heads_ve)
        p.wt("scalar", te, aT_free)
        cast_sc = p.op("scalar", lambda e: e.copy(out=a_bf[:], in_=a_tm[:]), inc=sc)
        atm_free = cast_sc
        p.wt("tensor", sc, cast_sc)
        p.wt("tensor", ve, pt_free)
        for ti in range(2):
            for j in range(8):
                p.op("tensor", lambda e, ti=ti, j=j: e.transpose(
                    ps_t[:, (j if ti == 0 else j) * 128:(j + 1) * 128] if False else
                    (ps_t if ti == 0 else ps_t2)[:, j * 128:(j + 1) * 128],
                    a_bf[:, ti * D + j * 128: ti * D + (j + 1) * 128], ident[:]),
                    inc=(te if j == 7 else None))
        aTr_te = te.v
        psb_free = max(psb_free, aTr_te) if False else psb_free
        p.wt("vector", te, aTr_te)
        p.wt("vector", te, phase1_te if w == 0 else wout_te)
        for ti in range(2):
            pstx = ps_t if ti == 0 else ps_t2
            p.op("vector", lambda e, ti=ti, pstx=pstx: e.tensor_copy(
                out=aT[:].rearrange("p (j two c) -> p j two c", two=2, c=128)[:, :, ti, :],
                in_=pstx[:].rearrange("p (j c) -> p j c", c=128)), inc=(ve if ti == 1 else None))
        aT_ve = ve.v
        pt_free = max(pt_free, aT_ve)
        aT_free = aTr_te
        p.wt("tensor", ve, aT_ve)
        p.wt("tensor", sc, exp_prev)
        p.wt("vector", ldx, xw_ld)
        for ti in range(2):
            p.wt("tensor", ve, pf_free)
            for sp in range(2):
                for j in range(8):
                    p.op("tensor", lambda e, ti=ti, sp=sp, j=j: e.matmul(
                        ps_f[:, sp * 512:(sp + 1) * 512],
                        aT[:, (j * 2 + ti) * 128:(j * 2 + ti + 1) * 128],
                        wout[:, j * D + sp * 512: j * D + (sp + 1) * 512],
                        start=(j == 0), stop=False))
                p.op("tensor", lambda e, sp=sp: e.matmul(
                    ps_f[:, sp * 512:(sp + 1) * 512],
                    ones_b1[:, 0:128], bout[:, sp * 512:(sp + 1) * 512],
                    start=False, stop=True),
                    inc=(te if sp == 1 else None))
            wout_te = te.v
            p.wt("vector", te, wout_te)
            for sp in range(2):
                p.op("vector", lambda e, ti=ti, sp=sp: e.tensor_tensor(
                    out=y_sb[:, ti * D + sp * 512: ti * D + (sp + 1) * 512],
                    in0=ps_f[:, sp * 512:(sp + 1) * 512],
                    in1=x_sb[:, ti * D + sp * 512: ti * D + (sp + 1) * 512], op=ALU.add),
                    inc=(ve if sp == 1 else None))
            pf_free = ve.v
        yv = ve.v
        p.wt("scalar", ve, yv)
        for ti in range(2):
            p.op("scalar", lambda e, w=w, ti=ti: e.dma_start(
                out=y_h[(2 * w + ti) * 128:(2 * w + ti + 1) * 128, :],
                in_=y_sb[:, ti * D:(ti + 1) * D]), inc=sty[wp], delta=16)
        st_done[wp] = sty[wp].v
    p.wt("vector", sty[0], sty[0].v)
    p.wt("vector", sty[1], sty[1].v)
    return p



SEG = 1024
NSEG = 4


def build_l1a():
    p = P()
    x_h = p.inp("x", [R, D])
    xh_h = p.inp("xh", [SEG, D])
    wq_h = p.inp("wq", [D, D])
    wkv_h = p.inp("wkv", [D, D])
    wo_h = p.inp("wo", [D, D])
    wof_h = p.inp("wof", [D, D])
    bo_h = p.inp("bo", [1, D])
    bof_h = p.inp("bof", [1, D])
    cos_h = p.inp("cosr", [SEG + R, 32])
    sin_h = p.inp("sinr", [SEG + R, 32])
    msk0_h = p.inp("msk0", [1, 1])
    tri_h = p.inp("trimask", [128, 128])
    ident_h = p.inp("ident", [128, 128])
    y_h = p.out("y", [R, D])

    NTA = NSEG * 8 + 8

    x_sb = p.sbuf("x_sb", [128, 2 * D], F32)
    ident = p.sbuf("ident_sb", [128, 128], BF16)
    wq = p.sbuf("wq_sb", [128, 8 * D], BF16)
    wkv = p.sbuf("wkv_sb", [128, 8 * D], BF16)
    wo = p.sbuf("wo_sb", [128, 8 * D], BF16)
    wof = p.sbuf("wof_sb", [128, 8 * D], BF16)
    bo = p.sbuf("bo_sb", [1, 2 * D], BF16)
    tri = p.sbuf("tri_sb", [128, 128], F32)
    msk0 = p.sbuf("msk0_sb", [1, 1], F32)
    msk0_bc = p.sbuf("msk0_bc", [128, 1], F32)
    onesf = p.sbuf("onesf_sb", [1, 128], F32)
    ones_b1 = p.sbuf("ones_b1s", [1, 128], BF16)
    cos_sb = p.sbuf("cos_sb", [128, (NTA) * 32], BF16)
    sin_sb = p.sbuf("sin_sb", [128, (NTA) * 32], BF16)
    st6 = p.sbuf("st6", [128, 12], F32)
    mvbuf = p.sbuf("mvbuf", [128, 2 * NTA], F32)
    nmean = p.sbuf("nmean", [128, NTA], F32)
    rstd = p.sbuf("rstd", [128, NTA], F32)
    ln_bf = p.sbuf("ln_bf", [128, D], BF16)
    rlnT = p.sbuf("rlnT", [128, 2 * D], BF16)
    q_tm = p.sbuf("q_tm", [128, D], BF16)
    q_ro = p.sbuf("q_ro", [128, D], BF16)
    vtmp2 = p.sbuf("vtmp2", [128, 512], F32)
    vtmp3 = p.sbuf("vtmp3", [128, 512], F32)
    qf = p.sbuf("qf", [128, 8 * SEG], BF16)
    kvf = p.sbuf("kvf", [128, 2 * 8 * SEG], BF16)
    kvt = p.sbuf("kvt", [128, 2 * 8 * D], BF16)
    p_sb = p.sbuf("p_sb", [128, 2048], BF16)
    pT = p.sbuf("pT", [128, 2048], BF16)
    nmax = p.sbuf("nmax", [128, 2], F32)
    a_tm = p.sbuf("a_tm", [128, 8 * D], BF16)
    aT = p.sbuf("aT", [128, D], BF16)
    y_sb = p.sbuf("y_sb", [128, D], F32)

    ps_t = p.psum("ps_t", [128, 1024], BF16)
    ps_t2 = p.psum("ps_t2", [128, 1024], BF16)
    ps_f = p.psum("ps_f", [128, 2048], F32)
    ps_v = p.psum("ps_v", [128, 128], F32)
    ps_a = ps_f[:, 1024:1536]
    ps_b = ps_f[:, 1536:2048]

    ldh = p.sem("ldh")
    ldr = [p.sem("ldr0"), p.sem("ldr1")]
    ldy = [p.sem("ldy0"), p.sem("ldy1")]
    ldg = p.sem("ldg")
    gp = p.sem("gp")
    sc = p.sem("sc")
    te = p.sem("te")
    ve = p.sem("ve")
    sty = [p.sem("sty0"), p.sem("sty1")]

    p.op("gpsimd", lambda e: e.dma_start(out=ident[:], in_=ident_h), inc=ldg, delta=16)
    for (dst, srch) in ((wq, wq_h), (wkv, wkv_h), (wo, wo_h), (wof, wof_h)):
        p.op("gpsimd", lambda e, dst=dst, srch=srch: e.dma_start(
            out=dst[:].rearrange("p (k c) -> p k c", c=D),
            in_=srch.rearrange("(k p) c -> p k c", p=128)), inc=ldg, delta=16)
    p.op("gpsimd", lambda e: e.dma_start(out=bo[:, 0:D], in_=bo_h), inc=ldg, delta=16)
    p.op("gpsimd", lambda e: e.dma_start(out=bo[:, D:2 * D], in_=bof_h), inc=ldg, delta=16)
    wld = ldg.v
    p.op("gpsimd", lambda e: e.memset(onesf[:], 1.0), inc=gp)
    p.op("gpsimd", lambda e: e.memset(ones_b1[:], 1.0), inc=gp)
    gp_done = gp.v
    p.op("sync", lambda e: e.dma_start(out=tri[:], in_=tri_h), inc=ldh, delta=16)
    p.op("sync", lambda e: e.dma_start(out=msk0[:], in_=msk0_h), inc=ldh, delta=16)
    p.op("gpsimd", lambda e: e.dma_start(
        out=cos_sb[:].rearrange("p (t c) -> p t c", c=32),
        in_=cos_h.rearrange("(t p) c -> p t c", p=128)), inc=ldg, delta=16)
    p.op("gpsimd", lambda e: e.dma_start(
        out=sin_sb[:].rearrange("p (t c) -> p t c", c=32),
        in_=sin_h.rearrange("(t p) c -> p t c", p=128)), inc=ldg, delta=16)
    wld = ldg.v
    tbl_ld = ldh.v
    p.q["tensor"].append(lambda e: e.wait_ge(ldg.h, wld))
    p.q["tensor"].append(lambda e: e.wait_ge(ldh.h, tbl_ld))
    p.q["tensor"].append(lambda e: e.wait_ge(gp.h, gp_done))
    p.q["vector"].append(lambda e: e.wait_ge(ldg.h, wld))
    p.op("vector", lambda e: e.tensor_scalar(out=wq[:], in0=wq[:], scalar1=0.125,
                                             scalar2=None, op0=ALU.mult), inc=ve)

    mb_mm = p.op("tensor", lambda e: e.matmul(ps_v[:, 0:1], onesf[:, 0:128], msk0[:, 0:1],
                                              start=True, stop=True), inc=te)
    p.wt("vector", te, mb_mm)
    p.op("vector", lambda e: e.tensor_copy(out=msk0_bc[:], in_=ps_v[:, 0:1]), inc=ve)
    msk0_ve = ve.v

    def rope_l1(dst_bf, src_ap, ti, nheads):
        cosb = cos_sb[:, 32 * ti:32 * (ti + 1)].unsqueeze(1).broadcast_to((128, nheads, 32))
        sinb = sin_sb[:, 32 * ti:32 * (ti + 1)].unsqueeze(1).broadcast_to((128, nheads, 32))
        se = src_ap.rearrange("p (h two c) -> p h two c", two=2, c=32)[:, :, 0, :]
        so = src_ap.rearrange("p (h two c) -> p h two c", two=2, c=32)[:, :, 1, :]
        de = dst_bf.rearrange("p (h two c) -> p h two c", two=2, c=32)[:, :, 0, :]
        do = dst_bf.rearrange("p (h two c) -> p h two c", two=2, c=32)[:, :, 1, :]
        v2 = vtmp2[:].rearrange("p (h c) -> p h c", c=32)[:, 0:nheads, :]
        v3 = vtmp3[:].rearrange("p (h c) -> p h c", c=32)[:, 0:nheads, :]
        p.op("vector", lambda e: e.tensor_tensor(out=v2, in0=se, in1=cosb, op=ALU.mult), inc=ve)
        p.op("vector", lambda e: e.tensor_tensor(out=v3, in0=so, in1=sinb, op=ALU.mult), inc=ve)
        p.wt("vector", ve)
        p.op("vector", lambda e: e.tensor_tensor(out=de, in0=v2, in1=v3, op=ALU.subtract), inc=ve)
        p.wt("vector", ve)
        p.op("vector", lambda e: e.tensor_tensor(out=v2, in0=so, in1=cosb, op=ALU.mult), inc=ve)
        p.op("vector", lambda e: e.tensor_tensor(out=v3, in0=se, in1=sinb, op=ALU.mult), inc=ve)
        p.wt("vector", ve)
        p.op("vector", lambda e: e.tensor_tensor(out=do, in0=v2, in1=v3, op=ALU.add), inc=ve)

    stats_done = [0] * NTA
    xld = [0] * NTA
    for ti in range(NTA):
        sl = ti % 2
        srch = xh_h if ti < 8 else x_h
        roff = ti * 128 if ti < 8 else (ti - 8) * 128
        if ti >= 2:
            p.wt("sync", ve, stats_done[ti - 2])
        p.op("sync", lambda e, sl=sl, srch=srch, roff=roff: e.dma_start(
            out=x_sb[:, sl * D:(sl + 1) * D], in_=srch[roff:roff + 128, :]),
            inc=ldr[sl], delta=16)
        xld[ti] = ldr[sl].v
        p.wt("vector", ldr[sl], xld[ti])
        p.op("vector", lambda e, sl=sl: e.bn_stats(out=st6[:, 0:6], in_=x_sb[:, sl * D:sl * D + 512]), inc=ve)
        p.wt("vector", ve)
        p.op("vector", lambda e, sl=sl: e.bn_stats(out=st6[:, 6:12], in_=x_sb[:, sl * D + 512:sl * D + 1024]), inc=ve)
        p.wt("vector", ve)
        p.op("vector", lambda e, ti=ti: e.bn_aggr(out=mvbuf[:, 2 * ti:2 * ti + 2], in_=st6[:, 0:12]), inc=ve)
        p.wt("vector", ve)
        stats_done[ti] = ve.v
    p.op("vector", lambda e: e.tensor_scalar(out=rstd[:, 0:NTA], in0=mvbuf[:, 1:2 * NTA:2],
                                             scalar1=float(EPS), scalar2=None, op0=ALU.add), inc=ve)
    p.wt("scalar", ve)
    p.op("scalar", lambda e: e.sqrt(out=rstd[:, 0:NTA], in_=rstd[:, 0:NTA]), inc=sc)
    p.wt("vector", sc)
    p.op("vector", lambda e: e.reciprocal(out=rstd[:, 0:NTA], in_=rstd[:, 0:NTA]), inc=ve)
    p.op("vector", lambda e: e.tensor_scalar(out=nmean[:, 0:NTA], in0=mvbuf[:, 0:2 * NTA:2],
                                             scalar1=-1.0, scalar2=None, op0=ALU.mult), inc=ve)
    p.wt("vector", ve)
    stats_all = ve.v

    norm_prev = [stats_all]
    tr_prev = [0]
    lnt_prev = [0, 0]
    proj_prev = [0, 0]

    def prep_tile(ti, make_q, kv_ring_slot, seg_tok):
        sl = ti % 2
        par = ti % 2
        srch = xh_h if ti < 8 else x_h
        roff = ti * 128 if ti < 8 else (ti - 8) * 128
        p.wt("sync", ve, norm_prev[0])
        p.op("sync", lambda e, sl=sl, srch=srch, roff=roff: e.dma_start(
            out=x_sb[:, sl * D:(sl + 1) * D], in_=srch[roff:roff + 128, :]),
            inc=ldr[sl], delta=16)
        xldv = ldr[sl].v
        p.wt("vector", ldr[sl], xldv)
        p.wt("vector", te, tr_prev[0])
        emit_norm_tile(p, x_sb[:, sl * D:(sl + 1) * D], ln_bf[:],
                       nmean[:, ti:ti + 1], rstd[:, ti:ti + 1], ve)
        norm_prev[0] = ve.v
        pst = ps_t if par == 0 else ps_t2
        p.wt("tensor", ve, norm_prev[0])
        p.wt("tensor", ve, lnt_prev[par])
        for j in range(8):
            p.op("tensor", lambda e, j=j, pst=pst: e.transpose(
                pst[:, j * 128:(j + 1) * 128], ln_bf[:, j * 128:(j + 1) * 128], ident[:]),
                inc=(te if j == 7 else None))
        tr_prev[0] = te.v
        p.wt("vector", te, tr_prev[0])
        p.wt("vector", te, proj_prev[par])
        p.op("vector", lambda e, par=par, pst=pst: e.tensor_copy(
            out=rlnT[:, par * D:(par + 1) * D], in_=pst[:, :]), inc=ve)
        lnt_prev[par] = ve.v
        p.wt("tensor", ve, lnt_prev[par])

        def one_proj(w_sb, dst_fm, dst_tm):
            for spi in range(2):
                ps = ps_a if spi == 0 else ps_b
                p.wt("tensor", ve)
                p.wt("tensor", sc)
                for k in range(8):
                    p.op("tensor", lambda e, k=k, spi=spi, ps=ps, w_sb=w_sb, par=par: e.matmul(
                        ps, rlnT[:, par * D + k * 128:par * D + (k + 1) * 128],
                        w_sb[:, k * D + spi * 512: k * D + spi * 512 + 512],
                        start=(k == 0), stop=(k == 7)),
                        inc=(te if k == 7 else None))
                mmv = te.v
                p.wt("vector", te, mmv)
                p.op("vector", lambda e, spi=spi, ps=ps: e.tensor_copy(
                    out=q_tm[:, spi * 512:(spi + 1) * 512], in_=ps), inc=ve)
            p.wt("vector", ve)
            rope_l1(q_ro[:], q_tm[:], ti, 16)
            rv = ve.v
            if dst_tm is not None:
                p.wt("vector", ve, rv)
                p.op("vector", lambda e, dst_tm=dst_tm: e.tensor_copy(out=dst_tm, in_=q_ro[:]), inc=ve)
            rv2 = ve.v
            p.wt("tensor", ve, rv2)
            pst2 = ps_t2 if par == 0 else ps_t
            for j in range(8):
                p.op("tensor", lambda e, j=j, pst2=pst2: e.transpose(
                    pst2[:, j * 128:(j + 1) * 128], q_ro[:, j * 128:(j + 1) * 128], ident[:]),
                    inc=(te if j == 7 else None))
            trv = te.v
            p.wt("vector", te, trv)
            p.op("vector", lambda e, pst2=pst2, dst_fm=dst_fm: e.tensor_copy(
                out=dst_fm, in_=pst2[:].rearrange("p (g c) -> p g c", c=128)), inc=ve)
            p.wt("tensor", ve)

        sp = kv_ring_slot
        one_proj(wkv,
                 kvf[:].rearrange("p (s g n) -> p s g n", s=2, n=SEG)[:, sp, :, seg_tok * 128:(seg_tok + 1) * 128],
                 kvt[:, sp * 8 * D + seg_tok * D: sp * 8 * D + (seg_tok + 1) * D])
        if make_q:
            one_proj(wq,
                     qf[:].rearrange("p (g n) -> p g n", n=SEG)[:, :, seg_tok * 128:(seg_tok + 1) * 128],
                     None)
        proj_prev[par] = te.v

    for ti in range(8):
        prep_tile(ti, False, 0, ti)
    st_done = [0, 0]
    pf_free = ve.v
    psb_free = te.v
    pt_free = ve.v
    atm_te_prev = 0
    exp_prev = sc.v
    aT_te = te.v
    for s in range(NSEG):
        cur = (s + 1) % 2
        prv = s % 2
        for i in range(8):
            prep_tile(8 + s * 8 + i, True, cur, i)
        prep_done_te = te.v
        prep_done_ve = ve.v
        for h in range(16):
            g = h // 2
            rb = (h % 2) * 64
            for qi in range(8):
                nk = 8 + qi + 1
                nkc = nk * 128
                pieces = []
                nsp = (nkc + 511) // 512
                for spi in range(nsp):
                    w0 = spi * 512
                    wlen = min(512, nkc - w0)
                    if w0 < SEG:
                        l1 = min(wlen, SEG - w0)
                        pieces.append((prv, w0, l1, w0))
                    if w0 + wlen > SEG:
                        o2 = max(w0, SEG)
                        pieces.append((cur, o2 - SEG, w0 + wlen - o2, o2))
                p.wt("tensor", ve, pf_free)
                p.wt("tensor", sc, exp_prev)
                for pi, (slot, koff, klen, dsto) in enumerate(pieces):
                    p.op("tensor", lambda e, slot=slot, koff=koff, klen=klen, dsto=dsto,
                         g=g, rb=rb, qi=qi: e.matmul(
                        ps_f[:, dsto: dsto + klen],
                        qf[rb:rb + 64, g * SEG + qi * 128: g * SEG + (qi + 1) * 128],
                        kvf[:].rearrange("p (s g n) -> p s g n", s=2, n=SEG)[rb:rb + 64, slot, g, koff:koff + klen],
                        start=True, stop=True),
                        inc=(te if pi == len(pieces) - 1 else None))
                sc_te = te.v
                p.wt("vector", te, sc_te)
                p.op("vector", lambda e, qi=qi, nkc=nkc: e.tensor_tensor(
                    out=ps_f[:, nkc - 128:nkc], in0=ps_f[:, nkc - 128:nkc],
                    in1=tri[:], op=ALU.add), inc=ve)
                if s == 0:
                    p.op("vector", lambda e: e.tensor_scalar(
                        out=ps_f[:, 0:SEG], in0=ps_f[:, 0:SEG],
                        scalar1=msk0_bc[:, 0:1], scalar2=None, op0=ALU.add), inc=ve)
                p.wt("vector", ve)
                p.op("vector", lambda e, nkc=nkc: e.reduce_max(
                    out=nmax[:, 0:1], in_=ps_f[:, 0:nkc], axis=AX.X, negate=True), inc=ve)
                nm_ve = ve.v
                p.wt("scalar", ve, nm_ve)
                p.op("scalar", lambda e, nkc=nkc: e.activation(
                    out=p_sb[:, 0:nkc], in_=ps_f[:, 0:nkc], func=AF.Exp,
                    bias=nmax[:, 0:1], scale=1.0, accum_out=nmax[:, 1:2]), inc=sc)
                exp_prev = sc.v
                p.wt("vector", sc, exp_prev)
                p.op("vector", lambda e: e.reciprocal(out=nmax[:, 1:2], in_=nmax[:, 1:2]), inc=ve)
                p.wt("vector", ve)
                p.op("vector", lambda e, nkc=nkc: e.tensor_scalar(
                    out=p_sb[:, 0:nkc], in0=p_sb[:, 0:nkc], scalar1=nmax[:, 1:2],
                    scalar2=None, op0=ALU.mult), inc=ve)
                pn_ve = ve.v
                pf_free = pn_ve
                p.wt("tensor", ve, pn_ve)
                p.wt("tensor", ve, pt_free)
                for j in range(nk):
                    pstx = ps_t if j < 8 else ps_t2
                    jj = j % 8
                    p.op("tensor", lambda e, j=j, jj=jj, pstx=pstx: e.transpose(
                        pstx[:, jj * 128:(jj + 1) * 128], p_sb[:, j * 128:(j + 1) * 128],
                        ident[:]), inc=(te if j == nk - 1 else None))
                ptr_te = te.v
                psb_free = ptr_te
                p.wt("vector", te, ptr_te)
                p.op("vector", lambda e, nk=nk: e.tensor_copy(
                    out=pT[:, 0:min(nk, 8) * 128], in_=ps_t[:, 0:min(nk, 8) * 128]), inc=ve)
                if nk > 8:
                    p.op("vector", lambda e, nk=nk: e.tensor_copy(
                        out=pT[:, 1024:nk * 128], in_=ps_t2[:, 0:(nk - 8) * 128]), inc=ve)
                pt_ve = ve.v
                pt_free = pt_ve
                p.wt("tensor", ve, pt_ve)
                for j in range(nk):
                    slot = prv if j < 8 else cur
                    i2 = j if j < 8 else j - 8
                    p.op("tensor", lambda e, j=j, slot=slot, i2=i2, h=h, nk=nk: e.matmul(
                        ps_v[:, 0:64], pT[:, j * 128:(j + 1) * 128],
                        kvt[:, slot * 8 * D + i2 * D + h * 64: slot * 8 * D + i2 * D + h * 64 + 64],
                        start=(j == 0), stop=(j == nk - 1)),
                        inc=(te if j == nk - 1 else None))
                pv_te = te.v
                p.wt("vector", te, pv_te)
                if h == 0 and qi == 0:
                    p.wt("vector", te, atm_te_prev)
                p.op("vector", lambda e, qi=qi, h=h: e.tensor_copy(
                    out=a_tm[:, qi * D + h * 64: qi * D + h * 64 + 64],
                    in_=ps_v[:, 0:64]), inc=ve)
        heads_ve = ve.v
        wsel = wof if s == 0 else wo
        boff = D if s == 0 else 0
        for i in range(8):
            p.wt("tensor", ve, heads_ve)
            p.wt("tensor", ve, pt_free)
            for j in range(8):
                p.op("tensor", lambda e, j=j, i=i: e.transpose(
                    ps_t[:, j * 128:(j + 1) * 128], a_tm[:, i * D + j * 128: i * D + (j + 1) * 128],
                    ident[:]), inc=(te if j == 7 else None))
            aTr = te.v
            aT_te = aTr
            p.wt("vector", te, aTr)
            aTe_prev = p.op("vector", lambda e: e.tensor_copy(out=aT[:], in_=ps_t[:, :]), inc=ve)
            pt_free = ve.v
            p.wt("tensor", ve, pt_free)
            ypar = 0
            p.wt("sync", sty[0], st_done[0])
            p.op("sync", lambda e, s=s, i=i: e.dma_start(
                out=y_sb[:, 0:D],
                in_=x_h[(s * 8 + i) * 128:(s * 8 + i + 1) * 128, :]), inc=ldy[0], delta=16)
            yld = ldy[0].v
            for spi in range(2):
                ps = ps_a if spi == 0 else ps_b
                p.wt("tensor", ve)
                p.wt("tensor", sc)
                for j in range(8):
                    p.op("tensor", lambda e, j=j, spi=spi, ps=ps, wsel=wsel: e.matmul(
                        ps, aT[:, j * 128:(j + 1) * 128],
                        wsel[:, j * D + spi * 512: j * D + spi * 512 + 512],
                        start=(j == 0), stop=False))
                p.op("tensor", lambda e, spi=spi, ps=ps, boff=boff: e.matmul(
                    ps, ones_b1[:, 0:128], bo[:, boff + spi * 512: boff + spi * 512 + 512],
                    start=False, stop=True), inc=te)
                wmv = te.v
                p.wt("vector", te, wmv)
                p.wt("vector", ldy[ypar], yld)
                p.op("vector", lambda e, spi=spi, ps=ps: e.tensor_tensor(
                    out=y_sb[:, spi * 512:(spi + 1) * 512],
                    in0=ps, in1=y_sb[:, spi * 512:(spi + 1) * 512],
                    op=ALU.add), inc=ve)
            yv = ve.v
            p.wt("scalar", ve, yv)
            p.op("scalar", lambda e, s=s, i=i: e.dma_start(
                out=y_h[(s * 8 + i) * 128:(s * 8 + i + 1) * 128, :],
                in_=y_sb[:, 0:D]), inc=sty[0], delta=16)
            st_done[0] = sty[0].v
        atm_te_prev = te.v
    p.wt("vector", sty[0], sty[0].v)
    p.wt("vector", sty[1], sty[1].v)
    return p



def build_l2a():
    p = P()
    x_h = p.inp("x", [R, D])
    xkv_h = p.inp("xkv", [256, D])
    xf_h = p.inp("xfirst", [R, D])
    wq_h = p.inp("wq", [D, D])
    wk_h = p.inp("wk", [D, D])
    wv_h = p.inp("wv", [D, D])
    wgq_h = p.inp("wgq", [D, 512])
    wgk_h = p.inp("wgk", [D, 512])
    wgv_h = p.inp("wgv", [D, 512])
    row_h = p.inp("r_out_w", [512, D])
    wo_h = p.inp("w_out", [D, D])
    glwb_h = p.inp("glwb", [D, 16])
    grwb_h = p.inp("grwb", [D, 16])
    tria_h = p.inp("tri_a", [128, 128])
    trib_h = p.inp("tri_b", [128, 128])
    mfirst_h = p.inp("mfirst", [1, 1])
    lmask_h = p.inp("lmask", [1, 32 * 64])
    ident_h = p.inp("ident", [128, 128])
    y_h = p.out("y", [R, D])

    NTL = 2 + 32 + 32

    x_sb = p.sbuf("x_sb", [128, 2 * D], F32)
    ident = p.sbuf("ident_sb", [128, 128], BF16)
    wq = p.sbuf("wq_sb", [128, 8 * D], BF16)
    wk = p.sbuf("wk_sb", [128, 8 * D], BF16)
    wv = p.sbuf("wv_sb", [128, 8 * D], BF16)
    wo = p.sbuf("wo_sb", [128, 8 * D], BF16)
    wgq = p.sbuf("wgq_sb", [128, 8 * 512], BF16)
    wgk = p.sbuf("wgk_sb", [128, 8 * 512], BF16)
    wgv = p.sbuf("wgv_sb", [128, 8 * 512], BF16)
    row_w = p.sbuf("row_sb", [128, 4 * D], BF16)
    glwb = p.sbuf("glwb_sb", [128, 8 * 32], BF16)
    tria = p.sbuf("tria_sb", [128, 128], F32)
    trib = p.sbuf("trib_sb", [128, 128], F32)
    mfirst = p.sbuf("mfirst_sb", [1, 1], F32)
    mf_bc = p.sbuf("mf_bc", [128, 1], F32)
    lmask = p.sbuf("lmask_sb", [1, 32 * 64], BF16)
    onesf = p.sbuf("onesf_sb", [1, 128], F32)
    ones_b1 = p.sbuf("ones_b1s", [1, 128], BF16)
    odiv = p.sbuf("odiv", [128, 1], BF16)
    st6 = p.sbuf("st6", [128, 12], F32)
    mvbuf = p.sbuf("mvbuf", [128, 2 * NTL], F32)
    nmean = p.sbuf("nmean", [128, NTL], F32)
    rstd = p.sbuf("rstd", [128, NTL], F32)
    ln_bf = p.sbuf("ln_bf", [128, D], BF16)
    rlnT = p.sbuf("rlnT", [128, 2 * D], BF16)
    pooledT = p.sbuf("pooledT", [128, 8 * 64], BF16)
    gkf = p.sbuf("gkf", [128, 4 * 64], BF16)
    gvl = p.sbuf("gvl", [64, 512], BF16)
    kf = p.sbuf("kf", [128, 4 * 8 * 128], BF16)
    vt = p.sbuf("vt", [128, 4 * D], BF16)
    qf = p.sbuf("qf", [128, 8 * 128], BF16)
    gqf = p.sbuf("gqf", [128, 4 * 128], BF16)
    gat = p.sbuf("gat", [128, 32], F32)
    q_tm = p.sbuf("q_tm", [128, D], BF16)
    p_sb = p.sbuf("p_sb", [128, 384], BF16)
    pT = p.sbuf("pT", [128, 384], BF16)
    lp = p.sbuf("lp", [128, 64], BF16)
    lpT = p.sbuf("lpT", [64, 128], BF16)
    nmax = p.sbuf("nmax", [128, 2], F32)
    loc = p.sbuf("loc", [128, D], F32)
    rem = p.sbuf("rem", [128, 512], F32)
    remT = p.sbuf("remT", [128, 512], BF16)
    remh = p.sbuf("remh", [128, D], F32)
    comb = p.sbuf("comb", [128, D], F32)
    comb_bf = p.sbuf("comb_bf", [128, D], BF16)
    combT = p.sbuf("combT", [128, D], BF16)
    y_sb = p.sbuf("y_sb", [128, D], F32)

    ps_t = p.psum("ps_t", [128, 1024], BF16)
    ps_t2 = p.psum("ps_t2", [128, 1024], BF16)
    ps_a = p.psum("ps_a", [128, 512], F32)
    ps_b = p.psum("ps_b", [128, 512], F32)
    ps_s = p.psum("ps_s", [128, 512], F32)
    ps_v = p.psum("ps_v", [128, 128], F32)

    ldh = p.sem("ldh")
    ldr = [p.sem("ldr0"), p.sem("ldr1")]
    ldy = [p.sem("ldy0")]
    ldg = p.sem("ldg")
    gp = p.sem("gp")
    sc = p.sem("sc")
    te = p.sem("te")
    ve = p.sem("ve")
    sty = [p.sem("sty0")]

    p.op("gpsimd", lambda e: e.dma_start(out=ident[:], in_=ident_h), inc=ldg, delta=16)
    for (dst, srch, cw) in ((wq, wq_h, D), (wk, wk_h, D), (wv, wv_h, D), (wo, wo_h, D),
                            (wgq, wgq_h, 512), (wgk, wgk_h, 512), (wgv, wgv_h, 512)):
        p.op("gpsimd", lambda e, dst=dst, srch=srch, cw=cw: e.dma_start(
            out=dst[:].rearrange("p (k c) -> p k c", c=cw),
            in_=srch.rearrange("(k p) c -> p k c", p=128)), inc=ldg, delta=16)
    p.op("gpsimd", lambda e: e.dma_start(
        out=row_w[:].rearrange("p (k c) -> p k c", c=D),
        in_=row_h.rearrange("(k p) c -> p k c", p=128)), inc=ldg, delta=16)
    p.op("gpsimd", lambda e: e.dma_start(
        out=glwb[:].rearrange("p (k c) -> p k c", c=32)[:, :, 0:16],
        in_=glwb_h.rearrange("(k p) c -> p k c", p=128)), inc=ldg, delta=16)
    p.op("gpsimd", lambda e: e.dma_start(
        out=glwb[:].rearrange("p (k c) -> p k c", c=32)[:, :, 16:32],
        in_=grwb_h.rearrange("(k p) c -> p k c", p=128)), inc=ldg, delta=16)
    p.op("gpsimd", lambda e: e.dma_start(out=lmask[:], in_=lmask_h), inc=ldg, delta=16)
    wld = ldg.v
    p.op("gpsimd", lambda e: e.memset(onesf[:], 1.0), inc=gp)
    p.op("gpsimd", lambda e: e.memset(ones_b1[:], 1.0), inc=gp)
    p.op("gpsimd", lambda e: e.memset(odiv[:], 1.0 / 128.0), inc=gp)
    gp_done = gp.v
    p.op("sync", lambda e: e.dma_start(out=tria[:], in_=tria_h), inc=ldh, delta=16)
    p.op("sync", lambda e: e.dma_start(out=trib[:], in_=trib_h), inc=ldh, delta=16)
    p.op("sync", lambda e: e.dma_start(out=mfirst[:], in_=mfirst_h), inc=ldh, delta=16)
    tbl_ld = ldh.v
    p.q["tensor"].append(lambda e: e.wait_ge(ldg.h, wld))
    p.q["tensor"].append(lambda e: e.wait_ge(ldh.h, tbl_ld))
    p.q["tensor"].append(lambda e: e.wait_ge(gp.h, gp_done))
    p.q["vector"].append(lambda e: e.wait_ge(ldg.h, wld))
    p.op("vector", lambda e: e.tensor_scalar(out=wq[:], in0=wq[:], scalar1=0.125,
                                             scalar2=None, op0=ALU.mult), inc=ve)
    p.op("vector", lambda e: e.tensor_scalar(out=wgq[:], in0=wgq[:], scalar1=0.125,
                                             scalar2=None, op0=ALU.mult), inc=ve)
    p.op("vector", lambda e: e.tensor_scalar(out=glwb[:], in0=glwb[:], scalar1=8.0,
                                             scalar2=None, op0=ALU.mult), inc=ve)

    mb = p.op("tensor", lambda e: e.matmul(ps_v[:, 0:1], onesf[:, 0:128], mfirst[:, 0:1],
                                           start=True, stop=True), inc=te)
    p.wt("vector", te, mb)
    p.op("vector", lambda e: e.tensor_copy(out=mf_bc[:], in_=ps_v[:, 0:1]), inc=ve)
    mf_ve = ve.v

    def tile_src(ti):
        if ti < 2:
            return xkv_h, ti * 128
        if ti < 34:
            return xf_h, (ti - 2) * 128
        return x_h, (ti - 34) * 128

    stats_done = [0] * NTL
    for ti in range(NTL):
        sl = ti % 2
        srch, roff = tile_src(ti)
        if ti >= 2:
            p.wt("sync", ve, stats_done[ti - 2])
        p.op("sync", lambda e, sl=sl, srch=srch, roff=roff: e.dma_start(
            out=x_sb[:, sl * D:(sl + 1) * D], in_=srch[roff:roff + 128, :]),
            inc=ldr[sl], delta=16)
        xldv = ldr[sl].v
        p.wt("vector", ldr[sl], xldv)
        p.op("vector", lambda e, sl=sl: e.bn_stats(out=st6[:, 0:6], in_=x_sb[:, sl * D:sl * D + 512]), inc=ve)
        p.wt("vector", ve)
        p.op("vector", lambda e, sl=sl: e.bn_stats(out=st6[:, 6:12], in_=x_sb[:, sl * D + 512:sl * D + 1024]), inc=ve)
        p.wt("vector", ve)
        p.op("vector", lambda e, ti=ti: e.bn_aggr(out=mvbuf[:, 2 * ti:2 * ti + 2], in_=st6[:, 0:12]), inc=ve)
        p.wt("vector", ve)
        stats_done[ti] = ve.v
    p.op("vector", lambda e: e.tensor_scalar(out=rstd[:, 0:NTL], in0=mvbuf[:, 1:2 * NTL:2],
                                             scalar1=float(EPS), scalar2=None, op0=ALU.add), inc=ve)
    p.wt("scalar", ve)
    p.op("scalar", lambda e: e.sqrt(out=rstd[:, 0:NTL], in_=rstd[:, 0:NTL]), inc=sc)
    p.wt("vector", sc)
    p.op("vector", lambda e: e.reciprocal(out=rstd[:, 0:NTL], in_=rstd[:, 0:NTL]), inc=ve)
    p.op("vector", lambda e: e.tensor_scalar(out=nmean[:, 0:NTL], in0=mvbuf[:, 0:2 * NTL:2],
                                             scalar1=-1.0, scalar2=None, op0=ALU.mult), inc=ve)
    p.wt("vector", ve)
    stats_all = ve.v

    norm_prev = [stats_all]
    tr_prev = [0]
    lnt_prev = [0, 0]
    proj_prev = [0, 0]
    pool_done = [0]

    def load_norm(ti):
        sl = ti % 2
        srch, roff = tile_src(ti)
        p.wt("sync", ve, norm_prev[0])
        p.op("sync", lambda e, sl=sl, srch=srch, roff=roff: e.dma_start(
            out=x_sb[:, sl * D:(sl + 1) * D], in_=srch[roff:roff + 128, :]),
            inc=ldr[sl], delta=16)
        xldv = ldr[sl].v
        p.wt("vector", ldr[sl], xldv)
        p.wt("vector", te, tr_prev[0])
        emit_norm_tile(p, x_sb[:, sl * D:(sl + 1) * D], ln_bf[:],
                       nmean[:, ti:ti + 1], rstd[:, ti:ti + 1], ve)
        norm_prev[0] = ve.v

    def emit_pool(lat_col):
        p.wt("tensor", ve, norm_prev[0])
        p.wt("tensor", ve, pool_done[0])
        for cc in range(8):
            p.op("tensor", lambda e, cc=cc: e.matmul(
                ps_v[:, cc * 8:cc * 8 + 1], ln_bf[:, cc * 128:(cc + 1) * 128],
                odiv[:, 0:1], start=True, stop=True),
                inc=(te if cc == 7 else None))
        pl_te = te.v
        p.wt("vector", te, pl_te)
        p.op("vector", lambda e, lat_col=lat_col: e.tensor_copy(
            out=pooledT[:].rearrange("p (cc n) -> p cc n", n=64)[:, :, lat_col:lat_col + 1],
            in_=ps_v[:, 0:64:8].rearrange("p (cc one) -> p cc one", one=1)), inc=ve)
        pool_done[0] = ve.v

    for i in range(32):
        load_norm(2 + i)
        emit_pool(i)
    for i in range(32):
        load_norm(34 + i)
        emit_pool(32 + i)
    poolT_ve = ve.v

    p.wt("tensor", ve, poolT_ve)
    for dc in range(4):
        p.wt("tensor", ve)
        for cc in range(8):
            p.op("tensor", lambda e, dc=dc, cc=cc: e.matmul(
                ps_a[:, 0:64] if False else ps_s[:, 0:64],
                wgk[:, cc * 512 + dc * 128: cc * 512 + (dc + 1) * 128],
                pooledT[:, cc * 64:(cc + 1) * 64], start=(cc == 0), stop=(cc == 7)),
                inc=(te if cc == 7 else None))
        gk_te = te.v
        p.wt("vector", te, gk_te)
        p.op("vector", lambda e, dc=dc: e.tensor_copy(
            out=gkf[:, dc * 64:(dc + 1) * 64], in_=ps_s[:, 0:64]), inc=ve)
    for cc in range(8):
        p.op("tensor", lambda e, cc=cc: e.matmul(
            ps_a[0:64, :], pooledT[:, cc * 64:(cc + 1) * 64],
            wgv[:, cc * 512:(cc + 1) * 512], start=(cc == 0), stop=(cc == 7)),
            inc=(te if cc == 7 else None))
    gv_te = te.v
    p.wt("vector", te, gv_te)
    p.op("vector", lambda e: e.tensor_copy(out=gvl[:], in_=ps_a[0:64, :]), inc=ve)
    lat_ve = ve.v

    def project(w_sb, width, dst_fm_fn, dst_tm_ap, par):
        nspan = (width + 511) // 512
        for spi in range(nspan):
            spw = min(512, width - spi * 512)
            ps = ps_a if spi % 2 == 0 else ps_b
            p.wt("tensor", ve)
            for k in range(8):
                p.op("tensor", lambda e, k=k, spi=spi, ps=ps, w_sb=w_sb, par=par,
                     spw=spw, width=width: e.matmul(
                    ps[:, 0:spw], rlnT[:, par * D + k * 128:par * D + (k + 1) * 128],
                    w_sb[:, k * width + spi * 512: k * width + spi * 512 + spw],
                    start=(k == 0), stop=(k == 7)),
                    inc=(te if k == 7 else None))
            mmv = te.v
            p.wt("vector", te, mmv)
            p.op("vector", lambda e, spi=spi, ps=ps, spw=spw: e.tensor_copy(
                out=q_tm[:, spi * 512:spi * 512 + spw], in_=ps[:, 0:spw]), inc=ve)
        ev = ve.v
        if dst_tm_ap is not None:
            p.wt("vector", ve, ev)
            p.op("vector", lambda e, dst_tm_ap=dst_tm_ap, width=width: e.tensor_copy(
                out=dst_tm_ap, in_=q_tm[:, 0:width]), inc=ve)
        if dst_fm_fn is not None:
            p.wt("tensor", ve, ve.v)
            pst2 = ps_t2 if par == 0 else ps_t
            nj = width // 128
            for j in range(nj):
                p.op("tensor", lambda e, j=j, pst2=pst2: e.transpose(
                    pst2[:, j * 128:(j + 1) * 128], q_tm[:, j * 128:(j + 1) * 128], ident[:]),
                    inc=(te if j == nj - 1 else None))
            trv = te.v
            p.wt("vector", te, trv)
            p.op("vector", lambda e, pst2=pst2: dst_fm_fn(e, pst2), inc=ve)
            p.wt("tensor", ve)

    def prep_kv(ti, slot):
        par = ti % 2
        load_norm(ti)
        pst = ps_t if par == 0 else ps_t2
        p.wt("tensor", ve, norm_prev[0])
        p.wt("tensor", ve, lnt_prev[par])
        for j in range(8):
            p.op("tensor", lambda e, j=j, pst=pst: e.transpose(
                pst[:, j * 128:(j + 1) * 128], ln_bf[:, j * 128:(j + 1) * 128], ident[:]),
                inc=(te if j == 7 else None))
        tr_prev[0] = te.v
        p.wt("vector", te, tr_prev[0])
        p.wt("vector", te, proj_prev[par])
        p.op("vector", lambda e, par=par, pst=pst: e.tensor_copy(
            out=rlnT[:, par * D:(par + 1) * D], in_=pst[:, :]), inc=ve)
        lnt_prev[par] = ve.v
        p.wt("tensor", ve, lnt_prev[par])
        project(wk, D, lambda e, ps, slot=slot: e.tensor_copy(
            out=kf[:].rearrange("p (s g c) -> p s g c", s=4, c=128)[:, slot, :, :],
            in_=ps[:].rearrange("p (g c) -> p g c", c=128)), None, par)
        project(wv, D, None, vt[:, slot * D:(slot + 1) * D], par)
        proj_prev[par] = te.v

    prep_kv(0, 2)
    prep_kv(1, 3)

    st_done = [0]
    pf_free = ve.v
    exp_prev = sc.v
    psb_te = te.v
    pt_free = ve.v
    comb_te_prev = te.v
    for gt in range(32):
        ti = 34 + gt
        par = ti % 2
        slot = gt % 4
        prep_kv(ti, slot)
        project(wq, D, lambda e, ps: e.tensor_copy(
            out=qf[:].rearrange("p (g c) -> p g c", c=128),
            in_=ps[:].rearrange("p (g c) -> p g c", c=128)), q_tm[:, 0:D] if False else None, par)
        project(wgq, 512, lambda e, ps: e.tensor_copy(
            out=gqf[:].rearrange("p (g c) -> p g c", c=128),
            in_=ps[:, 0:512].rearrange("p (g c) -> p g c", c=128)), None, par)
        proj_prev[par] = te.v
        qf_ve = ve.v
        p.wt("tensor", ve, qf_ve)
        for g in range(8):
            p.op("tensor", lambda e, g=g: e.matmul(
                ps_v[:, 0:32], qf[:, g * 128:(g + 1) * 128], glwb[:, g * 32:(g + 1) * 32],
                start=(g == 0), stop=(g == 7)), inc=(te if g == 7 else None))
        gat_te = te.v
        p.wt("scalar", te, gat_te)
        p.wt("scalar", ve)
        gat_sc = p.op("scalar", lambda e: e.activation(out=gat[:], in_=ps_v[:, 0:32],
                                                       func=AF.Sigmoid), inc=sc)
        exp_prev = gat_sc
        p.wt("vector", sc, gat_sc)
        for h in range(16):
            g = h // 2
            rb = (h % 2) * 64
            ks = [(gt - 2) % 4 + (4 if gt < 2 else 0), (gt - 1) % 4 + (4 if gt < 1 else 0), slot]
            s0 = (gt - 2) + 2
            ksl = [((gt - 2) % 4) if gt >= 2 else (2 + gt), ((gt - 1) % 4) if gt >= 1 else (3 if gt == 0 else None), slot]
            ksl[1] = ((gt - 1) % 4) if gt >= 1 else 3
            p.wt("tensor", ve, pf_free)
            p.wt("tensor", sc, exp_prev)
            for j, sl2 in enumerate(ksl):
                p.op("tensor", lambda e, j=j, sl2=sl2, g=g, rb=rb: e.matmul(
                    ps_s[:, j * 128:(j + 1) * 128],
                    qf[rb:rb + 64, g * 128:(g + 1) * 128],
                    kf[:].rearrange("p (s g c) -> p s g c", s=4, c=128)[rb:rb + 64, sl2, g, :],
                    start=True, stop=True), inc=(te if j == 2 else None))
            sc_te = te.v
            p.wt("vector", te, sc_te)
            p.op("vector", lambda e: e.tensor_tensor(out=ps_s[:, 0:128], in0=ps_s[:, 0:128],
                                                     in1=tria[:], op=ALU.add), inc=ve)
            p.op("vector", lambda e: e.tensor_tensor(out=ps_s[:, 256:384], in0=ps_s[:, 256:384],
                                                     in1=trib[:], op=ALU.add), inc=ve)
            if gt < 2:
                nneg = 2 - gt
                p.wt("vector", ve)
                p.op("vector", lambda e, nneg=nneg: e.tensor_scalar(
                    out=ps_s[:, 0:nneg * 128], in0=ps_s[:, 0:nneg * 128],
                    scalar1=mf_bc[:, 0:1], scalar2=None, op0=ALU.add), inc=ve)
            p.wt("vector", ve)
            p.op("vector", lambda e: e.reduce_max(out=nmax[:, 0:1], in_=ps_s[:, 0:384],
                                                  axis=AX.X, negate=True), inc=ve)
            nm_ve = ve.v
            p.wt("scalar", ve, nm_ve)
            p.wt("scalar", te, psb_te)
            ex = p.op("scalar", lambda e: e.activation(out=p_sb[:], in_=ps_s[:, 0:384],
                                                       func=AF.Exp, bias=nmax[:, 0:1],
                                                       scale=1.0, accum_out=nmax[:, 1:2]), inc=sc)
            exp_prev = ex
            p.wt("vector", sc, ex)
            p.op("vector", lambda e: e.reciprocal(out=nmax[:, 1:2], in_=nmax[:, 1:2]), inc=ve)
            p.wt("vector", ve)
            p.op("vector", lambda e: e.tensor_scalar(out=p_sb[:], in0=p_sb[:],
                                                     scalar1=nmax[:, 1:2], scalar2=None,
                                                     op0=ALU.mult), inc=ve)
            pn_ve = ve.v
            pf_free = pn_ve
            p.wt("tensor", ve, pn_ve)
            p.wt("tensor", ve, pt_free)
            for j in range(3):
                p.op("tensor", lambda e, j=j: e.transpose(
                    ps_t[:, j * 128:(j + 1) * 128], p_sb[:, j * 128:(j + 1) * 128], ident[:]),
                    inc=(te if j == 2 else None))
            ptr = te.v
            psb_te = ptr
            p.wt("vector", te, ptr)
            p.op("vector", lambda e: e.tensor_copy(out=pT[:], in_=ps_t[:, 0:384]), inc=ve)
            pt_ve = ve.v
            pt_free = pt_ve
            p.wt("tensor", ve, pt_ve)
            for j, sl2 in enumerate(ksl):
                p.op("tensor", lambda e, j=j, sl2=sl2, h=h: e.matmul(
                    ps_v[:, 64:128], pT[:, j * 128:(j + 1) * 128],
                    vt[:, sl2 * D + h * 64: sl2 * D + h * 64 + 64],
                    start=(j == 0), stop=(j == 2)), inc=(te if j == 2 else None))
            pv_te = te.v
            p.wt("vector", te, pv_te)
            p.op("vector", lambda e, h=h: e.tensor_copy(
                out=loc[:, h * 64:(h + 1) * 64], in_=ps_v[:, 64:128]), inc=ve)
        loc_ve = ve.v
        for lh in range(8):
            dc = lh // 2
            rb = (lh % 2) * 64
            p.wt("tensor", ve, max(pf_free, lat_ve))
            p.wt("tensor", sc, exp_prev)
            p.op("tensor", lambda e, dc=dc, rb=rb: e.matmul(
                ps_s[:, 0:64], gqf[rb:rb + 64, dc * 128:(dc + 1) * 128],
                gkf[rb:rb + 64, dc * 64 * 0 + 0:64] if False else gkf[rb:rb + 64, dc * 64:(dc + 1) * 64],
                start=True, stop=False))
            p.op("tensor", lambda e, gt=gt: e.matmul(
                ps_s[:, 0:64], ones_b1[:, 0:128], lmask[0:1, gt * 64:(gt + 1) * 64],
                start=False, stop=True), inc=te)
            lsc_te = te.v
            p.wt("vector", te, lsc_te)
            p.op("vector", lambda e: e.reduce_max(out=nmax[:, 0:1], in_=ps_s[:, 0:64],
                                                  axis=AX.X, negate=True), inc=ve)
            nm2 = ve.v
            p.wt("scalar", ve, nm2)
            ex2 = p.op("scalar", lambda e: e.activation(out=lp[:], in_=ps_s[:, 0:64],
                                                        func=AF.Exp, bias=nmax[:, 0:1],
                                                        scale=1.0, accum_out=nmax[:, 1:2]), inc=sc)
            exp_prev = ex2
            p.wt("vector", sc, ex2)
            p.op("vector", lambda e: e.reciprocal(out=nmax[:, 1:2], in_=nmax[:, 1:2]), inc=ve)
            p.wt("vector", ve)
            p.op("vector", lambda e: e.tensor_scalar(out=lp[:], in0=lp[:], scalar1=nmax[:, 1:2],
                                                     scalar2=None, op0=ALU.mult), inc=ve)
            lpn = ve.v
            pf_free = lpn
            p.wt("tensor", ve, lpn)
            p.wt("tensor", ve, pt_free)
            tr2 = p.op("tensor", lambda e: e.transpose(ps_t[0:64, 0:128], lp[:], ident[:]), inc=te)
            psb_te = tr2
            p.wt("vector", te, tr2)
            p.op("vector", lambda e: e.tensor_copy(out=lpT[:], in_=ps_t[0:64, 0:128]), inc=ve)
            lpt_ve = ve.v
            pt_free = lpt_ve
            p.wt("tensor", ve, lpt_ve)
            rpv = p.op("tensor", lambda e, lh=lh: e.matmul(
                ps_v[:, 64:128], lpT[:, 0:128], gvl[:, lh * 64:(lh + 1) * 64],
                start=True, stop=True), inc=te)
            p.wt("vector", te, rpv)
            p.op("vector", lambda e, lh=lh: e.tensor_copy(
                out=rem[:, lh * 64:(lh + 1) * 64], in_=ps_v[:, 64:128]), inc=ve)
        rem_ve = ve.v
        p.wt("vector", ve, rem_ve)
        p.op("vector", lambda e: e.tensor_copy(out=remT[:] if False else comb_bf[:, 0:512], in_=rem[:]), inc=ve)
        rc_ve = ve.v
        p.wt("tensor", ve, rc_ve)
        p.wt("tensor", ve, pt_free)
        for j in range(4):
            p.op("tensor", lambda e, j=j: e.transpose(
                ps_t[:, j * 128:(j + 1) * 128], comb_bf[:, j * 128:(j + 1) * 128], ident[:]),
                inc=(te if j == 3 else None))
        rT_te = te.v
        psb_te = rT_te
        p.wt("vector", te, rT_te)
        p.op("vector", lambda e: e.tensor_copy(out=remT[:], in_=ps_t[:, 0:512]), inc=ve)
        rT_ve = ve.v
        pt_free = rT_ve
        p.wt("tensor", ve, rT_ve)
        for spi in range(2):
            ps = ps_a if spi == 0 else ps_b
            p.wt("tensor", ve)
            for j in range(4):
                p.op("tensor", lambda e, j=j, spi=spi, ps=ps: e.matmul(
                    ps[:, :], remT[:, j * 128:(j + 1) * 128],
                    row_w[:, j * D + spi * 512: j * D + spi * 512 + 512],
                    start=(j == 0), stop=(j == 3)), inc=(te if j == 3 else None))
            rh_te = te.v
            p.wt("vector", te, rh_te)
            p.op("vector", lambda e, spi=spi, ps=ps: e.tensor_copy(
                out=remh[:, spi * 512:(spi + 1) * 512], in_=ps[:, :]), inc=ve)
        remh_ve = ve.v
        p.wt("vector", ve, remh_ve)
        p.op("vector", lambda e: e.tensor_tensor(
            out=comb[:].rearrange("p (h c) -> p h c", c=64),
            in0=loc[:].rearrange("p (h c) -> p h c", c=64),
            in1=gat[:, 0:16].unsqueeze(2).broadcast_to((128, 16, 64)), op=ALU.mult), inc=ve)
        p.op("vector", lambda e: e.tensor_tensor(
            out=remh[:].rearrange("p (h c) -> p h c", c=64),
            in0=remh[:].rearrange("p (h c) -> p h c", c=64),
            in1=gat[:, 16:32].unsqueeze(2).broadcast_to((128, 16, 64)), op=ALU.mult), inc=ve)
        p.wt("vector", ve)
        p.op("vector", lambda e: e.tensor_tensor(out=comb_bf[:], in0=comb[:], in1=remh[:],
                                                 op=ALU.add), inc=ve)
        comb_ve = ve.v
        p.wt("tensor", ve, comb_ve)
        p.wt("tensor", ve, pt_free)
        for j in range(8):
            p.op("tensor", lambda e, j=j: e.transpose(
                ps_t[:, j * 128:(j + 1) * 128], comb_bf[:, j * 128:(j + 1) * 128], ident[:]),
                inc=(te if j == 7 else None))
        cT_te = te.v
        psb_te = cT_te
        p.wt("vector", te, cT_te)
        p.op("vector", lambda e: e.tensor_copy(out=combT[:], in_=ps_t[:, :]), inc=ve)
        cT_ve = ve.v
        pt_free = cT_ve
        p.wt("sync", sty[0], st_done[0])
        p.op("sync", lambda e, gt=gt: e.dma_start(
            out=y_sb[:], in_=x_h[gt * 128:(gt + 1) * 128, :]), inc=ldy[0], delta=16)
        yld = ldy[0].v
        p.wt("tensor", ve, cT_ve)
        for spi in range(2):
            ps = ps_a if spi == 0 else ps_b
            p.wt("tensor", ve)
            for j in range(8):
                p.op("tensor", lambda e, j=j, spi=spi, ps=ps: e.matmul(
                    ps[:, :], combT[:, j * 128:(j + 1) * 128],
                    wo[:, j * D + spi * 512: j * D + spi * 512 + 512],
                    start=(j == 0), stop=(j == 7)), inc=(te if j == 7 else None))
            om_te = te.v
            p.wt("vector", te, om_te)
            p.wt("vector", ldy[0], yld)
            p.op("vector", lambda e, spi=spi, ps=ps: e.tensor_tensor(
                out=y_sb[:, spi * 512:(spi + 1) * 512], in0=ps[:, :],
                in1=y_sb[:, spi * 512:(spi + 1) * 512], op=ALU.add), inc=ve)
        yv = ve.v
        p.wt("scalar", ve, yv)
        p.op("scalar", lambda e, gt=gt: e.dma_start(
            out=y_h[gt * 128:(gt + 1) * 128, :], in_=y_sb[:]), inc=sty[0], delta=16)
        st_done[0] = sty[0].v
    p.wt("vector", sty[0], sty[0].v)
    return p



RL3 = 8192
NTL3 = RL3 // 128


L3A_STAGE = 0


def build_l3a():
    p = P()
    x_h = p.inp("x", [RL3, D])
    wkt_h = p.inp("wkt", [D, 512])
    wvt_h = p.inp("wvt", [D, 512])
    wrt_h = p.inp("wrt", [D, 512])
    wot_h = p.inp("wot", [512, D])
    dec_h = p.inp("decay2", [128, 4])
    tf_h = p.inp("tf2", [128, 4])
    maskT_h = p.inp("maskT", [16, 16])
    ident_h = p.inp("ident", [128, 128])
    y_h = p.out("ypart", [RL3, D])

    x_sb = p.sbuf("x_sb", [128, 2 * D], F32)
    ident = p.sbuf("ident_sb", [128, 128], BF16)
    wkt = p.sbuf("wkt_sb", [128, 8 * 512], BF16)
    wvt = p.sbuf("wvt_sb", [128, 8 * 512], BF16)
    wrt = p.sbuf("wrt_sb", [128, 8 * 512], BF16)
    wot = p.sbuf("wot_sb", [64, 8 * D], BF16)
    dec = p.sbuf("dec_sb", [128, 4], F32)
    tf = p.sbuf("tf_sb", [128, 4], F32)
    dinv = p.sbuf("dinv_sb", [128, 4], F32)
    dsq = p.sbuf("dsq_sb", [128, 4], F32)
    tpw = p.sbuf("tpw", [128, 4 * 16], F32)
    tpu = p.sbuf("tpu", [128, 4 * 16], BF16)
    tpuf = p.sbuf("tpuf", [128, 4 * 16], F32)
    tp1 = p.sbuf("tp1", [128, 4 * 16], F32)
    tpm = p.sbuf("tpm", [128, 4 * 16], F32)
    tdc8 = p.sbuf("tdc8", [64, 8], F32)
    maskT = p.sbuf("maskT_sb", [16, 16], F32)
    st6 = p.sbuf("st6", [128, 12], F32)
    mvbuf = p.sbuf("mvbuf", [128, 2 * NTL3], F32)
    nmean = p.sbuf("nmean", [128, NTL3], F32)
    rstd = p.sbuf("rstd", [128, NTL3], F32)
    ln_bf = p.sbuf("ln_bf", [128, D], BF16)
    rlnT = p.sbuf("rlnT", [128, 2 * D], BF16)
    k_tm = p.sbuf("k_tm", [128, 512], BF16)
    v_tm = p.sbuf("v_tm", [128, 512], BF16)
    r_tm = p.sbuf("r_tm", [128, 512], BF16)
    kf = p.sbuf("kf", [128, 4 * 128], BF16)
    vf = p.sbuf("vf", [128, 4 * 128], BF16)
    rf = p.sbuf("rf", [128, 4 * 128], BF16)
    wtl = p.sbuf("wtl", [64, 8 * 128], BF16)
    utl = p.sbuf("utl", [64, 8 * 128], BF16)
    rtl = p.sbuf("rtl", [64, 8 * 128], BF16)
    ktl = p.sbuf("ktl", [128, 4 * 128], BF16)
    vsub = p.sbuf("vsub", [16, 2 * 512], BF16)
    ktsub = p.sbuf("ktsub", [16, 2 * 512], BF16)
    a_sb = p.sbuf("a_sb", [16, 8 * 16], BF16)
    s_st = p.sbuf("s_st", [64, 512], F32)
    s_bf = p.sbuf("s_bf", [64, 2 * 512], BF16)
    s_tmp = p.sbuf("s_tmp", [64, 512], F32)
    y_fmt = p.sbuf("y_fmt", [64, 8 * 128], BF16)
    y_sb = p.sbuf("y_sb", [128, D], F32)

    ps_t = p.psum("ps_t", [128, 1024], BF16)
    ps_t2 = p.psum("ps_t2", [128, 512], BF16)
    ps_a = p.psum("ps_a", [128, 512], F32)
    ps_b = p.psum("ps_b", [128, 512], F32)
    psA = p.psum("psA", [16, 128], F32)
    psVT = p.psum("psVT", [16, 1024], BF16)
    psY = p.psum("psY", [64, 128], F32)
    psM = p.psum("psM", [64, 512], F32)

    ldh = p.sem("ldh")
    ldr = [p.sem("ldr0"), p.sem("ldr1")]
    ldg = p.sem("ldg")
    gp = p.sem("gp")
    sc = p.sem("sc")
    te = p.sem("te")
    ve = p.sem("ve")
    sty = [p.sem("sty0")]

    p.op("gpsimd", lambda e: e.dma_start(out=ident[:], in_=ident_h), inc=ldg, delta=16)
    for (dst, srch) in ((wkt, wkt_h), (wvt, wvt_h), (wrt, wrt_h)):
        p.op("gpsimd", lambda e, dst=dst, srch=srch: e.dma_start(
            out=dst[:].rearrange("p (k c) -> p k c", c=512),
            in_=srch.rearrange("(k p) c -> p k c", p=128)), inc=ldg, delta=16)
    p.op("gpsimd", lambda e: e.dma_start(
        out=wot[:].rearrange("p (k c) -> p k c", c=D),
        in_=wot_h.rearrange("(k p) c -> p k c", p=64)), inc=ldg, delta=16)
    wld = ldg.v
    p.op("sync", lambda e: e.dma_start(out=dec[:], in_=dec_h), inc=ldh, delta=16)
    p.op("sync", lambda e: e.dma_start(out=tf[:], in_=tf_h), inc=ldh, delta=16)
    p.op("sync", lambda e: e.dma_start(out=maskT[:], in_=maskT_h), inc=ldh, delta=16)
    tbl_ld = ldh.v
    p.op("gpsimd", lambda e: e.memset(s_st[:], 0.0), inc=gp)
    p.op("gpsimd", lambda e: e.memset(s_bf[:], 0.0), inc=gp)
    p.op("gpsimd", lambda e: e.memset(y_fmt[:], 0.0), inc=gp)
    gp_done = gp.v
    p.q["tensor"].append(lambda e: e.wait_ge(ldg.h, wld))
    p.q["tensor"].append(lambda e: e.wait_ge(ldh.h, tbl_ld))
    p.q["vector"].append(lambda e: e.wait_ge(ldh.h, tbl_ld))
    p.q["vector"].append(lambda e: e.wait_ge(gp.h, gp_done))
    p.q["scalar"].append(lambda e: e.wait_ge(ldh.h, tbl_ld))

    p.op("vector", lambda e: e.reciprocal(out=dinv[:], in_=dec[:]), inc=ve)
    p.op("scalar", lambda e: e.sqrt(out=dsq[:], in_=dec[:]), inc=sc)
    p.wt("vector", sc)
    p.wt("vector", ve)
    p.op("vector", lambda e: e.tensor_copy(out=tpw[:, 15:64:16], in_=dsq[:]), inc=ve)
    for it in range(7):
        p.wt("vector", ve)
        p.op("vector", lambda e: e.tensor_tensor(out=tpw[:, 15:64:16], in0=tpw[:, 15:64:16],
                                                 in1=dec[:], op=ALU.mult), inc=ve)
    for j in range(14, -1, -1):
        p.wt("vector", ve)
        p.op("vector", lambda e, j=j: e.tensor_tensor(
            out=tpw[:, j:64:16], in0=tpw[:, j + 1:64:16], in1=dinv[:], op=ALU.mult), inc=ve)
    p.op("vector", lambda e: e.tensor_copy(out=tp1[:, 0:64:16], in_=dec[:]), inc=ve)
    for j in range(1, 16):
        p.wt("vector", ve)
        p.op("vector", lambda e, j=j: e.tensor_tensor(
            out=tp1[:, j:64:16], in0=tp1[:, j - 1:64:16], in1=dec[:], op=ALU.mult), inc=ve)
    p.op("vector", lambda e: e.tensor_copy(out=tpm[:, 15:64:16], in_=tf[:]), inc=ve)
    for j in range(14, -1, -1):
        p.wt("vector", ve)
        p.op("vector", lambda e, j=j: e.tensor_tensor(
            out=tpm[:, j:64:16], in0=tpm[:, j + 1:64:16], in1=dec[:], op=ALU.mult), inc=ve)
    p.wt("vector", ve)
    p.op("vector", lambda e: e.tensor_tensor(
        out=tpuf[:].rearrange("p (pp j) -> p pp j", j=16),
        in0=tpm[:].rearrange("p (pp j) -> p pp j", j=16),
        in1=tpw[:, 0:64:16].unsqueeze(2).broadcast_to((128, 4, 16)), op=ALU.mult), inc=ve)
    p.wt("vector", ve)
    p.op("vector", lambda e: e.tensor_copy(out=tpu[:], in_=tpuf[:]), inc=ve)
    p.wt("vector", ve)
    p.op("vector", lambda e: e.tensor_tensor(out=dsq[:, 0:4], in0=tp1[:, 15:64:16],
                                             in1=dec[:], op=ALU.mult), inc=ve)
    p.wt("vector", ve)
    for h in range(8):
        pp, rb = h // 2, (h % 2) * 64
        p.op("vector", lambda e, h=h, pp=pp, rb=rb: e.tensor_copy(
            out=tdc8[:, h:h + 1], in_=dsq[rb:rb + 64, pp:pp + 1]),
            inc=(ve if h == 7 else None))
    tab_ve = ve.v

    stats_done = [0] * NTL3
    for ti in range(NTL3):
        sl = ti % 2
        if ti >= 2:
            p.wt("sync", ve, stats_done[ti - 2])
        p.op("sync", lambda e, ti=ti, sl=sl: e.dma_start(
            out=x_sb[:, sl * D:(sl + 1) * D], in_=x_h[ti * 128:(ti + 1) * 128, :]),
            inc=ldr[sl], delta=16)
        xldv = ldr[sl].v
        p.wt("vector", ldr[sl], xldv)
        p.op("vector", lambda e, sl=sl: e.bn_stats(out=st6[:, 0:6], in_=x_sb[:, sl * D:sl * D + 512]), inc=ve)
        p.wt("vector", ve)
        p.op("vector", lambda e, sl=sl: e.bn_stats(out=st6[:, 6:12], in_=x_sb[:, sl * D + 512:sl * D + 1024]), inc=ve)
        p.wt("vector", ve)
        p.op("vector", lambda e, ti=ti: e.bn_aggr(out=mvbuf[:, 2 * ti:2 * ti + 2], in_=st6[:, 0:12]), inc=ve)
        p.wt("vector", ve)
        stats_done[ti] = ve.v
    p.op("vector", lambda e: e.tensor_scalar(out=rstd[:, 0:NTL3], in0=mvbuf[:, 1:2 * NTL3:2],
                                             scalar1=float(EPS), scalar2=None, op0=ALU.add), inc=ve)
    p.wt("scalar", ve)
    p.op("scalar", lambda e: e.sqrt(out=rstd[:, 0:NTL3], in_=rstd[:, 0:NTL3]), inc=sc)
    p.wt("vector", sc)
    p.op("vector", lambda e: e.reciprocal(out=rstd[:, 0:NTL3], in_=rstd[:, 0:NTL3]), inc=ve)
    p.op("vector", lambda e: e.tensor_scalar(out=nmean[:, 0:NTL3], in0=mvbuf[:, 0:2 * NTL3:2],
                                             scalar1=-1.0, scalar2=None, op0=ALU.mult), inc=ve)
    p.wt("vector", ve)
    stats_all = ve.v

    norm_prev = stats_all
    tr_prev = 0
    lnt_prev = [0, 0]
    proj_prev = [0, 0]
    sig_prev = sc.v
    ktmf = [0, 0, 0]
    fm_free = 0
    st_done = [0]
    y_done = 0
    subs_free = 0
    mask_prev = 0
    state_prev = 0
    for ti in range(NTL3):
        sl = ti % 2
        par = ti % 2
        p.wt("sync", ve, norm_prev)
        p.op("sync", lambda e, ti=ti, sl=sl: e.dma_start(
            out=x_sb[:, sl * D:(sl + 1) * D], in_=x_h[ti * 128:(ti + 1) * 128, :]),
            inc=ldr[sl], delta=16)
        xldv = ldr[sl].v
        p.wt("vector", ldr[sl], xldv)
        p.wt("vector", te, tr_prev)
        emit_norm_tile(p, x_sb[:, sl * D:(sl + 1) * D], ln_bf[:],
                       nmean[:, ti:ti + 1], rstd[:, ti:ti + 1], ve)
        norm_prev = ve.v
        pst = ps_t if par == 0 else ps_t2
        p.wt("tensor", ve, norm_prev)
        p.wt("tensor", ve, lnt_prev[par])
        for j in range(8):
            p.op("tensor", lambda e, j=j: e.transpose(
                ps_t[:, j * 128:(j + 1) * 128], ln_bf[:, j * 128:(j + 1) * 128], ident[:]),
                inc=(te if j == 7 else None))
        tr_prev = te.v
        p.wt("vector", te, tr_prev)
        p.wt("vector", te, proj_prev[par])
        p.op("vector", lambda e, par=par: e.tensor_copy(
            out=rlnT[:, par * D:(par + 1) * D], in_=ps_t[:, :]), inc=ve)
        lnt_prev[par] = ve.v
        p.wt("tensor", ve, lnt_prev[par])
        for (w_sb, dst, use_sig, psx) in ((wkt, k_tm, False, ps_a), (wvt, v_tm, False, ps_b),
                                          (wrt, r_tm, True, ps_a)):
            p.wt("tensor", ve)
            p.wt("tensor", sc, sig_prev)
            for k in range(8):
                p.op("tensor", lambda e, k=k, w_sb=w_sb, par=par, psx=psx: e.matmul(
                    psx[:, :], rlnT[:, par * D + k * 128:par * D + (k + 1) * 128],
                    w_sb[:, k * 512:(k + 1) * 512], start=(k == 0), stop=(k == 7)),
                    inc=(te if k == 7 else None))
            mmv = te.v
            if use_sig:
                p.wt("scalar", te, mmv)
                p.wt("scalar", ve)
                sig_prev = p.op("scalar", lambda e, dst=dst, psx=psx: e.activation(
                    out=dst[:], in_=psx[:, :], func=AF.Sigmoid), inc=sc)
            else:
                p.wt("vector", te, mmv)
                p.op("vector", lambda e, dst=dst, psx=psx: e.tensor_copy(out=dst[:], in_=psx[:, :]), inc=ve)
        kv_ve = ve.v
        p.wt("tensor", ve, kv_ve)
        p.wt("tensor", sc, sig_prev)
        p.wt("tensor", ve, fm_free)
        for (src, dstf) in ((k_tm, kf), (v_tm, vf), (r_tm, rf)):
            for j in range(4):
                p.op("tensor", lambda e, j=j, src=src: e.transpose(
                    ps_t2[:, j * 128:(j + 1) * 128], src[:, j * 128:(j + 1) * 128], ident[:]),
                    inc=(te if j == 3 else None))
            trv = te.v
            p.wt("vector", te, trv)
            p.op("vector", lambda e, dstf=dstf: e.tensor_copy(out=dstf[:], in_=ps_t2[:, :]), inc=ve)
            p.wt("tensor", ve)
        fm_ve = ve.v
        p.wt("vector", ve, fm_ve)
        p.wt("vector", te)
        for hh in range(2):
            rb = hh * 64
            for (dstt, srcf, tb4) in ((wtl, rf, tpw), (utl, kf, tpuf)):
                p.op("vector", lambda e, hh=hh, rb=rb, dstt=dstt, srcf=srcf, tb4=tb4: e.tensor_tensor(
                    out=dstt[:].rearrange("p (pp two c) -> p pp two c", two=2, c=128)[:, :, hh, :]
                        .rearrange("p pp (cc j) -> p pp cc j", j=16),
                    in0=srcf[rb:rb + 64, :].rearrange("p (pp c) -> p pp c", c=128)
                        .rearrange("p pp (cc j) -> p pp cc j", j=16),
                    in1=tb4[rb:rb + 64, :].rearrange("p (pp j) -> p pp j", j=16)
                        .unsqueeze(2).broadcast_to((64, 4, 8, 16)),
                    op=ALU.mult), inc=ve)
            p.op("vector", lambda e, hh=hh, rb=rb: e.tensor_tensor(
                out=rtl[:].rearrange("p (pp two c) -> p pp two c", two=2, c=128)[:, :, hh, :]
                    .rearrange("p pp (cc j) -> p pp cc j", j=16),
                in0=rf[rb:rb + 64, :].rearrange("p (pp c) -> p pp c", c=128)
                    .rearrange("p pp (cc j) -> p pp cc j", j=16),
                in1=tp1[rb:rb + 64, :].rearrange("p (pp j) -> p pp j", j=16)
                    .unsqueeze(2).broadcast_to((64, 4, 8, 16)),
                op=ALU.mult), inc=ve)
        p.op("vector", lambda e: e.tensor_tensor(
            out=ktl[:].rearrange("p (pp cc j) -> p pp cc j", cc=8, j=16),
            in0=kf[:].rearrange("p (pp cc j) -> p pp cc j", cc=8, j=16),
            in1=tpm[:].rearrange("p (pp j) -> p pp j", j=16)
                .unsqueeze(2).broadcast_to((128, 4, 8, 16)),
            op=ALU.mult), inc=ve)
        tl_ve = ve.v
        p.wt("tensor", ve, tl_ve)
        p.wt("tensor", ve, subs_free)
        for pp in range(4):
            p.op("tensor", lambda e, pp=pp: e.transpose(
                psVT[0:16, pp * 128:(pp + 1) * 128],
                vf[:, pp * 128: pp * 128 + 16], ident[:]))
            p.op("tensor", lambda e, pp=pp: e.transpose(
                psVT[0:16, 512 + pp * 128: 512 + (pp + 1) * 128],
                ktl[:, pp * 128: pp * 128 + 16], ident[:]),
                inc=(te if pp == 3 else None))
        ph1_te = te.v
        p.wt("vector", te, ph1_te)
        p.op("vector", lambda e: e.tensor_copy(out=vsub[:, 0:512], in_=psVT[0:16, 0:512]), inc=ve)
        p.op("vector", lambda e: e.tensor_copy(out=ktsub[:, 0:512], in_=psVT[0:16, 512:1024]), inc=ve)
        vs_evac = ve.v
        subs_free = vs_evac
        for cc in range(8):
            so = (cc % 2) * 512
            sb_w = (cc % 2) * 512
            sb_r = ((cc + 1) % 2) * 512
            p.wt("tensor", ve, mask_prev)
            for pp in range(4):
                for hh in range(2):
                    p.op("tensor", lambda e, pp=pp, hh=hh, cc=cc: e.matmul(
                        psA[0:16, pp * 32 + hh * 16: pp * 32 + (hh + 1) * 16],
                        utl[:, (pp * 2 + hh) * 128 + cc * 16: (pp * 2 + hh) * 128 + (cc + 1) * 16],
                        wtl[:, (pp * 2 + hh) * 128 + cc * 16: (pp * 2 + hh) * 128 + (cc + 1) * 16],
                        start=True, stop=True), inc=(te if (pp, hh) == (3, 1) else None))
            at_te = te.v
            p.wt("tensor", ve, state_prev)
            if cc == 0:
                p.wt("tensor", ve, vs_evac)
            for pp in range(4):
                for hh in range(2):
                    p.op("tensor", lambda e, pp=pp, hh=hh, so=so: e.matmul(
                        psM[0:64, (pp * 2 + hh) * 64:(pp * 2 + hh + 1) * 64],
                        ktsub[:, so + pp * 128 + hh * 64: so + pp * 128 + (hh + 1) * 64],
                        vsub[:, so + pp * 128 + hh * 64: so + pp * 128 + (hh + 1) * 64],
                        start=True, stop=True), inc=(te if (pp, hh) == (3, 1) else None))
            m_te = te.v
            if cc < 7:
                p.wt("tensor", ve, vs_evac)
                for pp in range(4):
                    p.op("tensor", lambda e, pp=pp, cc=cc: e.transpose(
                        psVT[0:16, pp * 128:(pp + 1) * 128],
                        vf[:, pp * 128 + (cc + 1) * 16: pp * 128 + (cc + 2) * 16], ident[:]))
                    p.op("tensor", lambda e, pp=pp, cc=cc: e.transpose(
                        psVT[0:16, 512 + pp * 128: 512 + (pp + 1) * 128],
                        ktl[:, pp * 128 + (cc + 1) * 16: pp * 128 + (cc + 2) * 16], ident[:]),
                        inc=(te if pp == 3 else None))
                ph1_te = te.v
            p.wt("vector", te, at_te)
            p.op("vector", lambda e: e.tensor_tensor(
                out=a_sb[:].rearrange("p (g j) -> p g j", j=16),
                in0=psA[0:16, :].rearrange("p (g j) -> p g j", j=16),
                in1=maskT[:].unsqueeze(1).broadcast_to((16, 8, 16)), op=ALU.mult), inc=ve)
            mask_prev = ve.v
            if cc < 7:
                so2 = ((cc + 1) % 2) * 512
                p.wt("vector", te, ph1_te)
                p.op("vector", lambda e, so2=so2: e.tensor_copy(
                    out=vsub[:, so2:so2 + 512], in_=psVT[0:16, 0:512]), inc=ve)
                p.op("vector", lambda e, so2=so2: e.tensor_copy(
                    out=ktsub[:, so2:so2 + 512], in_=psVT[0:16, 512:1024]), inc=ve)
                vs_evac = ve.v
                subs_free = vs_evac
            p.wt("vector", te, m_te)
            p.op("vector", lambda e: e.tensor_tensor(
                out=s_tmp[:].rearrange("p (h c) -> p h c", c=64),
                in0=s_st[:].rearrange("p (h c) -> p h c", c=64),
                in1=tdc8[:].unsqueeze(2).broadcast_to((64, 8, 64)), op=ALU.mult), inc=ve)
            p.wt("vector", ve)
            p.op("vector", lambda e: e.tensor_tensor(
                out=s_st[:], in0=s_tmp[:], in1=psM[0:64, :], op=ALU.add), inc=ve)
            state_prev = ve.v
            p.wt("vector", ve)
            p.op("vector", lambda e, sb_w=sb_w: e.tensor_copy(
                out=s_bf[:, sb_w:sb_w + 512], in_=s_st[:]), inc=ve)
            p.wt("tensor", ve, mask_prev)
            for pp in range(4):
                for hh in range(2):
                    h = pp * 2 + hh
                    p.op("tensor", lambda e, h=h, pp=pp, hh=hh, cc=cc, sb_r=sb_r: e.matmul(
                        psY[0:64, pp * 32 + hh * 16: pp * 32 + (hh + 1) * 16],
                        s_bf[:, sb_r + h * 64: sb_r + (h + 1) * 64],
                        rtl[:, h * 128 + cc * 16: h * 128 + (cc + 1) * 16],
                        start=True, stop=False))
                    p.op("tensor", lambda e, h=h, pp=pp, hh=hh, so=so: e.matmul(
                        psY[0:64, pp * 32 + hh * 16: pp * 32 + (hh + 1) * 16],
                        vsub[:, so + pp * 128 + hh * 64: so + pp * 128 + (hh + 1) * 64],
                        a_sb[:, pp * 32 + hh * 16: pp * 32 + (hh + 1) * 16],
                        start=False, stop=True), inc=(te if (pp, hh) == (3, 1) else None))
            y_te = te.v
            p.wt("vector", te, y_te)
            p.op("vector", lambda e, cc=cc: e.tensor_copy(
                out=y_fmt[:].rearrange("p (h c) -> p h c", c=128)[:, :, cc * 16:(cc + 1) * 16],
                in_=psY[0:64, :].rearrange("p (h c) -> p h c", c=16)), inc=ve)
        yf_ve = ve.v
        fm_free = yf_ve
        p.wt("tensor", ve, yf_ve)
        for spi in range(2):
            ps = ps_a if spi == 0 else ps_b
            p.wt("tensor", ve)
            for h in range(8):
                pp, rb = h // 2, (h % 2) * 64
                p.op("tensor", lambda e, h=h, spi=spi, ps=ps: e.matmul(
                    ps[:, :], y_fmt[:, h * 128:(h + 1) * 128],
                    wot[:, h * D + spi * 512: h * D + spi * 512 + 512],
                    start=(h == 0), stop=(h == 7)), inc=(te if h == 7 else None))
            om_te = te.v
            p.wt("vector", te, om_te)
            if spi == 0:
                p.wt("vector", sty[0], st_done[0])
            p.op("vector", lambda e, spi=spi, ps=ps: e.tensor_copy(
                out=y_sb[:, spi * 512:(spi + 1) * 512], in_=ps[:, :]), inc=ve)
        yv = ve.v
        p.wt("scalar", ve, yv)
        p.op("scalar", lambda e, ti=ti: e.dma_start(
            out=y_h[ti * 128:(ti + 1) * 128, :], in_=y_sb[:]), inc=sty[0], delta=16)
        st_done[0] = sty[0].v
    p.wt("vector", sty[0], sty[0].v)
    return p


def _run(p, in_maps):
    nc = p.finalize()
    return run_bass_kernel_spmd(nc, in_maps, core_ids=list(range(NCORES)))


_IDENT = np.eye(128, dtype=np.float32)


def build_add3():
    p = P()
    x_h = p.inp("x", [R, D])
    p0_h = p.inp("p0", [R, D])
    p1_h = p.inp("p1", [R, D])
    y_h = p.out("y", [R, D])
    x_sb = p.sbuf("x_sb", [128, 2 * D], F32)
    p0_sb = p.sbuf("p0_sb", [128, 2 * D], F32)
    p1_sb = p.sbuf("p1_sb", [128, 2 * D], F32)
    y_sb = p.sbuf("y_sb", [128, 2 * D], F32)
    ld = [p.sem("ld0"), p.sem("ld1")]
    ve = p.sem("ve")
    sty = [p.sem("sty0"), p.sem("sty1")]
    st_done = [0, 0]
    for t in range(NT):
        par = t % 2
        if t >= 2:
            p.wt("sync", ve, (t - 1) * 2)
        for (buf, srch) in ((x_sb, x_h), (p0_sb, p0_h), (p1_sb, p1_h)):
            p.op("sync", lambda e, buf=buf, srch=srch, t=t, par=par: e.dma_start(
                out=buf[:, par * D:(par + 1) * D], in_=srch[t * 128:(t + 1) * 128, :]),
                inc=ld[par], delta=16)
        ldv = ld[par].v
        p.wt("vector", ld[par], ldv)
        p.op("vector", lambda e, par=par: e.tensor_tensor(
            out=y_sb[:, par * D:(par + 1) * D], in0=x_sb[:, par * D:(par + 1) * D],
            in1=p0_sb[:, par * D:(par + 1) * D], op=ALU.add), inc=ve)
        p.wt("vector", ve)
        if t >= 2:
            p.wt("vector", sty[par], st_done[par])
        p.op("vector", lambda e, par=par: e.tensor_tensor(
            out=y_sb[:, par * D:(par + 1) * D], in0=y_sb[:, par * D:(par + 1) * D],
            in1=p1_sb[:, par * D:(par + 1) * D], op=ALU.add), inc=ve)
        p.wt("scalar", ve, ve.v)
        p.op("scalar", lambda e, t=t, par=par: e.dma_start(
            out=y_h[t * 128:(t + 1) * 128, :], in_=y_sb[:, par * D:(par + 1) * D]),
            inc=sty[par], delta=16)
        st_done[par] = sty[par].v
    p.wt("vector", sty[0], sty[0].v)
    p.wt("vector", sty[1], sty[1].v)
    return p


_NC_CACHE = {}


def _get_nc(name, builder):
    if name not in _NC_CACHE:
        _NC_CACHE[name] = builder().finalize()
    return _NC_CACHE[name]


def _launch(name, builder, in_maps):
    nc = _get_nc(name, builder)
    res = run_bass_kernel_spmd(nc, in_maps, core_ids=list(range(NCORES)))
    return res.results


def _rope_tables(pos):
    div = np.exp(np.arange(0, 64, 2, dtype=np.float64) * (-np.log(10000.0) / 64.0))
    ang = pos[:, None].astype(np.float64) * div[None, :]
    return np.cos(ang).astype(np.float32), np.sin(ang).astype(np.float32)


def _permute_qk_cols(wmat, nheads):
    perm = np.concatenate([np.arange(0, 64, 2), np.arange(1, 64, 2)])
    idx = np.concatenate([h * 64 + perm for h in range(nheads)])
    return np.ascontiguousarray(wmat[:, idx])


def kernel(**inputs):
    g = {k: np.ascontiguousarray(np.asarray(v, np.float32)) for k, v in inputs.items()}
    x = g["L0_attn_x"].reshape(4, 8192, D)

    def slc(c):
        b, half = c // 2, c % 2
        return b, half

    def xparts(xf):
        return [xf[c // 2, (c % 2) * R:((c % 2) + 1) * R] for c in range(NCORES)]

    def gather(results, key="y"):
        out = np.empty((4, 8192, D), np.float32)
        for c in range(NCORES):
            b, half = slc(c)
            out[b, half * R:(half + 1) * R] = np.asarray(results[c][key])
        return out

    ident = _IDENT

    def run_l0f(xc):
        ims = []
        for c in range(NCORES):
            b, half = slc(c)
            ims.append({"x": xc[b, half * R:(half + 1) * R], "gate": g["L0_ffn_gate"],
                        "w1": g["L0_ffn_experts_w1"], "w2": g["L0_ffn_experts_w2"],
                        "w3": g["L0_ffn_experts_w3"], "ident": ident})
        return gather(_launch("l0f", build_l0f, ims))

    def run_l0a(xc):
        wq1 = _permute_qk_cols(g["L0_attn_Wq1"], 16)
        wq2 = _permute_qk_cols(g["L0_attn_Wq2"], 16)
        wk1 = _permute_qk_cols(g["L0_attn_Wk1"], 2)
        wk2 = _permute_qk_cols(g["L0_attn_Wk2"], 2)
        lvec = np.stack([g["L0_attn_lq1"], g["L0_attn_lk1"],
                         g["L0_attn_lq2"], g["L0_attn_lk2"]], axis=1)
        ims = []
        for c in range(NCORES):
            b, half = slc(c)
            r0 = 0 if half == 0 else 2048
            cos, sin = _rope_tables(np.arange(r0, r0 + RGN))
            ims.append({"xh": xc[b, r0:r0 + RGN], "x": xc[b, half * R:(half + 1) * R],
                        "wq1": wq1, "wq2": wq2, "wk1": wk1, "wk2": wk2,
                        "wv": g["L0_attn_Wv"], "wout": g["L0_attn_Wout"],
                        "bout": g["L0_attn_bout"].reshape(1, D), "lvec": lvec,
                        "cosr": cos, "sinr": sin, "ident": ident})
        return gather(_launch("l0a", build_l0a, ims))

    def run_l1f(xc):
        ims = []
        for c in range(NCORES):
            b, half = slc(c)
            ims.append({"x": xc[b, half * R:(half + 1) * R], "w": g["L1_ffn_weight"],
                        "ident": ident})
        return gather(_launch("l1f", build_l1f, ims))

    def run_l1a(xc):
        i = np.arange(128)
        tri = np.where(i[None, :] > i[:, None], -1e30, 0.0).astype(np.float32)
        ims = []
        for c in range(NCORES):
            b, half = slc(c)
            pos0 = half * R
            cos, sin = _rope_tables(np.arange(pos0 - 1024, pos0 + R))
            ims.append({"x": xc[b, half * R:(half + 1) * R],
                        "xh": xc[b, 3072:4096],
                        "wq": g["L1_attn_Wq"], "wkv": g["L1_attn_Wkv"],
                        "wo": g["L1_attn_Wo"], "bo": g["L1_attn_bo"].reshape(1, D),
                        "wof": g["L1_attn_Wo0"] if half == 0 else g["L1_attn_Wo"],
                        "bof": (g["L1_attn_bo0"] if half == 0 else g["L1_attn_bo"]).reshape(1, D),
                        "cosr": cos, "sinr": sin,
                        "msk0": np.full((1, 1), -1e30 if half == 0 else 0.0, np.float32),
                        "trimask": tri, "ident": ident})
        return gather(_launch("l1a", build_l1a, ims))

    def run_l2f(xc):
        ims = []
        for c in range(NCORES):
            b, half = slc(c)
            ims.append({"x": xc[b, half * R:(half + 1) * R], "W1": g["L2_ffn_W1"],
                        "b1": g["L2_ffn_b1"].reshape(1, 4096), "W2": g["L2_ffn_W2"],
                        "b2": g["L2_ffn_b2"].reshape(1, D),
                        "scale": g["L2_ffn_scale"].reshape(1, 1),
                        "lt_raw": xc[b, 1:17], "ident": ident})
        return gather(_launch("l2f", build_l2f, ims))

    def run_l2a(xc):
        glw = g["L2_attn_gate_local_w"]
        grw = g["L2_attn_gate_remote_w"]
        glwb = np.zeros((1024, 16), np.float32)
        grwb = np.zeros((1024, 16), np.float32)
        for h in range(16):
            glwb[h * 64:(h + 1) * 64, h] = glw[h]
            grwb[h * 64:(h + 1) * 64, h] = grw[h]
        i = np.arange(128)
        tri_a = np.where(i[None, :] > i[:, None], 0.0, -1e30).astype(np.float32)
        tri_b = np.where(i[None, :] <= i[:, None], 0.0, -1e30).astype(np.float32)
        ims = []
        for c in range(NCORES):
            b, half = slc(c)
            lmask = np.full((32, 64), -1e30, np.float32)
            for gt in range(32):
                lmask[gt, :half * 32 + gt + 1] = 0.0
            ims.append({"x": xc[b, half * R:(half + 1) * R], "xkv": xc[b, 3840:4096],
                        "xfirst": xc[b, 0:4096], "wq": g["L2_attn_wq"],
                        "wk": g["L2_attn_wk"], "wv": g["L2_attn_wv"],
                        "wgq": g["L2_attn_wgq"], "wgk": g["L2_attn_wgk"],
                        "wgv": g["L2_attn_wgv"], "r_out_w": g["L2_attn_r_out_w"],
                        "w_out": g["L2_attn_w_out"], "glwb": glwb, "grwb": grwb,
                        "tri_a": tri_a, "tri_b": tri_b,
                        "mfirst": np.full((1, 1), -1e30 if half == 0 else 0.0, np.float32),
                        "lmask": lmask.reshape(1, -1), "ident": ident})
        return gather(_launch("l2a", build_l2a, ims))

    def run_l3f(xc):
        zrow = np.zeros((1, D), np.float32)
        ims = []
        for c in range(NCORES):
            b, half = slc(c)
            ims.append({"x": xc[b, half * R:(half + 1) * R],
                        "wr": g["L3_ffn_W_r"], "wk": g["L3_ffn_W_k"], "wv": g["L3_ffn_W_v"],
                        "mur": g["L3_ffn_mu_r"].reshape(1, D),
                        "muk": g["L3_ffn_mu_k"].reshape(1, D),
                        "xp": (zrow if half == 0 else xc[b, 4095:4096]),
                        "ident": ident})
        return gather(_launch("l3f", build_l3f, ims))

    def run_l3a(xc):
        td = g["L3_attn_time_decay"]
        tfi = g["L3_attn_time_first"]
        ims = []
        for c in range(NCORES):
            b, hg = c // 2, c % 2
            hs = slice(hg * 512, (hg + 1) * 512)
            dec2 = np.zeros((128, 4), np.float32)
            tf2 = np.zeros((128, 4), np.float32)
            for h in range(8):
                pp, rb = h // 2, (h % 2) * 64
                dec2[rb:rb + 64, pp] = td[hg * 8 + h]
                tf2[rb:rb + 64, pp] = tfi[hg * 8 + h]
            i16 = np.arange(16)
            maskT = (i16[:, None] <= i16[None, :]).astype(np.float32)
            ims.append({"x": xc[b], "wkt": np.ascontiguousarray(g["L3_attn_Wk"][hs, :].T),
                        "wvt": np.ascontiguousarray(g["L3_attn_Wv"][hs, :].T),
                        "wrt": np.ascontiguousarray(g["L3_attn_Wr"][hs, :].T),
                        "wot": np.ascontiguousarray(g["L3_attn_Wo"].T[hs, :]),
                        "decay2": dec2, "tf2": tf2, "maskT": maskT, "ident": ident})
        res = _launch("l3a", build_l3a, ims)
        out = np.empty((4, 8192, D), np.float32)
        ims2 = []
        for c in range(NCORES):
            b, half = slc(c)
            p0 = np.asarray(res[2 * b]["ypart"])[half * R:(half + 1) * R]
            p1 = np.asarray(res[2 * b + 1]["ypart"])[half * R:(half + 1) * R]
            ims2.append({"x": xc[b, half * R:(half + 1) * R], "p0": p0, "p1": p1})
        return gather(_launch("add3", build_add3, ims2))

    x = run_l0f(x)
    x = run_l0a(x)
    x = run_l0f(x)
    x = run_l1f(x)
    x = run_l1a(x)
    x = run_l1f(x)
    x = run_l2f(x)
    x = run_l2a(x)
    x = run_l2f(x)
    x = run_l3f(x)
    x = run_l3a(x)
    x = run_l3f(x)
    return x

